# version 02
# baseline (speedup 1.0000x reference)
import numpy as np
import ml_dtypes
import contextlib
import concourse.bass as bass
import concourse.mybir as mybir
from concourse.bass_utils import run_bass_kernel_spmd

F32 = mybir.dt.float32
BF16 = mybir.dt.bfloat16
AF = mybir.ActivationFunctionType
ALU = mybir.AluOpType
AX = mybir.AxisListType
BF16NP = ml_dtypes.bfloat16

ENGINES = ["sync", "scalar", "vector", "tensor", "gpsimd"]

B, S, D, H, DH = 4, 4096, 1024, 16, 64
TOWN = S // 2


class Ser:

    def __init__(self, nc, sem):
        self.nc = nc
        self.sem = sem
        self.ops = []

    def op(self, engine, fn, inc=1):
        self.ops.append((engine, fn, inc))

    def dma(self, engine, out, in_):
        self.op(engine, lambda e: e.dma_start(out, in_), 16)

    def build(self, block):
        ops, sem = self.ops, self.sem

        def mk(ename):
            def body(eng):
                cum = 0
                for (e, fn, inc) in ops:
                    if e == ename:
                        eng.wait_ge(sem, cum)
                        fn(eng).then_inc(sem, inc)
                    cum += inc
                eng.wait_ge(sem, cum)
            return body

        for ename in ENGINES:
            getattr(block, ename)(mk(ename))


def t2d(x):
    return np.ascontiguousarray(np.swapaxes(x, -1, -2))


def bcol(v):
    return np.ascontiguousarray(v.reshape(8, 128).T)


def pe_table():
    pos = np.arange(S, dtype=np.float32)[:, None]
    div = np.exp(np.arange(0, D, 2, dtype=np.float32) * (-np.log(10000.0) / D))
    pe = np.zeros((S, D), np.float32)
    pe[:, 0::2] = np.sin(pos * div)
    pe[:, 1::2] = np.cos(pos * div)
    return pe



def build_l0_attn():
    nc = bass.Bass("TRN2", target_bir_lowering=False)
    xT = nc.dram_tensor("xT", [D, S], F32, kind="ExternalInput").ap()
    xTo = nc.dram_tensor("xTo", [D, TOWN], F32, kind="ExternalInput").ap()
    peT_d = nc.dram_tensor("peT", [D, S], F32, kind="ExternalInput").ap()
    peTo_d = nc.dram_tensor("peTo", [D, TOWN], F32, kind="ExternalInput").ap()
    wq_d = nc.dram_tensor("wq", [D, D], F32, kind="ExternalInput").ap()
    wk_d = nc.dram_tensor("wk", [D, D], F32, kind="ExternalInput").ap()
    wv_d = nc.dram_tensor("wv", [D, D], F32, kind="ExternalInput").ap()
    wo_d = nc.dram_tensor("wo", [D, D], F32, kind="ExternalInput").ap()
    bq_d = nc.dram_tensor("bq", [128, 8], F32, kind="ExternalInput").ap()
    bk_d = nc.dram_tensor("bk", [128, 8], F32, kind="ExternalInput").ap()
    bo_d = nc.dram_tensor("bo", [128, 8], F32, kind="ExternalInput").ap()
    gam_d = nc.dram_tensor("gam", [128, 8], F32, kind="ExternalInput").ap()
    bet_d = nc.dram_tensor("bet", [128, 8], F32, kind="ExternalInput").ap()
    bvr_d = nc.dram_tensor("bvr", [1, D], F32, kind="ExternalInput").ap()
    x1T_d = nc.dram_tensor("x1T", [D, TOWN], F32, kind="ExternalOutput").ap()
    ctx_d = nc.dram_tensor("ctxh", [D, TOWN], BF16).ap()
    qT_d = nc.dram_tensor("qTh", [D, TOWN], BF16).ap()
    st_d = nc.dram_tensor("sth", [4, S], F32).ap()

    ex = contextlib.ExitStack()
    with ex:
        sem = ex.enter_context(nc.semaphore("ser"))
        A_K = ex.enter_context(nc.sbuf_tensor("A_K", [128, 8 * S], BF16))
        A_V = ex.enter_context(nc.sbuf_tensor("A_V", [128, 32 * 1040], BF16))
        A_W = ex.enter_context(nc.sbuf_tensor("A_W", [128, 8192], BF16))
        A_P = ex.enter_context(nc.sbuf_tensor("A_P", [128, 4096], BF16))
        WL = ex.enter_context(nc.sbuf_tensor("WL", [128, 1024], F32))
        XS = ex.enter_context(nc.sbuf_tensor("XS", [128, 512], F32))
        SQ = ex.enter_context(nc.sbuf_tensor("SQ", [128, 512], F32))
        HF = ex.enter_context(nc.sbuf_tensor("HF", [128, 512], F32))
        PEs = ex.enter_context(nc.sbuf_tensor("PEs", [128, 512], F32))
        HBt = ex.enter_context(nc.sbuf_tensor("HBt", [128, 4096], BF16))
        QS = ex.enter_context(nc.sbuf_tensor("QS", [128, 512], BF16))
        R0 = ex.enter_context(nc.sbuf_tensor("R0", [1, 512], F32))
        R1 = ex.enter_context(nc.sbuf_tensor("R1", [1, 512], F32))
        R2 = ex.enter_context(nc.sbuf_tensor("R2", [1, 512], F32))
        RL = ex.enter_context(nc.sbuf_tensor("RL", [1, 512], F32))
        RLB = ex.enter_context(nc.sbuf_tensor("RLB", [64, 512], F32))
        CTXS = ex.enter_context(nc.sbuf_tensor("CTXS", [64, 512], BF16))
        ONE1 = ex.enter_context(nc.sbuf_tensor("ONE1", [128, 1], F32))
        ONER = ex.enter_context(nc.sbuf_tensor("ONER", [1, 128], F32))
        BIA = ex.enter_context(nc.sbuf_tensor("BIA", [128, 40], F32))
        BVRB = ex.enter_context(nc.sbuf_tensor("BVRB", [128, D], F32))
        VS = ex.enter_context(nc.sbuf_tensor("VS", [128, 512], F32))
        ps_s = ex.enter_context(nc.psum_tensor("ps_s", [128, 2048], F32))
        ps_o = ex.enter_context(nc.psum_tensor("ps_o", [65, 512], F32))
        ps_b = ex.enter_context(nc.psum_tensor("ps_b", [128, 512], F32))
        ps_c = ex.enter_context(nc.psum_tensor("ps_c", [128, 512], F32))
        blk = ex.enter_context(nc.Block())

        OA = A_V[:, 0:4096]
        YA = A_V[:, 8192:16384].bitcast(F32)
        XA = A_V[:, 16384:24576].bitcast(F32)

        s = Ser(nc, sem)
        s.op("vector", lambda e: e.memset(ONE1[:], 1.0))
        s.op("vector", lambda e: e.memset(ONER[:], 1.0))
        s.dma("sync", BIA[:, 0:8], bq_d)
        s.dma("sync", BIA[:, 8:16], bk_d)
        s.dma("sync", BIA[:, 16:24], bo_d)
        s.dma("sync", BIA[:, 24:32], gam_d)
        s.dma("sync", BIA[:, 32:40], bet_d)
        s.dma("sync", BVRB[:], bvr_d.partition_broadcast(128))
        for tt in range(32):
            s.op("gpsimd", lambda e, tt=tt: e.memset(
                A_V[:, tt * 1040:(tt + 1) * 1040].rearrange("p (h d) -> p h d", d=65)[:, :, 64:65], 1.0))

        def stats_sweep(src, ntg, mu_row, rstd_row):
            for tg in range(ntg):
                for ci in range(8):
                    s.dma("sync", XS[:], src[ci * 128:(ci + 1) * 128, tg * 512:(tg + 1) * 512])
                    s.op("scalar", lambda e: e.activation(SQ[:], XS[:], AF.Square))
                    s.op("tensor", lambda e, ci=ci: e.matmul(
                        ps_s[0:1, 0:512], ONE1[:], XS[:], start=(ci == 0), stop=(ci == 7), skip_group_check=True))
                    s.op("tensor", lambda e, ci=ci: e.matmul(
                        ps_s[0:1, 512:1024], ONE1[:], SQ[:], start=(ci == 0), stop=(ci == 7), skip_group_check=True))
                s.op("vector", lambda e: e.tensor_scalar(R0[:], ps_s[0:1, 0:512], 1.0 / D, None, op0=ALU.mult))
                s.op("vector", lambda e: e.tensor_scalar(R1[:], ps_s[0:1, 512:1024], 1.0 / D, None, op0=ALU.mult))
                s.op("vector", lambda e: e.tensor_tensor(R2[:], R0[:], R0[:], op=ALU.mult))
                s.op("vector", lambda e: e.tensor_tensor(R2[:], R1[:], R2[:], op=ALU.subtract))
                s.op("vector", lambda e: e.tensor_scalar(R2[:], R2[:], 1e-5, None, op0=ALU.add))
                s.op("scalar", lambda e: e.activation(R2[:], R2[:], AF.Sqrt))
                s.op("vector", lambda e: e.reciprocal(R2[:], R2[:]))
                s.dma("sync", st_d[mu_row:mu_row + 1, tg * 512:(tg + 1) * 512], R0[:])
                s.dma("sync", st_d[rstd_row:rstd_row + 1, tg * 512:(tg + 1) * 512], R2[:])

        def load_w(w_d):
            for ci in range(8):
                s.dma("sync", WL[:], w_d[ci * 128:(ci + 1) * 128, :])
                s.op("vector", lambda e, ci=ci: e.tensor_copy(A_W[:, ci * 1024:(ci + 1) * 1024], WL[:]))

        def make_h(src, pe_src, tg, mu_row, rstd_row, add_pe=True):
            s.dma("sync", R0[:], st_d[mu_row:mu_row + 1, tg * 512:(tg + 1) * 512])
            s.dma("sync", R2[:], st_d[rstd_row:rstd_row + 1, tg * 512:(tg + 1) * 512])
            s.op("tensor", lambda e: e.matmul(ps_b[:], ONER[:], R0[:], start=True, stop=True))
            s.op("tensor", lambda e: e.matmul(ps_c[:], ONER[:], R2[:], start=True, stop=True))
            for ci in range(8):
                s.dma("sync", XS[:], src[ci * 128:(ci + 1) * 128, tg * 512:(tg + 1) * 512])
                s.op("vector", lambda e: e.tensor_tensor(HF[:], XS[:], ps_b[:], op=ALU.subtract))
                if add_pe:
                    s.dma("sync", PEs[:], pe_src[ci * 128:(ci + 1) * 128, tg * 512:(tg + 1) * 512])
                    s.op("vector", lambda e: e.tensor_tensor(HF[:], HF[:], ps_c[:], op=ALU.mult))
                    s.op("vector", lambda e, ci=ci: e.tensor_tensor(HBt[:, ci * 512:(ci + 1) * 512], HF[:], PEs[:], op=ALU.add))
                else:
                    s.op("vector", lambda e, ci=ci: e.tensor_tensor(HBt[:, ci * 512:(ci + 1) * 512], HF[:], ps_c[:], op=ALU.mult))

        stats_sweep(xT, 8, 0, 1)
        stats_sweep(xTo, 4, 2, 3)

        load_w(wk_d)
        for tg in range(8):
            make_h(xT, peT_d, tg, 0, 1)
            for oc in range(8):
                for ci in range(8):
                    s.op("tensor", lambda e, oc=oc, ci=ci: e.matmul(
                        ps_s[:, 0:512], A_W[:, ci * 1024 + oc * 128: ci * 1024 + oc * 128 + 128],
                        HBt[:, ci * 512:(ci + 1) * 512], start=(ci == 0), stop=(ci == 7)))
                s.op("vector", lambda e, oc=oc, tg=tg: e.tensor_scalar(
                    A_K[:, oc * S + tg * 512: oc * S + tg * 512 + 512], ps_s[:, 0:512],
                    BIA[:, 8 + oc:9 + oc], None, op0=ALU.add))

        load_w(wv_d)
        for tg in range(8):
            make_h(xT, peT_d, tg, 0, 1)
            for tt4 in range(4):
                tt = tg * 4 + tt4
                for nh in range(2):
                    for ci in range(8):
                        s.op("tensor", lambda e, tt4=tt4, nh=nh, ci=ci: e.matmul(
                            ps_b[:], HBt[:, ci * 512 + tt4 * 128: ci * 512 + tt4 * 128 + 128],
                            A_W[:, ci * 1024 + nh * 512: ci * 1024 + nh * 512 + 512],
                            start=(ci == 0), stop=(ci == 7)))
                    s.op("scalar", lambda e: e.activation(VS[:], ps_b[:], AF.Copy))
                    s.op("vector", lambda e, tt=tt, nh=nh: e.tensor_tensor(
                        A_V[:, tt * 1040 + nh * 520: tt * 1040 + nh * 520 + 520].rearrange("p (h d) -> p h d", d=65)[:, :, 0:64],
                        VS[:].rearrange("p (h d) -> p h d", d=64),
                        BVRB[:, nh * 512:(nh + 1) * 512].rearrange("p (h d) -> p h d", d=64), op=ALU.add))

        load_w(wq_d)
        for tg in range(4):
            make_h(xTo, peTo_d, tg, 2, 3)
            for oc in range(8):
                for ci in range(8):
                    s.op("tensor", lambda e, oc=oc, ci=ci: e.matmul(
                        ps_s[:, 0:512], A_W[:, ci * 1024 + oc * 128: ci * 1024 + oc * 128 + 128],
                        HBt[:, ci * 512:(ci + 1) * 512], start=(ci == 0), stop=(ci == 7)))
                s.op("vector", lambda e, oc=oc: e.tensor_scalar(QS[:], ps_s[:, 0:512], BIA[:, oc:oc + 1], None, op0=ALU.add))
                s.dma("sync", qT_d[oc * 128:(oc + 1) * 128, tg * 512:(tg + 1) * 512], QS[:])

        for qg in range(4):
            for ci in range(8):
                s.dma("sync", A_W[:, ci * 512:(ci + 1) * 512], qT_d[ci * 128:(ci + 1) * 128, qg * 512:(qg + 1) * 512])
            for h in range(16):
                hr = (h % 2) * 64
                hc = (h // 2)
                for kb in range(8):
                    for j in range(4):
                        kc = kb * 4 + j
                        s.op("tensor", lambda e, hr=hr, hc=hc, kc=kc, j=j: e.matmul(
                            ps_s[:, j * 512:(j + 1) * 512],
                            A_K[hr:hr + 64, hc * S + kc * 128: hc * S + kc * 128 + 128],
                            A_W[hr:hr + 64, hc * 512: hc * 512 + 512],
                            start=True, stop=True, skip_group_check=True))
                    s.op("scalar", lambda e: e.activation(A_P[:, 0:2048], ps_s[:], AF.Exp, scale=0.125))
                    for j in range(4):
                        kc = kb * 4 + j
                        s.op("tensor", lambda e, kc=kc, h=h, j=j, kb=kb: e.matmul(
                            ps_o[:], A_V[:, kc * 1040 + 65 * h: kc * 1040 + 65 * h + 65],
                            A_P[:, j * 512:(j + 1) * 512],
                            start=(kb == 0 and j == 0), stop=(kb == 7 and j == 3), skip_group_check=True))
                s.op("vector", lambda e: e.reciprocal(RL[:], ps_o[64:65, :]))
                s.op("tensor", lambda e: e.matmul(ps_b[0:64, :], ONER[:, 0:64], RL[:], start=True, stop=True))
                s.op("scalar", lambda e: e.activation(RLB[:], ps_b[0:64, :], AF.Copy))
                s.op("vector", lambda e: e.tensor_tensor(CTXS[:], ps_o[0:64, :], RLB[:], op=ALU.mult))
                s.dma("sync", ctx_d[h * 64:(h + 1) * 64, qg * 512:(qg + 1) * 512], CTXS[:])

        load_w(wo_d)
        for tg in range(4):
            for ci in range(8):
                s.dma("sync", A_P[:, ci * 512:(ci + 1) * 512], ctx_d[ci * 128:(ci + 1) * 128, tg * 512:(tg + 1) * 512])
            for oc in range(8):
                for ci in range(8):
                    s.op("tensor", lambda e, oc=oc, ci=ci: e.matmul(
                        ps_s[:, 0:512], A_W[:, ci * 1024 + oc * 128: ci * 1024 + oc * 128 + 128],
                        A_P[:, ci * 512:(ci + 1) * 512], start=(ci == 0), stop=(ci == 7)))
                s.op("vector", lambda e, oc=oc: e.tensor_scalar(OA[:, oc * 512:(oc + 1) * 512], ps_s[:, 0:512],
                                                                BIA[:, 16 + oc:17 + oc], None, op0=ALU.add))
            s.dma("sync", R0[:], st_d[2:3, tg * 512:(tg + 1) * 512])
            s.dma("sync", R2[:], st_d[3:4, tg * 512:(tg + 1) * 512])
            s.op("tensor", lambda e: e.matmul(ps_b[:], ONER[:], R0[:], start=True, stop=True))
            s.op("tensor", lambda e: e.matmul(ps_c[:], ONER[:], R2[:], start=True, stop=True))
            for ci in range(8):
                s.dma("sync", XA[:, ci * 512:(ci + 1) * 512], xTo[ci * 128:(ci + 1) * 128, tg * 512:(tg + 1) * 512])
                s.dma("sync", PEs[:], peTo_d[ci * 128:(ci + 1) * 128, tg * 512:(tg + 1) * 512])
                s.op("vector", lambda e, ci=ci: e.tensor_tensor(HF[:], XA[:, ci * 512:(ci + 1) * 512], ps_b[:], op=ALU.subtract))
                s.op("vector", lambda e: e.tensor_tensor(HF[:], HF[:], ps_c[:], op=ALU.mult))
                s.op("vector", lambda e: e.tensor_tensor(HF[:], HF[:], PEs[:], op=ALU.add))
                s.op("vector", lambda e, ci=ci: e.tensor_tensor(
                    YA[:, ci * 512:(ci + 1) * 512], HF[:], OA[:, ci * 512:(ci + 1) * 512], op=ALU.add))
                s.op("scalar", lambda e, ci=ci: e.activation(SQ[:], YA[:, ci * 512:(ci + 1) * 512], AF.Square))
                s.op("tensor", lambda e, ci=ci: e.matmul(
                    ps_s[0:1, 0:512], ONE1[:], YA[:, ci * 512:(ci + 1) * 512], start=(ci == 0), stop=(ci == 7), skip_group_check=True))
                s.op("tensor", lambda e, ci=ci: e.matmul(
                    ps_s[0:1, 512:1024], ONE1[:], SQ[:], start=(ci == 0), stop=(ci == 7), skip_group_check=True))
            s.op("vector", lambda e: e.tensor_scalar(R0[:], ps_s[0:1, 0:512], 1.0 / D, None, op0=ALU.mult))
            s.op("vector", lambda e: e.tensor_scalar(R1[:], ps_s[0:1, 512:1024], 1.0 / D, None, op0=ALU.mult))
            s.op("vector", lambda e: e.tensor_tensor(R2[:], R0[:], R0[:], op=ALU.mult))
            s.op("vector", lambda e: e.tensor_tensor(R2[:], R1[:], R2[:], op=ALU.subtract))
            s.op("vector", lambda e: e.tensor_scalar(R2[:], R2[:], 1e-5, None, op0=ALU.add))
            s.op("scalar", lambda e: e.activation(R2[:], R2[:], AF.Sqrt))
            s.op("vector", lambda e: e.reciprocal(R2[:], R2[:]))
            s.op("tensor", lambda e: e.matmul(ps_b[:], ONER[:], R0[:], start=True, stop=True))
            s.op("tensor", lambda e: e.matmul(ps_c[:], ONER[:], R2[:], start=True, stop=True))
            for ci in range(8):
                s.op("vector", lambda e, ci=ci: e.tensor_tensor(HF[:], YA[:, ci * 512:(ci + 1) * 512], ps_b[:], op=ALU.subtract))
                s.op("vector", lambda e: e.tensor_tensor(HF[:], HF[:], ps_c[:], op=ALU.mult))
                s.op("vector", lambda e, ci=ci: e.tensor_scalar(HF[:], HF[:], BIA[:, 24 + ci:25 + ci], BIA[:, 32 + ci:33 + ci], op0=ALU.mult, op1=ALU.add))
                s.op("vector", lambda e, ci=ci: e.tensor_tensor(HF[:], HF[:], XA[:, ci * 512:(ci + 1) * 512], op=ALU.add))
                s.dma("sync", x1T_d[ci * 128:(ci + 1) * 128, tg * 512:(tg + 1) * 512], HF[:])

        s.build(blk)
    return nc


def run_l0_attn(d):
    nc = build_l0_attn()
    pe = pe_table()
    peT = t2d(pe)
    xT_all = t2d(d["L0_attn_x"])
    in_maps = []
    for c in range(8):
        b, half = c // 2, c % 2
        own = slice(half * TOWN, (half + 1) * TOWN)
        in_maps.append({
            "xT": xT_all[b],
            "xTo": np.ascontiguousarray(xT_all[b][:, own]),
            "peT": peT,
            "peTo": np.ascontiguousarray(peT[:, own]),
            "wq": d["L0_attn_Wq"], "wk": d["L0_attn_Wk"], "wv": d["L0_attn_Wv"], "wo": d["L0_attn_Wo"],
            "bq": bcol(d["L0_attn_bq"]), "bk": bcol(d["L0_attn_bk"]), "bo": bcol(d["L0_attn_bo"]),
            "gam": bcol(d["L0_attn_gamma"]), "bet": bcol(d["L0_attn_beta"]),
            "bvr": d["L0_attn_bv"].reshape(1, D),
        })
    res = run_bass_kernel_spmd(nc, in_maps, core_ids=list(range(8)))
    x1 = np.empty((B, S, D), np.float32)
    for c in range(8):
        b, half = c // 2, c % 2
        x1[b, half * TOWN:(half + 1) * TOWN] = res.results[c]["x1T"].T
    return x1




def build_l0_ffn():
    nc = bass.Bass("TRN2", target_bir_lowering=False)
    IN2 = 2816
    x1o_d = nc.dram_tensor("x1o", [TOWN, D], F32, kind="ExternalInput").ap()
    wg_d = nc.dram_tensor("wg", [IN2, D], F32, kind="ExternalInput").ap()
    wu_d = nc.dram_tensor("wu", [IN2, D], F32, kind="ExternalInput").ap()
    wd_d = nc.dram_tensor("wd", [D, IN2], F32, kind="ExternalInput").ap()
    idn_d = nc.dram_tensor("idn", [128, 128], BF16, kind="ExternalInput").ap()
    x2o_d = nc.dram_tensor("x2o", [TOWN, D], F32, kind="ExternalOutput").ap()

    ex = contextlib.ExitStack()
    with ex:
        sem = ex.enter_context(nc.semaphore("ser"))
        WGT = ex.enter_context(nc.sbuf_tensor("WGT", [128, 8 * IN2], BF16))
        WUT = ex.enter_context(nc.sbuf_tensor("WUT", [128, 8 * IN2], BF16))
        WDT = ex.enter_context(nc.sbuf_tensor("WDT", [128, 22 * D], BF16))
        XT_ = ex.enter_context(nc.sbuf_tensor("XT_", [128, 1408], F32))
        ZT_ = ex.enter_context(nc.sbuf_tensor("ZT_", [128, 1024], F32))
        T_A = ex.enter_context(nc.sbuf_tensor("T_A", [128, 1408], F32))
        T_CA = ex.enter_context(nc.sbuf_tensor("T_CA", [128, 1408], F32))
        T_CB = ex.enter_context(nc.sbuf_tensor("T_CB", [128, 1408], F32))
        MB = ex.enter_context(nc.sbuf_tensor("MB", [128, 88], F32))
        SB = ex.enter_context(nc.sbuf_tensor("SB", [128, 88], F32))
        RB = ex.enter_context(nc.sbuf_tensor("RB", [128, 88], F32))
        QB = ex.enter_context(nc.sbuf_tensor("QB", [128, 1408], BF16))
        ZTT = ex.enter_context(nc.sbuf_tensor("ZTT", [128, 1024], BF16))
        HT = ex.enter_context(nc.sbuf_tensor("HT", [128, IN2], BF16))
        GA = ex.enter_context(nc.sbuf_tensor("GA", [128, 512], F32))
        UA = ex.enter_context(nc.sbuf_tensor("UA", [128, 512], F32))
        HH = ex.enter_context(nc.sbuf_tensor("HH", [128, 512], F32))
        OUTT = ex.enter_context(nc.sbuf_tensor("OUTT", [128, 1024], F32))
        IDN = ex.enter_context(nc.sbuf_tensor("IDN", [128, 128], BF16))
        SQA = ex.enter_context(nc.sbuf_tensor("SQA", [128, 1024], F32))
        MU = ex.enter_context(nc.sbuf_tensor("MU", [128, 1], F32))
        MSQ = ex.enter_context(nc.sbuf_tensor("MSQ", [128, 1], F32))
        RSTD = ex.enter_context(nc.sbuf_tensor("RSTD", [128, 1], F32))
        ps_g = ex.enter_context(nc.psum_tensor("ps_g", [128, 512], F32))
        ps_u = ex.enter_context(nc.psum_tensor("ps_u", [128, 512], F32))
        ps_t = ex.enter_context(nc.psum_tensor("ps_t", [128, 128], BF16))
        blk = ex.enter_context(nc.Block())

        s = Ser(nc, sem)
        s.dma("sync", IDN[:], idn_d)

        def quant(src, out, F):
            G = F // 16
            s.op("vector", lambda e: e.tensor_reduce(
                MB[:, 0:G], src.rearrange("p (g s) -> p g s", s=16), axis=AX.X,
                op=ALU.max, apply_absolute_value=True))
            s.op("vector", lambda e: e.tensor_scalar(SB[:, 0:G], MB[:, 0:G], 1.0 / 6.0, 1e-12, op0=ALU.mult, op1=ALU.max))
            s.op("vector", lambda e: e.reciprocal(RB[:, 0:G], SB[:, 0:G]))
            s.op("vector", lambda e: e.tensor_tensor(
                src.rearrange("p (g s) -> p g s", s=16), src.rearrange("p (g s) -> p g s", s=16),
                RB[:, 0:G].unsqueeze(2).broadcast_to([128, G, 16]), op=ALU.mult))
            s.op("scalar", lambda e: e.activation(T_A[:, 0:F], src, AF.Abs))
            s.op("vector", lambda e: e.tensor_scalar(T_CB[:, 0:F], T_A[:, 0:F], 5.0, 2.0, op0=ALU.is_gt, op1=ALU.mult))
            s.op("vector", lambda e: e.scalar_tensor_tensor(T_CB[:, 0:F], T_A[:, 0:F], 2.5, T_CB[:, 0:F], op0=ALU.is_gt, op1=ALU.add))
            s.op("vector", lambda e: e.scalar_tensor_tensor(T_CB[:, 0:F], T_A[:, 0:F], 3.5, T_CB[:, 0:F], op0=ALU.is_gt, op1=ALU.add))
            s.op("vector", lambda e: e.tensor_scalar(T_CA[:, 0:F], T_A[:, 0:F], 0.25, None, op0=ALU.is_gt))
            s.op("vector", lambda e: e.scalar_tensor_tensor(T_CA[:, 0:F], T_A[:, 0:F], 0.75, T_CA[:, 0:F], op0=ALU.is_gt, op1=ALU.add))
            s.op("vector", lambda e: e.scalar_tensor_tensor(T_CA[:, 0:F], T_A[:, 0:F], 1.25, T_CA[:, 0:F], op0=ALU.is_gt, op1=ALU.add))
            s.op("vector", lambda e: e.scalar_tensor_tensor(T_CA[:, 0:F], T_A[:, 0:F], 1.75, T_CA[:, 0:F], op0=ALU.is_gt, op1=ALU.add))
            s.op("vector", lambda e: e.scalar_tensor_tensor(T_CA[:, 0:F], T_CA[:, 0:F], 0.5, T_CB[:, 0:F], op0=ALU.mult, op1=ALU.add))
            s.op("scalar", lambda e: e.activation(T_CB[:, 0:F], src, AF.Sign))
            s.op("vector", lambda e: e.tensor_tensor(
                T_CB[:, 0:F].rearrange("p (g s) -> p g s", s=16), T_CB[:, 0:F].rearrange("p (g s) -> p g s", s=16),
                SB[:, 0:G].unsqueeze(2).broadcast_to([128, G, 16]), op=ALU.mult))
            s.op("gpsimd", lambda e: e.tensor_tensor(out, T_CA[:, 0:F], T_CB[:, 0:F], op=ALU.mult))

        def transpose_to(src_bf16, dst):
            s.op("tensor", lambda e: e.transpose(ps_t[:], src_bf16, IDN[:]))
            s.op("vector", lambda e: e.tensor_copy(dst, ps_t[:]))

        for r in range(22):
            s.dma("sync", XT_[:, 0:1024], wg_d[r * 128:(r + 1) * 128, :])
            quant(XT_[:, 0:1024], QB[:, 0:1024], 1024)
            for ci in range(8):
                transpose_to(QB[:, ci * 128:(ci + 1) * 128], WGT[:, ci * IN2 + r * 128: ci * IN2 + r * 128 + 128])
        for r in range(22):
            s.dma("sync", XT_[:, 0:1024], wu_d[r * 128:(r + 1) * 128, :])
            quant(XT_[:, 0:1024], QB[:, 0:1024], 1024)
            for ci in range(8):
                transpose_to(QB[:, ci * 128:(ci + 1) * 128], WUT[:, ci * IN2 + r * 128: ci * IN2 + r * 128 + 128])
        for r in range(8):
            for hh in range(2):
                s.dma("sync", XT_[:, 0:1408], wd_d[r * 128:(r + 1) * 128, hh * 1408:(hh + 1) * 1408])
                quant(XT_[:, 0:1408], QB[:, 0:1408], 1408)
                for cj in range(11):
                    ri = hh * 11 + cj
                    transpose_to(QB[:, cj * 128:(cj + 1) * 128], WDT[:, ri * D + r * 128: ri * D + r * 128 + 128])

        OCW = [(0, 512), (512, 512), (1024, 512), (1536, 512), (2048, 512), (2560, 256)]
        for t in range(16):
            s.dma("sync", XT_[:, 0:1024], x1o_d[t * 128:(t + 1) * 128, :])
            s.op("vector", lambda e: e.tensor_reduce(MU[:], XT_[:, 0:1024], axis=AX.X, op=ALU.add))
            s.op("vector", lambda e: e.tensor_scalar(MU[:], MU[:], 1.0 / D, None, op0=ALU.mult))
            s.op("scalar", lambda e: e.activation(SQA[:], XT_[:, 0:1024], AF.Square))
            s.op("vector", lambda e: e.tensor_reduce(MSQ[:], SQA[:], axis=AX.X, op=ALU.add))
            s.op("vector", lambda e: e.tensor_scalar(MSQ[:], MSQ[:], 1.0 / D, None, op0=ALU.mult))
            s.op("vector", lambda e: e.tensor_tensor(RSTD[:], MU[:], MU[:], op=ALU.mult))
            s.op("vector", lambda e: e.tensor_tensor(RSTD[:], MSQ[:], RSTD[:], op=ALU.subtract))
            s.op("vector", lambda e: e.tensor_scalar(RSTD[:], RSTD[:], 1e-5, None, op0=ALU.add))
            s.op("scalar", lambda e: e.activation(RSTD[:], RSTD[:], AF.Sqrt))
            s.op("vector", lambda e: e.reciprocal(RSTD[:], RSTD[:]))
            s.op("vector", lambda e: e.tensor_scalar(ZT_[:], XT_[:, 0:1024], MU[:], RSTD[:], op0=ALU.subtract, op1=ALU.mult))
            quant(ZT_[:], QB[:, 0:1024], 1024)
            for ci in range(8):
                transpose_to(QB[:, ci * 128:(ci + 1) * 128], ZTT[:, ci * 128:(ci + 1) * 128])
            for (ooff, ow) in OCW:
                for ci in range(8):
                    s.op("tensor", lambda e, ci=ci, ooff=ooff, ow=ow: e.matmul(
                        ps_g[:, 0:ow], ZTT[:, ci * 128:(ci + 1) * 128],
                        WGT[:, ci * IN2 + ooff: ci * IN2 + ooff + ow], start=(ci == 0), stop=(ci == 7), skip_group_check=True))
                for ci in range(8):
                    s.op("tensor", lambda e, ci=ci, ooff=ooff, ow=ow: e.matmul(
                        ps_u[:, 0:ow], ZTT[:, ci * 128:(ci + 1) * 128],
                        WUT[:, ci * IN2 + ooff: ci * IN2 + ooff + ow], start=(ci == 0), stop=(ci == 7), skip_group_check=True))
                s.op("scalar", lambda e, ow=ow: e.activation(GA[:, 0:ow], ps_g[:, 0:ow], AF.Silu))
                s.op("scalar", lambda e, ow=ow: e.activation(UA[:, 0:ow], ps_u[:, 0:ow], AF.Copy))
                s.op("vector", lambda e, ow=ow: e.tensor_tensor(HH[:, 0:ow], GA[:, 0:ow], UA[:, 0:ow], op=ALU.mult))
                quant(HH[:, 0:ow], QB[:, 0:ow], ow)
                for k in range(ow // 128):
                    transpose_to(QB[:, k * 128:(k + 1) * 128], HT[:, ooff + k * 128: ooff + (k + 1) * 128])
            for oc2 in range(2):
                for ri in range(22):
                    s.op("tensor", lambda e, ri=ri, oc2=oc2: e.matmul(
                        ps_g[:], HT[:, ri * 128:(ri + 1) * 128],
                        WDT[:, ri * D + oc2 * 512: ri * D + oc2 * 512 + 512], start=(ri == 0), stop=(ri == 21), skip_group_check=True))
                s.op("vector", lambda e, oc2=oc2: e.tensor_tensor(
                    OUTT[:, oc2 * 512:(oc2 + 1) * 512], ps_g[:], XT_[:, oc2 * 512: oc2 * 512 + 512], op=ALU.add))
            s.dma("sync", x2o_d[t * 128:(t + 1) * 128, :], OUTT[:])

        s.build(blk)
    return nc


def run_l0_ffn(d, x1):
    nc = build_l0_ffn()
    idn = np.eye(128).astype(BF16NP)
    in_maps = []
    for c in range(8):
        b, half = c // 2, c % 2
        in_maps.append({
            "x1o": np.ascontiguousarray(x1[b, half * TOWN:(half + 1) * TOWN]),
            "wg": d["L0_ffn_w_gate"], "wu": d["L0_ffn_w_up"], "wd": d["L0_ffn_w_down"],
            "idn": idn,
        })
    res = run_bass_kernel_spmd(nc, in_maps, core_ids=list(range(8)))
    x2 = np.empty((B, S, D), np.float32)
    for c in range(8):
        b, half = c // 2, c % 2
        x2[b, half * TOWN:(half + 1) * TOWN] = res.results[c]["x2o"]
    return x2




def build_l1_attn():
    nc = bass.Bass("TRN2", target_bir_lowering=False)
    xT = nc.dram_tensor("xT", [D, S], F32, kind="ExternalInput").ap()
    xTo = nc.dram_tensor("xTo", [D, TOWN], F32, kind="ExternalInput").ap()
    wq_d = nc.dram_tensor("wq", [D, D], F32, kind="ExternalInput").ap()
    wk_d = nc.dram_tensor("wk", [D, D], F32, kind="ExternalInput").ap()
    wv_d = nc.dram_tensor("wv", [D, D], F32, kind="ExternalInput").ap()
    wo_d = nc.dram_tensor("wo", [D, D], F32, kind="ExternalInput").ap()
    bq_d = nc.dram_tensor("bq", [128, 8], F32, kind="ExternalInput").ap()
    bk_d = nc.dram_tensor("bk", [128, 8], F32, kind="ExternalInput").ap()
    bo_d = nc.dram_tensor("bo", [128, 8], F32, kind="ExternalInput").ap()
    bvr_d = nc.dram_tensor("bvr", [1, D], F32, kind="ExternalInput").ap()
    qidx_d = nc.dram_tensor("qidx", [1, TOWN], F32, kind="ExternalInput").ap()
    kidx_d = nc.dram_tensor("kidx", [128, 32], F32, kind="ExternalInput").ap()
    x3T_d = nc.dram_tensor("x3T", [D, TOWN], F32, kind="ExternalOutput").ap()
    z0T_d = nc.dram_tensor("z0T", [D, TOWN], F32, kind="ExternalOutput").ap()
    ctx_d = nc.dram_tensor("ctxh", [D, TOWN], BF16).ap()
    qT_d = nc.dram_tensor("qTh", [D, TOWN], BF16).ap()
    st_d = nc.dram_tensor("sth", [4, S], F32).ap()

    ex = contextlib.ExitStack()
    with ex:
        sem = ex.enter_context(nc.semaphore("ser"))
        A_K = ex.enter_context(nc.sbuf_tensor("A_K", [128, 8 * S], BF16))
        A_V = ex.enter_context(nc.sbuf_tensor("A_V", [128, 32 * 1040], BF16))
        A_W = ex.enter_context(nc.sbuf_tensor("A_W", [128, 8192], BF16))
        A_P = ex.enter_context(nc.sbuf_tensor("A_P", [128, 4096], BF16))
        WS = ex.enter_context(nc.sbuf_tensor("WS", [128, 16384], BF16))
        QS = ex.enter_context(nc.sbuf_tensor("QS", [128, 512], BF16))
        QIF = ex.enter_context(nc.sbuf_tensor("QIF", [128, 512], F32))
        KIX = ex.enter_context(nc.sbuf_tensor("KIX", [128, 32], F32))
        R0 = ex.enter_context(nc.sbuf_tensor("R0", [1, 512], F32))
        R1 = ex.enter_context(nc.sbuf_tensor("R1", [1, 512], F32))
        R2 = ex.enter_context(nc.sbuf_tensor("R2", [1, 512], F32))
        RL = ex.enter_context(nc.sbuf_tensor("RL", [1, 512], F32))
        RLB = ex.enter_context(nc.sbuf_tensor("RLB", [64, 512], F32))
        CTXS = ex.enter_context(nc.sbuf_tensor("CTXS", [64, 512], BF16))
        ONE1 = ex.enter_context(nc.sbuf_tensor("ONE1", [128, 1], F32))
        ONER = ex.enter_context(nc.sbuf_tensor("ONER", [1, 128], F32))
        BIA = ex.enter_context(nc.sbuf_tensor("BIA", [128, 24], F32))
        BVRB = ex.enter_context(nc.sbuf_tensor("BVRB", [128, D], F32))
        ps_s = ex.enter_context(nc.psum_tensor("ps_s", [128, 2048], F32))
        ps_o = ex.enter_context(nc.psum_tensor("ps_o", [65, 512], F32))
        ps_b = ex.enter_context(nc.psum_tensor("ps_b", [128, 512], F32))
        ps_c = ex.enter_context(nc.psum_tensor("ps_c", [128, 512], F32))
        blk = ex.enter_context(nc.Block())

        HBt = WS[:, 0:4096]
        WLv = WS[:, 4096:6144].bitcast(F32)
        XSv = WS[:, 6144:7168].bitcast(F32)
        SQv = WS[:, 7168:8192].bitcast(F32)
        HFv = WS[:, 8192:9216].bitcast(F32)
        VSv = WS[:, 9216:10240].bitcast(F32)
        OA = A_V[:, 0:4096]
        YA = A_V[:, 8192:16384].bitcast(F32)
        XA = A_V[:, 16384:24576].bitcast(F32)

        s = Ser(nc, sem)
        s.op("vector", lambda e: e.memset(ONE1[:], 1.0))
        s.op("vector", lambda e: e.memset(ONER[:], 1.0))
        s.dma("sync", BIA[:, 0:8], bq_d)
        s.dma("sync", BIA[:, 8:16], bk_d)
        s.dma("sync", BIA[:, 16:24], bo_d)
        s.dma("sync", BVRB[:], bvr_d.partition_broadcast(128))
        s.dma("sync", KIX[:], kidx_d)
        for tt in range(32):
            s.op("gpsimd", lambda e, tt=tt: e.memset(
                A_V[:, tt * 1040:(tt + 1) * 1040].rearrange("p (h d) -> p h d", d=65)[:, :, 64:65], 1.0))

        def stats_sweep(src, ntg, mu_row, rstd_row):
            for tg in range(ntg):
                for ci in range(8):
                    s.dma("sync", XSv, src[ci * 128:(ci + 1) * 128, tg * 512:(tg + 1) * 512])
                    s.op("scalar", lambda e: e.activation(SQv, XSv, AF.Square))
                    s.op("tensor", lambda e, ci=ci: e.matmul(
                        ps_s[0:1, 0:512], ONE1[:], XSv, start=(ci == 0), stop=(ci == 7), skip_group_check=True))
                    s.op("tensor", lambda e, ci=ci: e.matmul(
                        ps_s[0:1, 512:1024], ONE1[:], SQv, start=(ci == 0), stop=(ci == 7), skip_group_check=True))
                s.op("vector", lambda e: e.tensor_scalar(R0[:], ps_s[0:1, 0:512], 1.0 / D, None, op0=ALU.mult))
                s.op("vector", lambda e: e.tensor_scalar(R1[:], ps_s[0:1, 512:1024], 1.0 / D, None, op0=ALU.mult))
                s.op("vector", lambda e: e.tensor_tensor(R2[:], R0[:], R0[:], op=ALU.mult))
                s.op("vector", lambda e: e.tensor_tensor(R2[:], R1[:], R2[:], op=ALU.subtract))
                s.op("vector", lambda e: e.tensor_scalar(R2[:], R2[:], 1e-5, None, op0=ALU.add))
                s.op("scalar", lambda e: e.activation(R2[:], R2[:], AF.Sqrt))
                s.op("vector", lambda e: e.reciprocal(R2[:], R2[:]))
                s.dma("sync", st_d[mu_row:mu_row + 1, tg * 512:(tg + 1) * 512], R0[:])
                s.dma("sync", st_d[rstd_row:rstd_row + 1, tg * 512:(tg + 1) * 512], R2[:])

        def load_w(w_d):
            for ci in range(8):
                s.dma("sync", WLv, w_d[ci * 128:(ci + 1) * 128, :])
                s.op("vector", lambda e, ci=ci: e.tensor_copy(A_W[:, ci * 1024:(ci + 1) * 1024], WLv))

        def make_h(src, tg, mu_row, rstd_row):
            s.dma("sync", R0[:], st_d[mu_row:mu_row + 1, tg * 512:(tg + 1) * 512])
            s.dma("sync", R2[:], st_d[rstd_row:rstd_row + 1, tg * 512:(tg + 1) * 512])
            s.op("tensor", lambda e: e.matmul(ps_b[:], ONER[:], R0[:], start=True, stop=True))
            s.op("tensor", lambda e: e.matmul(ps_c[:], ONER[:], R2[:], start=True, stop=True))
            for ci in range(8):
                s.dma("sync", XSv, src[ci * 128:(ci + 1) * 128, tg * 512:(tg + 1) * 512])
                s.op("vector", lambda e: e.tensor_tensor(HFv, XSv, ps_b[:], op=ALU.subtract))
                s.op("vector", lambda e, ci=ci: e.tensor_tensor(HBt[:, ci * 512:(ci + 1) * 512], HFv, ps_c[:], op=ALU.mult))

        stats_sweep(xT, 8, 0, 1)
        stats_sweep(xTo, 4, 2, 3)

        load_w(wk_d)
        for tg in range(8):
            make_h(xT, tg, 0, 1)
            for oc in range(8):
                for ci in range(8):
                    s.op("tensor", lambda e, oc=oc, ci=ci: e.matmul(
                        ps_s[:, 0:512], A_W[:, ci * 1024 + oc * 128: ci * 1024 + oc * 128 + 128],
                        HBt[:, ci * 512:(ci + 1) * 512], start=(ci == 0), stop=(ci == 7)))
                s.op("vector", lambda e, oc=oc, tg=tg: e.tensor_scalar(
                    A_K[:, oc * S + tg * 512: oc * S + tg * 512 + 512], ps_s[:, 0:512],
                    BIA[:, 8 + oc:9 + oc], None, op0=ALU.add))

        load_w(wv_d)
        for tg in range(8):
            make_h(xT, tg, 0, 1)
            for tt4 in range(4):
                tt = tg * 4 + tt4
                for nh in range(2):
                    for ci in range(8):
                        s.op("tensor", lambda e, tt4=tt4, nh=nh, ci=ci: e.matmul(
                            ps_b[:], HBt[:, ci * 512 + tt4 * 128: ci * 512 + tt4 * 128 + 128],
                            A_W[:, ci * 1024 + nh * 512: ci * 1024 + nh * 512 + 512],
                            start=(ci == 0), stop=(ci == 7)))
                    s.op("scalar", lambda e: e.activation(VSv, ps_b[:], AF.Copy))
                    s.op("vector", lambda e, tt=tt, nh=nh: e.tensor_tensor(
                        A_V[:, tt * 1040 + nh * 520: tt * 1040 + nh * 520 + 520].rearrange("p (h d) -> p h d", d=65)[:, :, 0:64],
                        VSv.rearrange("p (h d) -> p h d", d=64),
                        BVRB[:, nh * 512:(nh + 1) * 512].rearrange("p (h d) -> p h d", d=64), op=ALU.add))

        load_w(wq_d)
        for tg in range(4):
            make_h(xTo, tg, 2, 3)
            for oc in range(8):
                for ci in range(8):
                    s.op("tensor", lambda e, oc=oc, ci=ci: e.matmul(
                        ps_s[:, 0:512], A_W[:, ci * 1024 + oc * 128: ci * 1024 + oc * 128 + 128],
                        HBt[:, ci * 512:(ci + 1) * 512], start=(ci == 0), stop=(ci == 7)))
                s.op("vector", lambda e, oc=oc: e.tensor_scalar(QS[:], ps_s[:, 0:512], BIA[:, oc:oc + 1], None, op0=ALU.add))
                s.dma("sync", qT_d[oc * 128:(oc + 1) * 128, tg * 512:(tg + 1) * 512], QS[:])

        MSKA = WS[:]
        for qg in range(4):
            for ci in range(8):
                s.dma("sync", A_W[:, ci * 512:(ci + 1) * 512], qT_d[ci * 128:(ci + 1) * 128, qg * 512:(qg + 1) * 512])
            s.dma("sync", QIF[:], qidx_d[0:1, qg * 512:(qg + 1) * 512].partition_broadcast(128))
            for kc in range(32):
                s.op("vector", lambda e, kc=kc: e.tensor_scalar(
                    MSKA[:, kc * 512:(kc + 1) * 512], QIF[:], KIX[:, kc:kc + 1], None, op0=ALU.is_ge))
            for h in range(16):
                hr = (h % 2) * 64
                hc = (h // 2)
                for kb in range(8):
                    for j in range(4):
                        kc = kb * 4 + j
                        s.op("tensor", lambda e, hr=hr, hc=hc, kc=kc, j=j: e.matmul(
                            ps_s[:, j * 512:(j + 1) * 512],
                            A_K[hr:hr + 64, hc * S + kc * 128: hc * S + kc * 128 + 128],
                            A_W[hr:hr + 64, hc * 512: hc * 512 + 512],
                            start=True, stop=True, skip_group_check=True))
                    s.op("scalar", lambda e: e.activation(A_P[:, 0:2048], ps_s[:], AF.Exp, scale=0.03125))
                    eng = "vector" if kb % 2 == 0 else "gpsimd"
                    s.op(eng, lambda e, kb=kb: e.tensor_tensor(
                        A_P[:, 0:2048], A_P[:, 0:2048], MSKA[:, kb * 2048:(kb + 1) * 2048], op=ALU.mult))
                    for j in range(4):
                        kc = kb * 4 + j
                        s.op("tensor", lambda e, kc=kc, h=h, j=j, kb=kb: e.matmul(
                            ps_o[:], A_V[:, kc * 1040 + 65 * h: kc * 1040 + 65 * h + 65],
                            A_P[:, j * 512:(j + 1) * 512],
                            start=(kb == 0 and j == 0), stop=(kb == 7 and j == 3), skip_group_check=True))
                s.op("vector", lambda e: e.reciprocal(RL[:], ps_o[64:65, :]))
                s.op("tensor", lambda e: e.matmul(ps_b[0:64, :], ONER[:, 0:64], RL[:], start=True, stop=True))
                s.op("scalar", lambda e: e.activation(RLB[:], ps_b[0:64, :], AF.Copy))
                s.op("vector", lambda e: e.tensor_tensor(CTXS[:], ps_o[0:64, :], RLB[:], op=ALU.mult))
                s.dma("sync", ctx_d[h * 64:(h + 1) * 64, qg * 512:(qg + 1) * 512], CTXS[:])

        load_w(wo_d)
        for tg in range(4):
            for ci in range(8):
                s.dma("sync", A_P[:, ci * 512:(ci + 1) * 512], ctx_d[ci * 128:(ci + 1) * 128, tg * 512:(tg + 1) * 512])
            for oc in range(8):
                for ci in range(8):
                    s.op("tensor", lambda e, oc=oc, ci=ci: e.matmul(
                        ps_s[:, 0:512], A_W[:, ci * 1024 + oc * 128: ci * 1024 + oc * 128 + 128],
                        A_P[:, ci * 512:(ci + 1) * 512], start=(ci == 0), stop=(ci == 7)))
                s.op("vector", lambda e, oc=oc: e.tensor_scalar(OA[:, oc * 512:(oc + 1) * 512], ps_s[:, 0:512],
                                                                BIA[:, 16 + oc:17 + oc], None, op0=ALU.add))
            for ci in range(8):
                s.dma("sync", XA[:, ci * 512:(ci + 1) * 512], xTo[ci * 128:(ci + 1) * 128, tg * 512:(tg + 1) * 512])
                s.op("vector", lambda e, ci=ci: e.tensor_tensor(
                    YA[:, ci * 512:(ci + 1) * 512], XA[:, ci * 512:(ci + 1) * 512], OA[:, ci * 512:(ci + 1) * 512], op=ALU.add))
                s.dma("sync", x3T_d[ci * 128:(ci + 1) * 128, tg * 512:(tg + 1) * 512], YA[:, ci * 512:(ci + 1) * 512])
                s.op("scalar", lambda e, ci=ci: e.activation(SQv, YA[:, ci * 512:(ci + 1) * 512], AF.Square))
                s.op("tensor", lambda e, ci=ci: e.matmul(
                    ps_s[0:1, 0:512], ONE1[:], YA[:, ci * 512:(ci + 1) * 512], start=(ci == 0), stop=(ci == 7), skip_group_check=True))
                s.op("tensor", lambda e, ci=ci: e.matmul(
                    ps_s[0:1, 512:1024], ONE1[:], SQv, start=(ci == 0), stop=(ci == 7), skip_group_check=True))
            s.op("vector", lambda e: e.tensor_scalar(R0[:], ps_s[0:1, 0:512], 1.0 / D, None, op0=ALU.mult))
            s.op("vector", lambda e: e.tensor_scalar(R1[:], ps_s[0:1, 512:1024], 1.0 / D, None, op0=ALU.mult))
            s.op("vector", lambda e: e.tensor_tensor(R2[:], R0[:], R0[:], op=ALU.mult))
            s.op("vector", lambda e: e.tensor_tensor(R2[:], R1[:], R2[:], op=ALU.subtract))
            s.op("vector", lambda e: e.tensor_scalar(R2[:], R2[:], 1e-5, None, op0=ALU.add))
            s.op("scalar", lambda e: e.activation(R2[:], R2[:], AF.Sqrt))
            s.op("vector", lambda e: e.reciprocal(R2[:], R2[:]))
            s.op("tensor", lambda e: e.matmul(ps_b[:], ONER[:], R0[:], start=True, stop=True))
            s.op("tensor", lambda e: e.matmul(ps_c[:], ONER[:], R2[:], start=True, stop=True))
            for ci in range(8):
                s.op("vector", lambda e, ci=ci: e.tensor_tensor(HFv, YA[:, ci * 512:(ci + 1) * 512], ps_b[:], op=ALU.subtract))
                s.op("vector", lambda e: e.tensor_tensor(HFv, HFv, ps_c[:], op=ALU.mult))
                s.dma("sync", z0T_d[ci * 128:(ci + 1) * 128, tg * 512:(tg + 1) * 512], HFv)

        s.build(blk)
    return nc


def run_l1_attn(d, x2):
    nc = build_l1_attn()
    x2T_all = t2d(x2)
    kidx = (np.arange(32)[None, :] * 128 + np.arange(128)[:, None]).astype(np.float32)
    in_maps = []
    for c in range(8):
        b, half = c // 2, c % 2
        own = slice(half * TOWN, (half + 1) * TOWN)
        in_maps.append({
            "xT": x2T_all[b],
            "xTo": np.ascontiguousarray(x2T_all[b][:, own]),
            "wq": t2d(d["L1_attn_Wq_w"]), "wk": t2d(d["L1_attn_Wk_w"]),
            "wv": t2d(d["L1_attn_Wv_w"]), "wo": t2d(d["L1_attn_Wo_w"]),
            "bq": bcol(d["L1_attn_Wq_b"]), "bk": bcol(d["L1_attn_Wk_b"]), "bo": bcol(d["L1_attn_Wo_b"]),
            "bvr": d["L1_attn_Wv_b"].reshape(1, D),
            "qidx": np.arange(half * TOWN, (half + 1) * TOWN, dtype=np.float32).reshape(1, TOWN),
            "kidx": kidx,
        })
    res = run_bass_kernel_spmd(nc, in_maps, core_ids=list(range(8)))
    x3 = np.empty((B, S, D), np.float32)
    z0 = np.empty((B, S, D), np.float32)
    for c in range(8):
        b, half = c // 2, c % 2
        x3[b, half * TOWN:(half + 1) * TOWN] = res.results[c]["x3T"].T
        z0[b, half * TOWN:(half + 1) * TOWN] = res.results[c]["z0T"].T
    return x3, z0



HALO = 384
NBUF = HALO + TOWN
NQ = TOWN + 128
DI = 2730
ACH = [128] * 21 + [42]
GW = [(0, 512), (512, 512), (1024, 512), (1536, 512), (2048, NBUF - 2048)]
GQ = [(0, 512), (512, 512), (1024, 512), (1536, 512), (2048, NQ - 2048)]


def build_l1f(last):
    nc = bass.Bass("TRN2", target_bir_lowering=False)
    zH_d = nc.dram_tensor("zH", [D, 1 + NBUF], F32, kind="ExternalInput").ap()
    wqkv_d = nc.dram_tensor("wqkv", [D, 3 * D], F32, kind="ExternalInput").ap()
    wo2_d = nc.dram_tensor("wo2", [D, D], F32, kind="ExternalInput").ap()
    w1_d = nc.dram_tensor("w1", [D, 2 * DI], F32, kind="ExternalInput").ap()
    w2_d = nc.dram_tensor("w2", [DI, D], F32, kind="ExternalInput").ap()
    b1a_d = nc.dram_tensor("b1a", [128, 22], F32, kind="ExternalInput").ap()
    b1g_d = nc.dram_tensor("b1g", [128, 22], F32, kind="ExternalInput").ap()
    b2_d = nc.dram_tensor("b2c", [128, 8], F32, kind="ExternalInput").ap()
    lng_d = nc.dram_tensor("lng", [128, 8], F32, kind="ExternalInput").ap()
    lnb_d = nc.dram_tensor("lnb", [128, 8], F32, kind="ExternalInput").ap()
    qsc_d = nc.dram_tensor("qsc", [128, 1], F32, kind="ExternalInput").ap()
    ksc_d = nc.dram_tensor("ksc", [128, 1], F32, kind="ExternalInput").ap()
    csq_d = nc.dram_tensor("csq", [128, NQ], F32, kind="ExternalInput").ap()
    snq_d = nc.dram_tensor("snq", [128, NQ], F32, kind="ExternalInput").ap()
    csk_d = nc.dram_tensor("csk", [128, NBUF], F32, kind="ExternalInput").ap()
    snk_d = nc.dram_tensor("snk", [128, NBUF], F32, kind="ExternalInput").ap()
    rotm_d = nc.dram_tensor("rotm", [128, 128], BF16, kind="ExternalInput").ap()
    sel2_d = nc.dram_tensor("sel2", [128, 2], F32, kind="ExternalInput").ap()
    bb2_d = nc.dram_tensor("bb2", [2, 128], F32, kind="ExternalInput").ap()
    qidx_d = nc.dram_tensor("qidx", [1, NQ], F32, kind="ExternalInput").ap()
    kix_d = nc.dram_tensor("kix", [128, 19], F32, kind="ExternalInput").ap()
    kix2_d = nc.dram_tensor("kix2", [128, 19], F32, kind="ExternalInput").ap()
    if last:
        xres_d = nc.dram_tensor("xres", [D, TOWN], F32, kind="ExternalInput").ap()
    zo_d = nc.dram_tensor("zo", [D, TOWN], F32, kind="ExternalOutput").ap()
    ctx_d = nc.dram_tensor("ctxh", [D, NQ], BF16).ap()
    z2_d = nc.dram_tensor("z2h", [D, NQ], F32).ap()
    w1b_d = nc.dram_tensor("w1bh", [D, 2 * DI], BF16).ap()
    w2b_d = nc.dram_tensor("w2bh", [DI, D], BF16).ap()

    ex = contextlib.ExitStack()
    with ex:
        sem = ex.enter_context(nc.semaphore("ser"))
        SH = ex.enter_context(nc.sbuf_tensor("SH", [128, 36864], BF16))
        ZB = ex.enter_context(nc.sbuf_tensor("ZB", [128, 8 * NBUF], BF16))
        TB1 = ex.enter_context(nc.sbuf_tensor("TB1", [128, NBUF], F32))
        TB2 = ex.enter_context(nc.sbuf_tensor("TB2", [128, NBUF], F32))
        WOB = ex.enter_context(nc.sbuf_tensor("WOB", [128, 8192], BF16))
        MSKA = ex.enter_context(nc.sbuf_tensor("MSKA", [128, 17 * 384], BF16))
        QIF = ex.enter_context(nc.sbuf_tensor("QIF", [128, 128], F32))
        KIX = ex.enter_context(nc.sbuf_tensor("KIX", [128, 19], F32))
        KIX2 = ex.enter_context(nc.sbuf_tensor("KIX2", [128, 19], F32))
        PP = ex.enter_context(nc.sbuf_tensor("PP", [128, 384], BF16))
        N2 = ex.enter_context(nc.sbuf_tensor("N2", [2, NBUF], F32))
        RL = ex.enter_context(nc.sbuf_tensor("RL", [1, 128], F32))
        RLB = ex.enter_context(nc.sbuf_tensor("RLB", [64, 128], F32))
        CTXS = ex.enter_context(nc.sbuf_tensor("CTXS", [64, 128], BF16))
        ROTM = ex.enter_context(nc.sbuf_tensor("ROTM", [128, 128], BF16))
        SEL2 = ex.enter_context(nc.sbuf_tensor("SEL2", [128, 2], F32))
        BB2 = ex.enter_context(nc.sbuf_tensor("BB2", [2, 128], F32))
        QSC = ex.enter_context(nc.sbuf_tensor("QSC", [128, 2], F32))
        ONER = ex.enter_context(nc.sbuf_tensor("ONER", [1, 128], F32))
        ONE1 = ex.enter_context(nc.sbuf_tensor("ONE1", [128, 1], F32))
        BIA = ex.enter_context(nc.sbuf_tensor("BIA", [128, 68], F32))
        WSL = ex.enter_context(nc.sbuf_tensor("WSL", [128, 512], F32))
        W2S = ex.enter_context(nc.sbuf_tensor("W2S", [128, 128], BF16))
        R0 = ex.enter_context(nc.sbuf_tensor("R0", [1, 512], F32))
        R1 = ex.enter_context(nc.sbuf_tensor("R1", [1, 512], F32))
        R2 = ex.enter_context(nc.sbuf_tensor("R2", [1, 512], F32))
        psA = ex.enter_context(nc.psum_tensor("psA", [128, 512], F32))
        psB = ex.enter_context(nc.psum_tensor("psB", [128, 512], F32))
        psC = ex.enter_context(nc.psum_tensor("psC", [128, 512], F32))
        psD = ex.enter_context(nc.psum_tensor("psD", [65, 128], F32))
        psN = ex.enter_context(nc.psum_tensor("psN", [2, 512], F32))
        blk = ex.enter_context(nc.Block())

        QR = SH[:, 0:4864].bitcast(F32)
        QH = SH[:, 4864:9728].bitcast(F32)
        SQ2 = SH[:, 9728:14592].bitcast(F32)
        QHB = SH[:, 14592:17024]
        Q2 = SH[:, 17024:19200]
        K2 = SH[:, 19200:21632]
        VT = SH[:, 21632:24102]
        WQS = SH[:, 24102:27174]
        XF = SH[:, 0:8192].bitcast(F32)
        XN = SH[:, 8192:12288]
        H2T = SH[:, 12288:23552]
        WSLB = SH[:, 23552:27648]
        GEV = SH[:, 27648:28672].bitcast(F32)
        AEV = SH[:, 28672:29696].bitcast(F32)
        OUV = SH[:, 29696:30720].bitcast(F32)
        CXS = SH[:, 30720:34816]

        s = Ser(nc, sem)
        s.op("vector", lambda e: e.memset(ONER[:], 1.0))
        s.op("vector", lambda e: e.memset(ONE1[:], 1.0))
        s.dma("sync", ROTM[:], rotm_d)
        s.dma("sync", SEL2[:], sel2_d)
        s.dma("sync", BB2[:], bb2_d)
        s.dma("sync", QSC[:, 0:1], qsc_d)
        s.dma("sync", QSC[:, 1:2], ksc_d)
        s.dma("sync", KIX[:], kix_d)
        s.dma("sync", KIX2[:], kix2_d)
        s.dma("sync", BIA[:, 0:22], b1a_d)
        s.dma("sync", BIA[:, 22:44], b1g_d)
        s.dma("sync", BIA[:, 44:52], b2_d)
        s.dma("sync", BIA[:, 52:60], lng_d)
        s.dma("sync", BIA[:, 60:68], lnb_d)

        for ti in range(17):
            s.dma("sync", QIF[:], qidx_d[0:1, ti * 128:(ti + 1) * 128].partition_broadcast(128))
            for c3 in range(3):
                ck = ti + c3
                ms = MSKA[:, (ti * 3 + c3) * 128: (ti * 3 + c3) * 128 + 128]
                s.op("vector", lambda e, ms=ms, ck=ck: e.tensor_scalar(
                    ms, QIF[:], KIX[:, ck:ck + 1], None, op0=ALU.is_ge))
                s.op("vector", lambda e, ms=ms, ck=ck: e.scalar_tensor_tensor(
                    ms, QIF[:], KIX2[:, ck:ck + 1], ms, op0=ALU.is_le, op1=ALU.mult))

        W1G = [(0, 512)] * 0 + [(g * 512, min(512, 5460 - g * 512)) for g in range(11)]
        for ci in range(8):
            for (go, gw) in W1G:
                s.dma("sync", WSL[:, 0:gw], w1_d[ci * 128:(ci + 1) * 128, go:go + gw])
                s.op("vector", lambda e, gw=gw: e.tensor_copy(W2S[:, 0:0] if False else PP[:, 0:gw] if gw <= 384 else QHB[:, 0:gw], WSL[:, 0:gw]))
                s.dma("sync", w1b_d[ci * 128:(ci + 1) * 128, go:go + gw], PP[:, 0:gw] if gw <= 384 else QHB[:, 0:gw])
        for ri in range(22):
            rh = ACH[ri]
            for g2 in range(2):
                s.dma("sync", WSL[0:rh, 0:512], w2_d[ri * 128: ri * 128 + rh, g2 * 512:(g2 + 1) * 512])
                s.op("vector", lambda e, rh=rh: e.tensor_copy(QHB[0:rh, 0:512], WSL[0:rh, 0:512]))
                s.dma("sync", w2b_d[ri * 128: ri * 128 + rh, g2 * 512:(g2 + 1) * 512], QHB[0:rh, 0:512])

        for ci in range(8):
            off = 1 if ci < 4 else 0
            for g in range(5):
                w = 512 if g < 4 else NBUF - 2048
                s.dma("sync", WSL[:, 0:w], zH_d[ci * 128:(ci + 1) * 128, off + g * 512: off + g * 512 + w])
                s.op("vector", lambda e, ci=ci, g=g, w=w: e.tensor_copy(
                    ZB[:, ci * NBUF + g * 512: ci * NBUF + g * 512 + w], WSL[:, 0:w]))

        def rotary(kind):
            boff = 0
            groups = GQ if kind == "q" else GW
            scol = 0 if kind == "q" else 1
            out = Q2 if kind == "q" else K2
            n = NQ if kind == "q" else NBUF
            s.op("scalar", lambda e, n=n: e.activation(SQ2[:, 0:n], QR[:, 0:n], AF.Square))
            for (go, gw) in groups:
                s.op("tensor", lambda e, go=go, gw=gw: e.matmul(
                    psN[:, 0:gw], SEL2[:], SQ2[:, go:go + gw], start=True, stop=True))
                s.op("vector", lambda e, go=go, gw=gw: e.tensor_copy(N2[:, go:go + gw], psN[:, 0:gw]))
            s.op("scalar", lambda e, n=n: e.activation(N2[:, 0:n], N2[:, 0:n], AF.Sqrt))
            s.op("vector", lambda e, n=n: e.tensor_scalar(N2[:, 0:n], N2[:, 0:n], 1e-12, None, op0=ALU.max))
            s.op("vector", lambda e, n=n: e.reciprocal(N2[:, 0:n], N2[:, 0:n]))
            for (go, gw) in groups:
                s.op("tensor", lambda e, go=go, gw=gw: e.matmul(
                    psB[:, 0:gw], BB2[:], N2[:, go:go + gw], start=True, stop=True))
                s.op("vector", lambda e, go=go, gw=gw, scol=scol: e.scalar_tensor_tensor(
                    QH[:, go:go + gw], QR[:, go:go + gw], QSC[:, scol:scol + 1], psB[:, 0:gw],
                    op0=ALU.mult, op1=ALU.mult))
                s.op("vector", lambda e, go=go, gw=gw: e.tensor_copy(QHB[:, go:go + gw], QH[:, go:go + gw]))
                s.op("tensor", lambda e, go=go, gw=gw: e.matmul(
                    psC[:, 0:gw], ROTM[:], QHB[:, go:go + gw], start=True, stop=True))
                s.op("vector", lambda e, go=go, gw=gw, boff=boff: e.tensor_tensor(
                    QH[:, go:go + gw], QH[:, go:go + gw], TB1[:, boff + go: boff + go + gw], op=ALU.mult))
                s.op("vector", lambda e, go=go, gw=gw, boff=boff: e.tensor_tensor(
                    SQ2[:, go:go + gw], psC[:, 0:gw], TB2[:, boff + go: boff + go + gw], op=ALU.mult))
                s.op("vector", lambda e, go=go, gw=gw, out=out: e.tensor_tensor(
                    out[:, go:go + gw], QH[:, go:go + gw], SQ2[:, go:go + gw], op=ALU.add))

        for st in range(8):
            for part in range(3):
                for ci in range(8):
                    s.dma("sync", WSL[:, 0:128], wqkv_d[ci * 128:(ci + 1) * 128, part * D + st * 128: part * D + st * 128 + 128])
                    s.op("vector", lambda e, part=part, ci=ci: e.tensor_copy(
                        WQS[:, (part * 8 + ci) * 128:(part * 8 + ci) * 128 + 128], WSL[:, 0:128]))
            for tt in range(19):
                s.op("gpsimd", lambda e, tt=tt: e.memset(
                    VT[:, tt * 130:(tt + 1) * 130].rearrange("p (h d) -> p h d", d=65)[:, :, 64:65], 1.0))
                for ci in range(8):
                    s.op("tensor", lambda e, tt=tt, ci=ci: e.matmul(
                        psA[:, 0:128], ZB[:, ci * NBUF + tt * 128: ci * NBUF + tt * 128 + 128],
                        WQS[:, (2 * 8 + ci) * 128:(2 * 8 + ci) * 128 + 128],
                        start=(ci == 0), stop=(ci == 7)))
                s.op("vector", lambda e, tt=tt: e.tensor_copy(
                    VT[:, tt * 130:(tt + 1) * 130].rearrange("p (h d) -> p h d", d=65)[:, :, 0:64],
                    psA[:, 0:128].rearrange("p (h d) -> p h d", d=64)))
            s.dma("sync", TB1[:, 0:NBUF], csk_d)
            s.dma("sync", TB2[:, 0:NBUF], snk_d)
            for (go, gw) in GW:
                for ci in range(8):
                    s.op("tensor", lambda e, go=go, gw=gw, ci=ci: e.matmul(
                        psA[:, 0:gw], WQS[:, (1 * 8 + ci) * 128:(1 * 8 + ci) * 128 + 128],
                        ZB[:, ci * NBUF + go: ci * NBUF + go + gw],
                        start=(ci == 0), stop=(ci == 7)))
                s.op("scalar", lambda e, go=go, gw=gw: e.activation(QR[:, go:go + gw], psA[:, 0:gw], AF.Copy))
            rotary("k")
            s.dma("sync", TB1[:, 0:NQ], csq_d)
            s.dma("sync", TB2[:, 0:NQ], snq_d)
            for (go, gw) in GQ:
                for ci in range(8):
                    s.op("tensor", lambda e, go=go, gw=gw, ci=ci: e.matmul(
                        psA[:, 0:gw], WQS[:, (0 * 8 + ci) * 128:(0 * 8 + ci) * 128 + 128],
                        ZB[:, ci * NBUF + 256 + go: ci * NBUF + 256 + go + gw],
                        start=(ci == 0), stop=(ci == 7)))
                s.op("scalar", lambda e, go=go, gw=gw: e.activation(QR[:, go:go + gw], psA[:, 0:gw], AF.Copy))
            rotary("q")
            for ti in range(17):
                for h2 in range(2):
                    hr = h2 * 64
                    for c3 in range(3):
                        ck = ti + c3
                        s.op("tensor", lambda e, hr=hr, ck=ck, c3=c3, ti=ti: e.matmul(
                            psA[:, c3 * 128:(c3 + 1) * 128],
                            K2[hr:hr + 64, ck * 128: ck * 128 + 128],
                            Q2[hr:hr + 64, ti * 128: ti * 128 + 128],
                            start=True, stop=True, skip_group_check=True))
                    s.op("scalar", lambda e: e.activation(PP[:], psA[:, 0:384], AF.Exp, scale=8.0))
                    s.op("vector", lambda e, ti=ti: e.tensor_tensor(
                        PP[:], PP[:], MSKA[:, ti * 384:(ti + 1) * 384], op=ALU.mult))
                    for c3 in range(3):
                        ck = ti + c3
                        s.op("tensor", lambda e, ck=ck, h2=h2, c3=c3: e.matmul(
                            psD[:], VT[:, ck * 130 + h2 * 65: ck * 130 + h2 * 65 + 65],
                            PP[:, c3 * 128:(c3 + 1) * 128],
                            start=(c3 == 0), stop=(c3 == 2), skip_group_check=True))
                    s.op("vector", lambda e: e.tensor_scalar(RL[:], psD[64:65, :], 1e-20, None, op0=ALU.add))
                    s.op("vector", lambda e: e.reciprocal(RL[:], RL[:]))
                    s.op("tensor", lambda e: e.matmul(psB[0:64, 0:128], ONER[:, 0:64], RL[:], start=True, stop=True))
                    s.op("scalar", lambda e: e.activation(RLB[:], psB[0:64, 0:128], AF.Copy))
                    s.op("vector", lambda e: e.tensor_tensor(CTXS[:], psD[0:64, :], RLB[:], op=ALU.mult))
                    s.dma("sync", ctx_d[(st * 2 + h2) * 64:(st * 2 + h2) * 64 + 64, ti * 128:(ti + 1) * 128], CTXS[:])

        for ci in range(8):
            s.dma("sync", WSL[:, 0:512], wo2_d[ci * 128:(ci + 1) * 128, 0:512])
            s.op("vector", lambda e, ci=ci: e.tensor_copy(WOB[:, ci * 1024: ci * 1024 + 512], WSL[:, 0:512]))
            s.dma("sync", WSL[:, 0:512], wo2_d[ci * 128:(ci + 1) * 128, 512:1024])
            s.op("vector", lambda e, ci=ci: e.tensor_copy(WOB[:, ci * 1024 + 512: ci * 1024 + 1024], WSL[:, 0:512]))
        for (go, gw) in GQ:
            for ci in range(8):
                s.dma("sync", CXS[:, ci * 512: ci * 512 + gw], ctx_d[ci * 128:(ci + 1) * 128, go:go + gw])
            for oc in range(8):
                for ci in range(8):
                    s.op("tensor", lambda e, oc=oc, ci=ci, gw=gw: e.matmul(
                        psA[:, 0:gw], WOB[:, ci * 1024 + oc * 128: ci * 1024 + oc * 128 + 128],
                        CXS[:, ci * 512: ci * 512 + gw], start=(ci == 0), stop=(ci == 7)))
                s.dma("sync", WSL[:, 0:gw], zH_d[oc * 128:(oc + 1) * 128, 1 + 256 + go: 1 + 256 + go + gw])
                s.op("vector", lambda e, gw=gw: e.tensor_tensor(OUV[:, 0:gw], psA[:, 0:gw], WSL[:, 0:gw], op=ALU.add))
                s.dma("sync", z2_d[oc * 128:(oc + 1) * 128, go:go + gw], OUV[:, 0:gw])

        for tg in range(4):
            for ci in range(8):
                off = 128 if ci < 4 else 127
                s.dma("sync", XF[:, ci * 512:(ci + 1) * 512], z2_d[ci * 128:(ci + 1) * 128, off + tg * 512: off + tg * 512 + 512])
            for ci in range(8):
                s.op("scalar", lambda e, ci=ci: e.activation(GEV[:], XF[:, ci * 512:(ci + 1) * 512], AF.Square))
                s.op("tensor", lambda e, ci=ci: e.matmul(
                    psB[0:1, 0:512], ONE1[:], XF[:, ci * 512:(ci + 1) * 512], start=(ci == 0), stop=(ci == 7), skip_group_check=True))
                s.op("tensor", lambda e, ci=ci: e.matmul(
                    psC[0:1, 0:512], ONE1[:], GEV[:], start=(ci == 0), stop=(ci == 7), skip_group_check=True))
            s.op("vector", lambda e: e.tensor_scalar(R0[:], psB[0:1, 0:512], 1.0 / D, None, op0=ALU.mult))
            s.op("vector", lambda e: e.tensor_scalar(R1[:], psC[0:1, 0:512], 1.0 / D, None, op0=ALU.mult))
            s.op("vector", lambda e: e.tensor_tensor(R2[:], R0[:], R0[:], op=ALU.mult))
            s.op("vector", lambda e: e.tensor_tensor(R2[:], R1[:], R2[:], op=ALU.subtract))
            s.op("vector", lambda e: e.tensor_scalar(R2[:], R2[:], 1e-5, None, op0=ALU.add))
            s.op("scalar", lambda e: e.activation(R2[:], R2[:], AF.Sqrt))
            s.op("vector", lambda e: e.reciprocal(R2[:], R2[:]))
            s.op("tensor", lambda e: e.matmul(psB[:], ONER[:], R0[:], start=True, stop=True))
            s.op("tensor", lambda e: e.matmul(psC[:], ONER[:], R2[:], start=True, stop=True))
            for ci in range(8):
                s.op("vector", lambda e, ci=ci: e.tensor_tensor(GEV[:], XF[:, ci * 512:(ci + 1) * 512], psB[:], op=ALU.subtract))
                s.op("vector", lambda e: e.tensor_tensor(GEV[:], GEV[:], psC[:], op=ALU.mult))
                s.op("vector", lambda e, ci=ci: e.tensor_scalar(
                    XN[:, ci * 512:(ci + 1) * 512], GEV[:], BIA[:, 52 + ci:53 + ci], BIA[:, 60 + ci:61 + ci], op0=ALU.mult, op1=ALU.add))
            for ac in range(22):
                aw = ACH[ac]
                for ci in range(8):
                    s.dma("sync", WSLB[:, ci * 128: ci * 128 + aw], w1b_d[ci * 128:(ci + 1) * 128, ac * 128: ac * 128 + aw])
                    s.dma("sync", WSLB[:, 1024 + ci * 128: 1024 + ci * 128 + aw], w1b_d[ci * 128:(ci + 1) * 128, DI + ac * 128: DI + ac * 128 + aw])
                for ci in range(8):
                    s.op("tensor", lambda e, ac=ac, aw=aw, ci=ci: e.matmul(
                        psA[0:aw, :] if aw < 128 else psA[:], WSLB[:, ci * 128: ci * 128 + aw],
                        XN[:, ci * 512:(ci + 1) * 512], start=(ci == 0), stop=(ci == 7), skip_group_check=True))
                for ci in range(8):
                    s.op("tensor", lambda e, ac=ac, aw=aw, ci=ci: e.matmul(
                        psB[0:aw, :] if aw < 128 else psB[:], WSLB[:, 1024 + ci * 128: 1024 + ci * 128 + aw],
                        XN[:, ci * 512:(ci + 1) * 512], start=(ci == 0), stop=(ci == 7), skip_group_check=True))
                s.op("vector", lambda e, ac=ac, aw=aw: e.tensor_scalar(
                    AEV[0:aw, :] if aw < 128 else AEV[:], psA[0:aw, :] if aw < 128 else psA[:],
                    BIA[0:aw, ac:ac + 1] if aw < 128 else BIA[:, ac:ac + 1], None, op0=ALU.add))
                s.op("scalar", lambda e, ac=ac, aw=aw: e.activation(
                    GEV[0:aw, :] if aw < 128 else GEV[:], psB[0:aw, :] if aw < 128 else psB[:],
                    AF.Gelu, bias=BIA[0:aw, 22 + ac:23 + ac] if aw < 128 else BIA[:, 22 + ac:23 + ac]))
                s.op("vector", lambda e, ac=ac, aw=aw: e.tensor_tensor(
                    H2T[0:aw, ac * 512:(ac + 1) * 512] if aw < 128 else H2T[:, ac * 512:(ac + 1) * 512],
                    AEV[0:aw, :] if aw < 128 else AEV[:], GEV[0:aw, :] if aw < 128 else GEV[:], op=ALU.mult))
            for oc in range(8):
                for ri in range(22):
                    rh = ACH[ri]
                    s.dma("sync", W2S[0:rh, :], w2b_d[ri * 128: ri * 128 + rh, oc * 128:(oc + 1) * 128])
                    s.op("tensor", lambda e, ri=ri, rh=rh: e.matmul(
                        psA[:], W2S[0:rh, :], H2T[0:rh, ri * 512:(ri + 1) * 512],
                        start=(ri == 0), stop=(ri == 21), skip_group_check=True))
                s.op("vector", lambda e, oc=oc: e.tensor_scalar(OUV[:], psA[:], BIA[:, 44 + oc:45 + oc], None, op0=ALU.add))
                s.dma("sync", WSL[:], z2_d[oc * 128:(oc + 1) * 128, 128 + tg * 512: 128 + tg * 512 + 512])
                s.op("vector", lambda e: e.tensor_tensor(OUV[:], OUV[:], WSL[:], op=ALU.add))
                if last:
                    s.dma("sync", WSL[:], xres_d[oc * 128:(oc + 1) * 128, tg * 512:(tg + 1) * 512])
                    s.op("vector", lambda e: e.tensor_tensor(OUV[:], OUV[:], WSL[:], op=ALU.add))
                s.dma("sync", zo_d[oc * 128:(oc + 1) * 128, tg * 512:(tg + 1) * 512], OUV[:])

        s.build(blk)
    return nc


def rot_tables_host(own0):
    half = np.arange(0, DH, 2, dtype=np.float32)
    inv_freq = (1.0 / 10000.0 ** (half / DH)).astype(np.float32)
    base_scale = ((half + 0.4 * DH) / (1.4 * DH)).astype(np.float32)

    def tabs(pos):
        fr = pos[:, None] * inv_freq[None, :]
        fr64 = np.concatenate([fr, fr], 1)
        sc = base_scale[None, :] ** ((pos - S // 2) / 128.0)[:, None]
        sc64 = np.concatenate([sc, sc], 1).astype(np.float32)
        cos, sin = np.cos(fr64), np.sin(fr64)
        csq = (cos * sc64).T
        snq = (sin * sc64).T
        csk = (cos / sc64).T
        snk = (sin / sc64).T
        stack = lambda a: np.ascontiguousarray(np.concatenate([a, a], 0), dtype=np.float32)
        return stack(csq), stack(snq), stack(csk), stack(snk)
    pk = (own0 - HALO + np.arange(NBUF)).astype(np.float32)
    pq = (own0 - 128 + np.arange(NQ)).astype(np.float32)
    csq, snq, _, _ = tabs(pq)
    _, _, csk, snk = tabs(pk)
    return csq, snq, csk, snk


def bchunk(v, widths):
    out = np.zeros((128, len(widths)), np.float32)
    o = 0
    for i, w in enumerate(widths):
        out[0:w, i] = v[o:o + w]
        o += w
    return out


def l1f_inmaps(d, i, z, last, x3=None):
    rotm = np.zeros((128, 128), np.float32)
    for dd in range(128):
        m = dd % 64
        if m < 32:
            rotm[dd, (dd // 64) * 64 + m + 32] = -1.0
        else:
            rotm[dd, (dd // 64) * 64 + m - 32] = 1.0
    rotmT = np.ascontiguousarray(rotm.T).astype(BF16NP)
    sel2 = np.zeros((128, 2), np.float32); sel2[0:64, 0] = 1; sel2[64:128, 1] = 1
    bb2 = np.zeros((2, 128), np.float32); bb2[0, 0:64] = 1; bb2[1, 64:128] = 1
    zT = t2d(z)
    maps = []
    for c in range(8):
        b, halfc = c // 2, c % 2
        own0 = halfc * TOWN
        zH = np.zeros((D, 1 + NBUF), np.float32)
        lo = own0 - HALO - 1
        src_lo = max(0, lo)
        zH[:, 1 + NBUF - (own0 + TOWN - src_lo) - 0 - 0:] = 0
        seg = zT[b][:, src_lo: own0 + TOWN]
        zH[:, 1 + NBUF - seg.shape[1]:] = seg
        csq, snq, csk, snk = rot_tables_host(own0)
        qidx = (own0 - 128 + np.arange(NQ)).astype(np.float32)
        qidx[qidx < 0] = -1e6
        kbuf = (own0 - HALO + np.arange(NBUF)).astype(np.float32)
        kbuf[kbuf < 0] = -1e6
        kix = np.zeros((128, 19), np.float32)
        for ck in range(19):
            kix[:, ck] = kbuf[ck * 128:(ck + 1) * 128]
        kix2 = kix + 256.0
        mp = {
            "zH": zH, "wqkv": d["L1_ffn_Wqkv"][i], "wo2": d["L1_ffn_Wo"][i],
            "w1": d["L1_ffn_W1"][i], "w2": d["L1_ffn_W2"][i],
            "b1a": bchunk(d["L1_ffn_b1"][i][:DI], ACH), "b1g": bchunk(d["L1_ffn_b1"][i][DI:], ACH),
            "b2c": bcol(d["L1_ffn_b2"][i]), "lng": bcol(d["L1_ffn_ln_g"][i]), "lnb": bcol(d["L1_ffn_ln_b"][i]),
            "qsc": np.tile(d["L1_ffn_q_scale"][i], 2).reshape(128, 1).astype(np.float32),
            "ksc": np.tile(d["L1_ffn_k_scale"][i], 2).reshape(128, 1).astype(np.float32),
            "csq": csq[:, 0:NQ], "snq": snq[:, 0:NQ], "csk": csk[:, 0:NBUF], "snk": snk[:, 0:NBUF],
            "rotm": rotmT, "sel2": sel2, "bb2": bb2,
            "qidx": qidx.reshape(1, NQ), "kix": kix, "kix2": kix2,
        }
        if last:
            mp["xres"] = np.ascontiguousarray(t2d(x3[b])[:, own0:own0 + TOWN])
        maps.append(mp)
    return maps


def run_l1f(nc, maps):
    res = run_bass_kernel_spmd(nc, maps, core_ids=list(range(8)))
    z = np.empty((B, S, D), np.float32)
    for c in range(8):
        b, halfc = c // 2, c % 2
        z[b, halfc * TOWN:(halfc + 1) * TOWN] = res.results[c]["zo"].T
    return z




def kernel(**inputs):
    d = {k: np.asarray(v, dtype=np.float32) for k, v in inputs.items()}
    x1 = run_l0_attn(d)
    x2 = run_l0_ffn(d, x1)
    x3, z = run_l1_attn(d, x2)
    nc_mid = build_l1f(False)
    for i in range(5):
        z = run_l1f(nc_mid, l1f_inmaps(d, i, z, False))
    nc_last = build_l1f(True)
    x4 = run_l1f(nc_last, l1f_inmaps(d, 5, z, True, x3))
    return x4




# version 04
# speedup vs baseline: 1.3450x; 1.3450x over previous
import numpy as np
import ml_dtypes
import contextlib
import concourse.bass as bass
import concourse.mybir as mybir
from concourse.bass_utils import run_bass_kernel_spmd

F32 = mybir.dt.float32
BF16 = mybir.dt.bfloat16
AF = mybir.ActivationFunctionType
ALU = mybir.AluOpType
AX = mybir.AxisListType
BF16NP = ml_dtypes.bfloat16

ENGINES = ["sync", "scalar", "vector", "tensor", "gpsimd"]

B, S, D, H, DH = 4, 4096, 1024, 16, 64
TOWN = S // 2


class Ser:

    def __init__(self, nc, sem):
        self.nc = nc
        self.sem = sem
        self.ops = []

    def op(self, engine, fn, inc=1):
        self.ops.append((engine, fn, inc))

    def dma(self, engine, out, in_):
        self.op(engine, lambda e: e.dma_start(out, in_), 16)

    RELAX = ("tensor", "sync")

    def build(self, block):
        ops, sem = self.ops, self.sem
        waits = []
        cum = 0
        prev_e = None
        run_wait = 0
        for (e, fn, inc) in ops:
            if e in self.RELAX and e == prev_e:
                waits.append(run_wait)
            else:
                waits.append(cum)
                run_wait = cum
            prev_e = e
            cum += inc
        total = cum

        def mk(ename):
            def body(eng):
                cum = 0
                for idx, (e, fn, inc) in enumerate(ops):
                    if e == ename:
                        eng.wait_ge(sem, waits[idx])
                        fn(eng).then_inc(sem, inc)
                    cum += inc
                eng.wait_ge(sem, total)
            return body

        for ename in ENGINES:
            getattr(block, ename)(mk(ename))


def t2d(x):
    return np.ascontiguousarray(np.swapaxes(x, -1, -2))


def bcol(v):
    return np.ascontiguousarray(v.reshape(8, 128).T)


def pe_table():
    pos = np.arange(S, dtype=np.float32)[:, None]
    div = np.exp(np.arange(0, D, 2, dtype=np.float32) * (-np.log(10000.0) / D))
    pe = np.zeros((S, D), np.float32)
    pe[:, 0::2] = np.sin(pos * div)
    pe[:, 1::2] = np.cos(pos * div)
    return pe



def build_l0_attn():
    nc = bass.Bass("TRN2", target_bir_lowering=False)
    xT = nc.dram_tensor("xT", [D, S], F32, kind="ExternalInput").ap()
    xTo = nc.dram_tensor("xTo", [D, TOWN], F32, kind="ExternalInput").ap()
    peT_d = nc.dram_tensor("peT", [D, S], F32, kind="ExternalInput").ap()
    peTo_d = nc.dram_tensor("peTo", [D, TOWN], F32, kind="ExternalInput").ap()
    wq_d = nc.dram_tensor("wq", [D, D], F32, kind="ExternalInput").ap()
    wk_d = nc.dram_tensor("wk", [D, D], F32, kind="ExternalInput").ap()
    wv_d = nc.dram_tensor("wv", [D, D], F32, kind="ExternalInput").ap()
    wo_d = nc.dram_tensor("wo", [D, D], F32, kind="ExternalInput").ap()
    bq_d = nc.dram_tensor("bq", [128, 8], F32, kind="ExternalInput").ap()
    bk_d = nc.dram_tensor("bk", [128, 8], F32, kind="ExternalInput").ap()
    bo_d = nc.dram_tensor("bo", [128, 8], F32, kind="ExternalInput").ap()
    gam_d = nc.dram_tensor("gam", [128, 8], F32, kind="ExternalInput").ap()
    bet_d = nc.dram_tensor("bet", [128, 8], F32, kind="ExternalInput").ap()
    bvr_d = nc.dram_tensor("bvr", [1, D], F32, kind="ExternalInput").ap()
    x1T_d = nc.dram_tensor("x1T", [D, TOWN], F32, kind="ExternalOutput").ap()
    ctx_d = nc.dram_tensor("ctxh", [D, TOWN], BF16).ap()
    qT_d = nc.dram_tensor("qTh", [D, TOWN], BF16).ap()
    st_d = nc.dram_tensor("sth", [4, S], F32).ap()

    ex = contextlib.ExitStack()
    with ex:
        sem = ex.enter_context(nc.semaphore("ser"))
        A_K = ex.enter_context(nc.sbuf_tensor("A_K", [128, 8 * S], BF16))
        A_V = ex.enter_context(nc.sbuf_tensor("A_V", [128, 32 * 1040], BF16))
        A_W = ex.enter_context(nc.sbuf_tensor("A_W", [128, 8192], BF16))
        A_P = ex.enter_context(nc.sbuf_tensor("A_P", [128, 4096], BF16))
        WL = ex.enter_context(nc.sbuf_tensor("WL", [128, 1024], F32))
        XS = ex.enter_context(nc.sbuf_tensor("XS", [128, 512], F32))
        SQ = ex.enter_context(nc.sbuf_tensor("SQ", [128, 512], F32))
        HF = ex.enter_context(nc.sbuf_tensor("HF", [128, 512], F32))
        PEs = ex.enter_context(nc.sbuf_tensor("PEs", [128, 512], F32))
        HBt = ex.enter_context(nc.sbuf_tensor("HBt", [128, 4096], BF16))
        QS = ex.enter_context(nc.sbuf_tensor("QS", [128, 512], BF16))
        R0 = ex.enter_context(nc.sbuf_tensor("R0", [1, 512], F32))
        R1 = ex.enter_context(nc.sbuf_tensor("R1", [1, 512], F32))
        R2 = ex.enter_context(nc.sbuf_tensor("R2", [1, 512], F32))
        RL = ex.enter_context(nc.sbuf_tensor("RL", [1, 512], F32))
        RLB = ex.enter_context(nc.sbuf_tensor("RLB", [64, 512], F32))
        CTXS = ex.enter_context(nc.sbuf_tensor("CTXS", [64, 512], BF16))
        ONE1 = ex.enter_context(nc.sbuf_tensor("ONE1", [128, 1], F32))
        ONER = ex.enter_context(nc.sbuf_tensor("ONER", [1, 128], F32))
        BIA = ex.enter_context(nc.sbuf_tensor("BIA", [128, 40], F32))
        BVRB = ex.enter_context(nc.sbuf_tensor("BVRB", [128, D], F32))
        VS = ex.enter_context(nc.sbuf_tensor("VS", [128, 512], F32))
        ps_s = ex.enter_context(nc.psum_tensor("ps_s", [128, 2048], F32))
        ps_o = ex.enter_context(nc.psum_tensor("ps_o", [65, 512], F32))
        ps_b = ex.enter_context(nc.psum_tensor("ps_b", [128, 512], F32))
        ps_c = ex.enter_context(nc.psum_tensor("ps_c", [128, 512], F32))
        blk = ex.enter_context(nc.Block())

        OA = A_V[:, 0:4096]
        YA = A_V[:, 8192:16384].bitcast(F32)
        XA = A_V[:, 16384:24576].bitcast(F32)

        s = Ser(nc, sem)
        s.op("vector", lambda e: e.memset(ONE1[:], 1.0))
        s.op("vector", lambda e: e.memset(ONER[:], 1.0))
        s.dma("sync", BIA[:, 0:8], bq_d)
        s.dma("sync", BIA[:, 8:16], bk_d)
        s.dma("sync", BIA[:, 16:24], bo_d)
        s.dma("sync", BIA[:, 24:32], gam_d)
        s.dma("sync", BIA[:, 32:40], bet_d)
        s.dma("sync", BVRB[:], bvr_d.partition_broadcast(128))
        for tt in range(32):
            s.op("gpsimd", lambda e, tt=tt: e.memset(
                A_V[:, tt * 1040:(tt + 1) * 1040].rearrange("p (h d) -> p h d", d=65)[:, :, 64:65], 1.0))

        def stats_sweep(src, ntg, mu_row, rstd_row):
            for tg in range(ntg):
                for ci in range(8):
                    s.dma("sync", XS[:], src[ci * 128:(ci + 1) * 128, tg * 512:(tg + 1) * 512])
                    s.op("scalar", lambda e: e.activation(SQ[:], XS[:], AF.Square))
                    s.op("tensor", lambda e, ci=ci: e.matmul(
                        ps_s[0:1, 0:512], ONE1[:], XS[:], start=(ci == 0), stop=(ci == 7), skip_group_check=True))
                    s.op("tensor", lambda e, ci=ci: e.matmul(
                        ps_s[0:1, 512:1024], ONE1[:], SQ[:], start=(ci == 0), stop=(ci == 7), skip_group_check=True))
                s.op("vector", lambda e: e.tensor_scalar(R0[:], ps_s[0:1, 0:512], 1.0 / D, None, op0=ALU.mult))
                s.op("vector", lambda e: e.tensor_scalar(R1[:], ps_s[0:1, 512:1024], 1.0 / D, None, op0=ALU.mult))
                s.op("vector", lambda e: e.tensor_tensor(R2[:], R0[:], R0[:], op=ALU.mult))
                s.op("vector", lambda e: e.tensor_tensor(R2[:], R1[:], R2[:], op=ALU.subtract))
                s.op("vector", lambda e: e.tensor_scalar(R2[:], R2[:], 1e-5, None, op0=ALU.add))
                s.op("scalar", lambda e: e.activation(R2[:], R2[:], AF.Sqrt))
                s.op("vector", lambda e: e.reciprocal(R2[:], R2[:]))
                s.dma("sync", st_d[mu_row:mu_row + 1, tg * 512:(tg + 1) * 512], R0[:])
                s.dma("sync", st_d[rstd_row:rstd_row + 1, tg * 512:(tg + 1) * 512], R2[:])

        def load_w(w_d):
            for ci in range(8):
                s.dma("sync", WL[:], w_d[ci * 128:(ci + 1) * 128, :])
                s.op("vector", lambda e, ci=ci: e.tensor_copy(A_W[:, ci * 1024:(ci + 1) * 1024], WL[:]))

        def make_h(src, pe_src, tg, mu_row, rstd_row, add_pe=True):
            s.dma("sync", R0[:], st_d[mu_row:mu_row + 1, tg * 512:(tg + 1) * 512])
            s.dma("sync", R2[:], st_d[rstd_row:rstd_row + 1, tg * 512:(tg + 1) * 512])
            s.op("tensor", lambda e: e.matmul(ps_b[:], ONER[:], R0[:], start=True, stop=True))
            s.op("tensor", lambda e: e.matmul(ps_c[:], ONER[:], R2[:], start=True, stop=True))
            for ci in range(8):
                s.dma("sync", XS[:], src[ci * 128:(ci + 1) * 128, tg * 512:(tg + 1) * 512])
                s.op("vector", lambda e: e.tensor_tensor(HF[:], XS[:], ps_b[:], op=ALU.subtract))
                if add_pe:
                    s.dma("sync", PEs[:], pe_src[ci * 128:(ci + 1) * 128, tg * 512:(tg + 1) * 512])
                    s.op("vector", lambda e: e.tensor_tensor(HF[:], HF[:], ps_c[:], op=ALU.mult))
                    s.op("vector", lambda e, ci=ci: e.tensor_tensor(HBt[:, ci * 512:(ci + 1) * 512], HF[:], PEs[:], op=ALU.add))
                else:
                    s.op("vector", lambda e, ci=ci: e.tensor_tensor(HBt[:, ci * 512:(ci + 1) * 512], HF[:], ps_c[:], op=ALU.mult))

        stats_sweep(xT, 8, 0, 1)
        stats_sweep(xTo, 4, 2, 3)

        load_w(wk_d)
        for tg in range(8):
            make_h(xT, peT_d, tg, 0, 1)
            for oc in range(8):
                for ci in range(8):
                    s.op("tensor", lambda e, oc=oc, ci=ci: e.matmul(
                        ps_s[:, 0:512], A_W[:, ci * 1024 + oc * 128: ci * 1024 + oc * 128 + 128],
                        HBt[:, ci * 512:(ci + 1) * 512], start=(ci == 0), stop=(ci == 7)))
                s.op("vector", lambda e, oc=oc, tg=tg: e.tensor_scalar(
                    A_K[:, oc * S + tg * 512: oc * S + tg * 512 + 512], ps_s[:, 0:512],
                    BIA[:, 8 + oc:9 + oc], None, op0=ALU.add))

        load_w(wv_d)
        for tg in range(8):
            make_h(xT, peT_d, tg, 0, 1)
            for tt4 in range(4):
                tt = tg * 4 + tt4
                for nh in range(2):
                    for ci in range(8):
                        s.op("tensor", lambda e, tt4=tt4, nh=nh, ci=ci: e.matmul(
                            ps_b[:], HBt[:, ci * 512 + tt4 * 128: ci * 512 + tt4 * 128 + 128],
                            A_W[:, ci * 1024 + nh * 512: ci * 1024 + nh * 512 + 512],
                            start=(ci == 0), stop=(ci == 7)))
                    s.op("scalar", lambda e: e.activation(VS[:], ps_b[:], AF.Copy))
                    s.op("vector", lambda e, tt=tt, nh=nh: e.tensor_tensor(
                        A_V[:, tt * 1040 + nh * 520: tt * 1040 + nh * 520 + 520].rearrange("p (h d) -> p h d", d=65)[:, :, 0:64],
                        VS[:].rearrange("p (h d) -> p h d", d=64),
                        BVRB[:, nh * 512:(nh + 1) * 512].rearrange("p (h d) -> p h d", d=64), op=ALU.add))

        load_w(wq_d)
        for tg in range(4):
            make_h(xTo, peTo_d, tg, 2, 3)
            for oc in range(8):
                for ci in range(8):
                    s.op("tensor", lambda e, oc=oc, ci=ci: e.matmul(
                        ps_s[:, 0:512], A_W[:, ci * 1024 + oc * 128: ci * 1024 + oc * 128 + 128],
                        HBt[:, ci * 512:(ci + 1) * 512], start=(ci == 0), stop=(ci == 7)))
                s.op("vector", lambda e, oc=oc: e.tensor_scalar(QS[:], ps_s[:, 0:512], BIA[:, oc:oc + 1], None, op0=ALU.add))
                s.dma("sync", qT_d[oc * 128:(oc + 1) * 128, tg * 512:(tg + 1) * 512], QS[:])

        for qg in range(4):
            for ci in range(8):
                s.dma("sync", A_W[:, ci * 512:(ci + 1) * 512], qT_d[ci * 128:(ci + 1) * 128, qg * 512:(qg + 1) * 512])
            for h in range(16):
                hr = (h % 2) * 64
                hc = (h // 2)
                for kb in range(8):
                    for j in range(4):
                        kc = kb * 4 + j
                        s.op("tensor", lambda e, hr=hr, hc=hc, kc=kc, j=j: e.matmul(
                            ps_s[:, j * 512:(j + 1) * 512],
                            A_K[hr:hr + 64, hc * S + kc * 128: hc * S + kc * 128 + 128],
                            A_W[hr:hr + 64, hc * 512: hc * 512 + 512],
                            start=True, stop=True, skip_group_check=True))
                    s.op("scalar", lambda e: e.activation(A_P[:, 0:2048], ps_s[:], AF.Exp, scale=0.125))
                    for j in range(4):
                        kc = kb * 4 + j
                        s.op("tensor", lambda e, kc=kc, h=h, j=j, kb=kb: e.matmul(
                            ps_o[:], A_V[:, kc * 1040 + 65 * h: kc * 1040 + 65 * h + 65],
                            A_P[:, j * 512:(j + 1) * 512],
                            start=(kb == 0 and j == 0), stop=(kb == 7 and j == 3), skip_group_check=True))
                s.op("vector", lambda e: e.reciprocal(RL[:], ps_o[64:65, :]))
                s.op("tensor", lambda e: e.matmul(ps_b[0:64, :], ONER[:, 0:64], RL[:], start=True, stop=True))
                s.op("scalar", lambda e: e.activation(RLB[:], ps_b[0:64, :], AF.Copy))
                s.op("vector", lambda e: e.tensor_tensor(CTXS[:], ps_o[0:64, :], RLB[:], op=ALU.mult))
                s.dma("sync", ctx_d[h * 64:(h + 1) * 64, qg * 512:(qg + 1) * 512], CTXS[:])

        load_w(wo_d)
        for tg in range(4):
            for ci in range(8):
                s.dma("sync", A_P[:, ci * 512:(ci + 1) * 512], ctx_d[ci * 128:(ci + 1) * 128, tg * 512:(tg + 1) * 512])
            for oc in range(8):
                for ci in range(8):
                    s.op("tensor", lambda e, oc=oc, ci=ci: e.matmul(
                        ps_s[:, 0:512], A_W[:, ci * 1024 + oc * 128: ci * 1024 + oc * 128 + 128],
                        A_P[:, ci * 512:(ci + 1) * 512], start=(ci == 0), stop=(ci == 7)))
                s.op("vector", lambda e, oc=oc: e.tensor_scalar(OA[:, oc * 512:(oc + 1) * 512], ps_s[:, 0:512],
                                                                BIA[:, 16 + oc:17 + oc], None, op0=ALU.add))
            s.dma("sync", R0[:], st_d[2:3, tg * 512:(tg + 1) * 512])
            s.dma("sync", R2[:], st_d[3:4, tg * 512:(tg + 1) * 512])
            s.op("tensor", lambda e: e.matmul(ps_b[:], ONER[:], R0[:], start=True, stop=True))
            s.op("tensor", lambda e: e.matmul(ps_c[:], ONER[:], R2[:], start=True, stop=True))
            for ci in range(8):
                s.dma("sync", XA[:, ci * 512:(ci + 1) * 512], xTo[ci * 128:(ci + 1) * 128, tg * 512:(tg + 1) * 512])
                s.dma("sync", PEs[:], peTo_d[ci * 128:(ci + 1) * 128, tg * 512:(tg + 1) * 512])
                s.op("vector", lambda e, ci=ci: e.tensor_tensor(HF[:], XA[:, ci * 512:(ci + 1) * 512], ps_b[:], op=ALU.subtract))
                s.op("vector", lambda e: e.tensor_tensor(HF[:], HF[:], ps_c[:], op=ALU.mult))
                s.op("vector", lambda e: e.tensor_tensor(HF[:], HF[:], PEs[:], op=ALU.add))
                s.op("vector", lambda e, ci=ci: e.tensor_tensor(
                    YA[:, ci * 512:(ci + 1) * 512], HF[:], OA[:, ci * 512:(ci + 1) * 512], op=ALU.add))
                s.op("scalar", lambda e, ci=ci: e.activation(SQ[:], YA[:, ci * 512:(ci + 1) * 512], AF.Square))
                s.op("tensor", lambda e, ci=ci: e.matmul(
                    ps_s[0:1, 0:512], ONE1[:], YA[:, ci * 512:(ci + 1) * 512], start=(ci == 0), stop=(ci == 7), skip_group_check=True))
                s.op("tensor", lambda e, ci=ci: e.matmul(
                    ps_s[0:1, 512:1024], ONE1[:], SQ[:], start=(ci == 0), stop=(ci == 7), skip_group_check=True))
            s.op("vector", lambda e: e.tensor_scalar(R0[:], ps_s[0:1, 0:512], 1.0 / D, None, op0=ALU.mult))
            s.op("vector", lambda e: e.tensor_scalar(R1[:], ps_s[0:1, 512:1024], 1.0 / D, None, op0=ALU.mult))
            s.op("vector", lambda e: e.tensor_tensor(R2[:], R0[:], R0[:], op=ALU.mult))
            s.op("vector", lambda e: e.tensor_tensor(R2[:], R1[:], R2[:], op=ALU.subtract))
            s.op("vector", lambda e: e.tensor_scalar(R2[:], R2[:], 1e-5, None, op0=ALU.add))
            s.op("scalar", lambda e: e.activation(R2[:], R2[:], AF.Sqrt))
            s.op("vector", lambda e: e.reciprocal(R2[:], R2[:]))
            s.op("tensor", lambda e: e.matmul(ps_b[:], ONER[:], R0[:], start=True, stop=True))
            s.op("tensor", lambda e: e.matmul(ps_c[:], ONER[:], R2[:], start=True, stop=True))
            for ci in range(8):
                s.op("vector", lambda e, ci=ci: e.tensor_tensor(HF[:], YA[:, ci * 512:(ci + 1) * 512], ps_b[:], op=ALU.subtract))
                s.op("vector", lambda e: e.tensor_tensor(HF[:], HF[:], ps_c[:], op=ALU.mult))
                s.op("vector", lambda e, ci=ci: e.tensor_scalar(HF[:], HF[:], BIA[:, 24 + ci:25 + ci], BIA[:, 32 + ci:33 + ci], op0=ALU.mult, op1=ALU.add))
                s.op("vector", lambda e, ci=ci: e.tensor_tensor(HF[:], HF[:], XA[:, ci * 512:(ci + 1) * 512], op=ALU.add))
                s.dma("sync", x1T_d[ci * 128:(ci + 1) * 128, tg * 512:(tg + 1) * 512], HF[:])

        s.build(blk)
    return nc


def run_l0_attn(d):
    nc = build_l0_attn()
    pe = pe_table()
    peT = t2d(pe)
    xT_all = t2d(d["L0_attn_x"])
    in_maps = []
    for c in range(8):
        b, half = c // 2, c % 2
        own = slice(half * TOWN, (half + 1) * TOWN)
        in_maps.append({
            "xT": xT_all[b],
            "xTo": np.ascontiguousarray(xT_all[b][:, own]),
            "peT": peT,
            "peTo": np.ascontiguousarray(peT[:, own]),
            "wq": d["L0_attn_Wq"], "wk": d["L0_attn_Wk"], "wv": d["L0_attn_Wv"], "wo": d["L0_attn_Wo"],
            "bq": bcol(d["L0_attn_bq"]), "bk": bcol(d["L0_attn_bk"]), "bo": bcol(d["L0_attn_bo"]),
            "gam": bcol(d["L0_attn_gamma"]), "bet": bcol(d["L0_attn_beta"]),
            "bvr": d["L0_attn_bv"].reshape(1, D),
        })
    res = run_bass_kernel_spmd(nc, in_maps, core_ids=list(range(8)))
    x1 = np.empty((B, S, D), np.float32)
    for c in range(8):
        b, half = c // 2, c % 2
        x1[b, half * TOWN:(half + 1) * TOWN] = res.results[c]["x1T"].T
    return x1




def build_l0_ffn():
    nc = bass.Bass("TRN2", target_bir_lowering=False)
    IN2 = 2816
    x1o_d = nc.dram_tensor("x1o", [TOWN, D], F32, kind="ExternalInput").ap()
    wg_d = nc.dram_tensor("wg", [IN2, D], F32, kind="ExternalInput").ap()
    wu_d = nc.dram_tensor("wu", [IN2, D], F32, kind="ExternalInput").ap()
    wd_d = nc.dram_tensor("wd", [D, IN2], F32, kind="ExternalInput").ap()
    idn_d = nc.dram_tensor("idn", [128, 128], BF16, kind="ExternalInput").ap()
    x2o_d = nc.dram_tensor("x2o", [TOWN, D], F32, kind="ExternalOutput").ap()

    ex = contextlib.ExitStack()
    with ex:
        sem = ex.enter_context(nc.semaphore("ser"))
        WGT = ex.enter_context(nc.sbuf_tensor("WGT", [128, 8 * IN2], BF16))
        WUT = ex.enter_context(nc.sbuf_tensor("WUT", [128, 8 * IN2], BF16))
        WDT = ex.enter_context(nc.sbuf_tensor("WDT", [128, 22 * D], BF16))
        XT_ = ex.enter_context(nc.sbuf_tensor("XT_", [128, 1408], F32))
        ZT_ = ex.enter_context(nc.sbuf_tensor("ZT_", [128, 1024], F32))
        T_A = ex.enter_context(nc.sbuf_tensor("T_A", [128, 1408], F32))
        T_CA = ex.enter_context(nc.sbuf_tensor("T_CA", [128, 1408], F32))
        T_CB = ex.enter_context(nc.sbuf_tensor("T_CB", [128, 1408], F32))
        MB = ex.enter_context(nc.sbuf_tensor("MB", [128, 88], F32))
        SB = ex.enter_context(nc.sbuf_tensor("SB", [128, 88], F32))
        RB = ex.enter_context(nc.sbuf_tensor("RB", [128, 88], F32))
        QB = ex.enter_context(nc.sbuf_tensor("QB", [128, 1408], BF16))
        ZTT = ex.enter_context(nc.sbuf_tensor("ZTT", [128, 1024], BF16))
        HT = ex.enter_context(nc.sbuf_tensor("HT", [128, IN2], BF16))
        GA = ex.enter_context(nc.sbuf_tensor("GA", [128, 512], F32))
        UA = ex.enter_context(nc.sbuf_tensor("UA", [128, 512], F32))
        HH = ex.enter_context(nc.sbuf_tensor("HH", [128, 512], F32))
        OUTT = ex.enter_context(nc.sbuf_tensor("OUTT", [128, 1024], F32))
        IDN = ex.enter_context(nc.sbuf_tensor("IDN", [128, 128], BF16))
        SQA = ex.enter_context(nc.sbuf_tensor("SQA", [128, 1024], F32))
        MU = ex.enter_context(nc.sbuf_tensor("MU", [128, 1], F32))
        MSQ = ex.enter_context(nc.sbuf_tensor("MSQ", [128, 1], F32))
        RSTD = ex.enter_context(nc.sbuf_tensor("RSTD", [128, 1], F32))
        ps_g = ex.enter_context(nc.psum_tensor("ps_g", [128, 512], F32))
        ps_u = ex.enter_context(nc.psum_tensor("ps_u", [128, 512], F32))
        ps_t = ex.enter_context(nc.psum_tensor("ps_t", [128, 128], BF16))
        blk = ex.enter_context(nc.Block())

        s = Ser(nc, sem)
        s.dma("sync", IDN[:], idn_d)

        def quant(src, out, F):
            G = F // 16
            s.op("vector", lambda e: e.tensor_reduce(
                MB[:, 0:G], src.rearrange("p (g s) -> p g s", s=16), axis=AX.X,
                op=ALU.max, apply_absolute_value=True))
            s.op("vector", lambda e: e.tensor_scalar(SB[:, 0:G], MB[:, 0:G], 1.0 / 6.0, 1e-12, op0=ALU.mult, op1=ALU.max))
            s.op("vector", lambda e: e.reciprocal(RB[:, 0:G], SB[:, 0:G]))
            s.op("vector", lambda e: e.tensor_tensor(
                src.rearrange("p (g s) -> p g s", s=16), src.rearrange("p (g s) -> p g s", s=16),
                RB[:, 0:G].unsqueeze(2).broadcast_to([128, G, 16]), op=ALU.mult))
            s.op("scalar", lambda e: e.activation(T_A[:, 0:F], src, AF.Abs))
            s.op("vector", lambda e: e.tensor_scalar(T_CB[:, 0:F], T_A[:, 0:F], 5.0, 2.0, op0=ALU.is_gt, op1=ALU.mult))
            s.op("vector", lambda e: e.scalar_tensor_tensor(T_CB[:, 0:F], T_A[:, 0:F], 2.5, T_CB[:, 0:F], op0=ALU.is_gt, op1=ALU.add))
            s.op("vector", lambda e: e.scalar_tensor_tensor(T_CB[:, 0:F], T_A[:, 0:F], 3.5, T_CB[:, 0:F], op0=ALU.is_gt, op1=ALU.add))
            s.op("vector", lambda e: e.tensor_scalar(T_CA[:, 0:F], T_A[:, 0:F], 0.25, None, op0=ALU.is_gt))
            s.op("vector", lambda e: e.scalar_tensor_tensor(T_CA[:, 0:F], T_A[:, 0:F], 0.75, T_CA[:, 0:F], op0=ALU.is_gt, op1=ALU.add))
            s.op("vector", lambda e: e.scalar_tensor_tensor(T_CA[:, 0:F], T_A[:, 0:F], 1.25, T_CA[:, 0:F], op0=ALU.is_gt, op1=ALU.add))
            s.op("vector", lambda e: e.scalar_tensor_tensor(T_CA[:, 0:F], T_A[:, 0:F], 1.75, T_CA[:, 0:F], op0=ALU.is_gt, op1=ALU.add))
            s.op("vector", lambda e: e.scalar_tensor_tensor(T_CA[:, 0:F], T_CA[:, 0:F], 0.5, T_CB[:, 0:F], op0=ALU.mult, op1=ALU.add))
            s.op("scalar", lambda e: e.activation(T_CB[:, 0:F], src, AF.Sign))
            s.op("vector", lambda e: e.tensor_tensor(
                T_CB[:, 0:F].rearrange("p (g s) -> p g s", s=16), T_CB[:, 0:F].rearrange("p (g s) -> p g s", s=16),
                SB[:, 0:G].unsqueeze(2).broadcast_to([128, G, 16]), op=ALU.mult))
            s.op("gpsimd", lambda e: e.tensor_tensor(out, T_CA[:, 0:F], T_CB[:, 0:F], op=ALU.mult))

        def transpose_to(src_bf16, dst):
            s.op("tensor", lambda e: e.transpose(ps_t[:], src_bf16, IDN[:]))
            s.op("vector", lambda e: e.tensor_copy(dst, ps_t[:]))

        for r in range(22):
            s.dma("sync", XT_[:, 0:1024], wg_d[r * 128:(r + 1) * 128, :])
            quant(XT_[:, 0:1024], QB[:, 0:1024], 1024)
            for ci in range(8):
                transpose_to(QB[:, ci * 128:(ci + 1) * 128], WGT[:, ci * IN2 + r * 128: ci * IN2 + r * 128 + 128])
        for r in range(22):
            s.dma("sync", XT_[:, 0:1024], wu_d[r * 128:(r + 1) * 128, :])
            quant(XT_[:, 0:1024], QB[:, 0:1024], 1024)
            for ci in range(8):
                transpose_to(QB[:, ci * 128:(ci + 1) * 128], WUT[:, ci * IN2 + r * 128: ci * IN2 + r * 128 + 128])
        for r in range(8):
            for hh in range(2):
                s.dma("sync", XT_[:, 0:1408], wd_d[r * 128:(r + 1) * 128, hh * 1408:(hh + 1) * 1408])
                quant(XT_[:, 0:1408], QB[:, 0:1408], 1408)
                for cj in range(11):
                    ri = hh * 11 + cj
                    transpose_to(QB[:, cj * 128:(cj + 1) * 128], WDT[:, ri * D + r * 128: ri * D + r * 128 + 128])

        OCW = [(0, 512), (512, 512), (1024, 512), (1536, 512), (2048, 512), (2560, 256)]
        for t in range(16):
            s.dma("sync", XT_[:, 0:1024], x1o_d[t * 128:(t + 1) * 128, :])
            s.op("vector", lambda e: e.tensor_reduce(MU[:], XT_[:, 0:1024], axis=AX.X, op=ALU.add))
            s.op("vector", lambda e: e.tensor_scalar(MU[:], MU[:], 1.0 / D, None, op0=ALU.mult))
            s.op("scalar", lambda e: e.activation(SQA[:], XT_[:, 0:1024], AF.Square))
            s.op("vector", lambda e: e.tensor_reduce(MSQ[:], SQA[:], axis=AX.X, op=ALU.add))
            s.op("vector", lambda e: e.tensor_scalar(MSQ[:], MSQ[:], 1.0 / D, None, op0=ALU.mult))
            s.op("vector", lambda e: e.tensor_tensor(RSTD[:], MU[:], MU[:], op=ALU.mult))
            s.op("vector", lambda e: e.tensor_tensor(RSTD[:], MSQ[:], RSTD[:], op=ALU.subtract))
            s.op("vector", lambda e: e.tensor_scalar(RSTD[:], RSTD[:], 1e-5, None, op0=ALU.add))
            s.op("scalar", lambda e: e.activation(RSTD[:], RSTD[:], AF.Sqrt))
            s.op("vector", lambda e: e.reciprocal(RSTD[:], RSTD[:]))
            s.op("vector", lambda e: e.tensor_scalar(ZT_[:], XT_[:, 0:1024], MU[:], RSTD[:], op0=ALU.subtract, op1=ALU.mult))
            quant(ZT_[:], QB[:, 0:1024], 1024)
            for ci in range(8):
                transpose_to(QB[:, ci * 128:(ci + 1) * 128], ZTT[:, ci * 128:(ci + 1) * 128])
            for (ooff, ow) in OCW:
                for ci in range(8):
                    s.op("tensor", lambda e, ci=ci, ooff=ooff, ow=ow: e.matmul(
                        ps_g[:, 0:ow], ZTT[:, ci * 128:(ci + 1) * 128],
                        WGT[:, ci * IN2 + ooff: ci * IN2 + ooff + ow], start=(ci == 0), stop=(ci == 7), skip_group_check=True))
                for ci in range(8):
                    s.op("tensor", lambda e, ci=ci, ooff=ooff, ow=ow: e.matmul(
                        ps_u[:, 0:ow], ZTT[:, ci * 128:(ci + 1) * 128],
                        WUT[:, ci * IN2 + ooff: ci * IN2 + ooff + ow], start=(ci == 0), stop=(ci == 7), skip_group_check=True))
                s.op("scalar", lambda e, ow=ow: e.activation(GA[:, 0:ow], ps_g[:, 0:ow], AF.Silu))
                s.op("scalar", lambda e, ow=ow: e.activation(UA[:, 0:ow], ps_u[:, 0:ow], AF.Copy))
                s.op("vector", lambda e, ow=ow: e.tensor_tensor(HH[:, 0:ow], GA[:, 0:ow], UA[:, 0:ow], op=ALU.mult))
                quant(HH[:, 0:ow], QB[:, 0:ow], ow)
                for k in range(ow // 128):
                    transpose_to(QB[:, k * 128:(k + 1) * 128], HT[:, ooff + k * 128: ooff + (k + 1) * 128])
            for oc2 in range(2):
                for ri in range(22):
                    s.op("tensor", lambda e, ri=ri, oc2=oc2: e.matmul(
                        ps_g[:], HT[:, ri * 128:(ri + 1) * 128],
                        WDT[:, ri * D + oc2 * 512: ri * D + oc2 * 512 + 512], start=(ri == 0), stop=(ri == 21), skip_group_check=True))
                s.op("vector", lambda e, oc2=oc2: e.tensor_tensor(
                    OUTT[:, oc2 * 512:(oc2 + 1) * 512], ps_g[:], XT_[:, oc2 * 512: oc2 * 512 + 512], op=ALU.add))
            s.dma("sync", x2o_d[t * 128:(t + 1) * 128, :], OUTT[:])

        s.build(blk)
    return nc


def run_l0_ffn(d, x1):
    nc = build_l0_ffn()
    idn = np.eye(128).astype(BF16NP)
    in_maps = []
    for c in range(8):
        b, half = c // 2, c % 2
        in_maps.append({
            "x1o": np.ascontiguousarray(x1[b, half * TOWN:(half + 1) * TOWN]),
            "wg": d["L0_ffn_w_gate"], "wu": d["L0_ffn_w_up"], "wd": d["L0_ffn_w_down"],
            "idn": idn,
        })
    res = run_bass_kernel_spmd(nc, in_maps, core_ids=list(range(8)))
    x2 = np.empty((B, S, D), np.float32)
    for c in range(8):
        b, half = c // 2, c % 2
        x2[b, half * TOWN:(half + 1) * TOWN] = res.results[c]["x2o"]
    return x2




def build_l1_attn():
    nc = bass.Bass("TRN2", target_bir_lowering=False)
    xT = nc.dram_tensor("xT", [D, S], F32, kind="ExternalInput").ap()
    xTo = nc.dram_tensor("xTo", [D, TOWN], F32, kind="ExternalInput").ap()
    wq_d = nc.dram_tensor("wq", [D, D], F32, kind="ExternalInput").ap()
    wk_d = nc.dram_tensor("wk", [D, D], F32, kind="ExternalInput").ap()
    wv_d = nc.dram_tensor("wv", [D, D], F32, kind="ExternalInput").ap()
    wo_d = nc.dram_tensor("wo", [D, D], F32, kind="ExternalInput").ap()
    bq_d = nc.dram_tensor("bq", [128, 8], F32, kind="ExternalInput").ap()
    bk_d = nc.dram_tensor("bk", [128, 8], F32, kind="ExternalInput").ap()
    bo_d = nc.dram_tensor("bo", [128, 8], F32, kind="ExternalInput").ap()
    bvr_d = nc.dram_tensor("bvr", [1, D], F32, kind="ExternalInput").ap()
    qidx_d = nc.dram_tensor("qidx", [1, TOWN], F32, kind="ExternalInput").ap()
    kidx_d = nc.dram_tensor("kidx", [128, 32], F32, kind="ExternalInput").ap()
    x3T_d = nc.dram_tensor("x3T", [D, TOWN], F32, kind="ExternalOutput").ap()
    z0T_d = nc.dram_tensor("z0T", [D, TOWN], F32, kind="ExternalOutput").ap()
    ctx_d = nc.dram_tensor("ctxh", [D, TOWN], BF16).ap()
    qT_d = nc.dram_tensor("qTh", [D, TOWN], BF16).ap()
    st_d = nc.dram_tensor("sth", [4, S], F32).ap()

    ex = contextlib.ExitStack()
    with ex:
        sem = ex.enter_context(nc.semaphore("ser"))
        A_K = ex.enter_context(nc.sbuf_tensor("A_K", [128, 8 * S], BF16))
        A_V = ex.enter_context(nc.sbuf_tensor("A_V", [128, 32 * 1040], BF16))
        A_W = ex.enter_context(nc.sbuf_tensor("A_W", [128, 8192], BF16))
        A_P = ex.enter_context(nc.sbuf_tensor("A_P", [128, 4096], BF16))
        WS = ex.enter_context(nc.sbuf_tensor("WS", [128, 16384], BF16))
        QS = ex.enter_context(nc.sbuf_tensor("QS", [128, 512], BF16))
        QIF = ex.enter_context(nc.sbuf_tensor("QIF", [128, 512], F32))
        KIX = ex.enter_context(nc.sbuf_tensor("KIX", [128, 32], F32))
        R0 = ex.enter_context(nc.sbuf_tensor("R0", [1, 512], F32))
        R1 = ex.enter_context(nc.sbuf_tensor("R1", [1, 512], F32))
        R2 = ex.enter_context(nc.sbuf_tensor("R2", [1, 512], F32))
        RL = ex.enter_context(nc.sbuf_tensor("RL", [1, 512], F32))
        RLB = ex.enter_context(nc.sbuf_tensor("RLB", [64, 512], F32))
        CTXS = ex.enter_context(nc.sbuf_tensor("CTXS", [64, 512], BF16))
        ONE1 = ex.enter_context(nc.sbuf_tensor("ONE1", [128, 1], F32))
        ONER = ex.enter_context(nc.sbuf_tensor("ONER", [1, 128], F32))
        BIA = ex.enter_context(nc.sbuf_tensor("BIA", [128, 24], F32))
        BVRB = ex.enter_context(nc.sbuf_tensor("BVRB", [128, D], F32))
        ps_s = ex.enter_context(nc.psum_tensor("ps_s", [128, 2048], F32))
        ps_o = ex.enter_context(nc.psum_tensor("ps_o", [65, 512], F32))
        ps_b = ex.enter_context(nc.psum_tensor("ps_b", [128, 512], F32))
        ps_c = ex.enter_context(nc.psum_tensor("ps_c", [128, 512], F32))
        blk = ex.enter_context(nc.Block())

        HBt = WS[:, 0:4096]
        WLv = WS[:, 4096:6144].bitcast(F32)
        XSv = WS[:, 6144:7168].bitcast(F32)
        SQv = WS[:, 7168:8192].bitcast(F32)
        HFv = WS[:, 8192:9216].bitcast(F32)
        VSv = WS[:, 9216:10240].bitcast(F32)
        OA = A_V[:, 0:4096]
        YA = A_V[:, 8192:16384].bitcast(F32)
        XA = A_V[:, 16384:24576].bitcast(F32)

        s = Ser(nc, sem)
        s.op("vector", lambda e: e.memset(ONE1[:], 1.0))
        s.op("vector", lambda e: e.memset(ONER[:], 1.0))
        s.dma("sync", BIA[:, 0:8], bq_d)
        s.dma("sync", BIA[:, 8:16], bk_d)
        s.dma("sync", BIA[:, 16:24], bo_d)
        s.dma("sync", BVRB[:], bvr_d.partition_broadcast(128))
        s.dma("sync", KIX[:], kidx_d)
        for tt in range(32):
            s.op("gpsimd", lambda e, tt=tt: e.memset(
                A_V[:, tt * 1040:(tt + 1) * 1040].rearrange("p (h d) -> p h d", d=65)[:, :, 64:65], 1.0))

        def stats_sweep(src, ntg, mu_row, rstd_row):
            for tg in range(ntg):
                for ci in range(8):
                    s.dma("sync", XSv, src[ci * 128:(ci + 1) * 128, tg * 512:(tg + 1) * 512])
                    s.op("scalar", lambda e: e.activation(SQv, XSv, AF.Square))
                    s.op("tensor", lambda e, ci=ci: e.matmul(
                        ps_s[0:1, 0:512], ONE1[:], XSv, start=(ci == 0), stop=(ci == 7), skip_group_check=True))
                    s.op("tensor", lambda e, ci=ci: e.matmul(
                        ps_s[0:1, 512:1024], ONE1[:], SQv, start=(ci == 0), stop=(ci == 7), skip_group_check=True))
                s.op("vector", lambda e: e.tensor_scalar(R0[:], ps_s[0:1, 0:512], 1.0 / D, None, op0=ALU.mult))
                s.op("vector", lambda e: e.tensor_scalar(R1[:], ps_s[0:1, 512:1024], 1.0 / D, None, op0=ALU.mult))
                s.op("vector", lambda e: e.tensor_tensor(R2[:], R0[:], R0[:], op=ALU.mult))
                s.op("vector", lambda e: e.tensor_tensor(R2[:], R1[:], R2[:], op=ALU.subtract))
                s.op("vector", lambda e: e.tensor_scalar(R2[:], R2[:], 1e-5, None, op0=ALU.add))
                s.op("scalar", lambda e: e.activation(R2[:], R2[:], AF.Sqrt))
                s.op("vector", lambda e: e.reciprocal(R2[:], R2[:]))
                s.dma("sync", st_d[mu_row:mu_row + 1, tg * 512:(tg + 1) * 512], R0[:])
                s.dma("sync", st_d[rstd_row:rstd_row + 1, tg * 512:(tg + 1) * 512], R2[:])

        def load_w(w_d):
            for ci in range(8):
                s.dma("sync", WLv, w_d[ci * 128:(ci + 1) * 128, :])
                s.op("vector", lambda e, ci=ci: e.tensor_copy(A_W[:, ci * 1024:(ci + 1) * 1024], WLv))

        def make_h(src, tg, mu_row, rstd_row):
            s.dma("sync", R0[:], st_d[mu_row:mu_row + 1, tg * 512:(tg + 1) * 512])
            s.dma("sync", R2[:], st_d[rstd_row:rstd_row + 1, tg * 512:(tg + 1) * 512])
            s.op("tensor", lambda e: e.matmul(ps_b[:], ONER[:], R0[:], start=True, stop=True))
            s.op("tensor", lambda e: e.matmul(ps_c[:], ONER[:], R2[:], start=True, stop=True))
            for ci in range(8):
                s.dma("sync", XSv, src[ci * 128:(ci + 1) * 128, tg * 512:(tg + 1) * 512])
                s.op("vector", lambda e: e.tensor_tensor(HFv, XSv, ps_b[:], op=ALU.subtract))
                s.op("vector", lambda e, ci=ci: e.tensor_tensor(HBt[:, ci * 512:(ci + 1) * 512], HFv, ps_c[:], op=ALU.mult))

        stats_sweep(xT, 8, 0, 1)
        stats_sweep(xTo, 4, 2, 3)

        load_w(wk_d)
        for tg in range(8):
            make_h(xT, tg, 0, 1)
            for oc in range(8):
                for ci in range(8):
                    s.op("tensor", lambda e, oc=oc, ci=ci: e.matmul(
                        ps_s[:, 0:512], A_W[:, ci * 1024 + oc * 128: ci * 1024 + oc * 128 + 128],
                        HBt[:, ci * 512:(ci + 1) * 512], start=(ci == 0), stop=(ci == 7)))
                s.op("vector", lambda e, oc=oc, tg=tg: e.tensor_scalar(
                    A_K[:, oc * S + tg * 512: oc * S + tg * 512 + 512], ps_s[:, 0:512],
                    BIA[:, 8 + oc:9 + oc], None, op0=ALU.add))

        load_w(wv_d)
        for tg in range(8):
            make_h(xT, tg, 0, 1)
            for tt4 in range(4):
                tt = tg * 4 + tt4
                for nh in range(2):
                    for ci in range(8):
                        s.op("tensor", lambda e, tt4=tt4, nh=nh, ci=ci: e.matmul(
                            ps_b[:], HBt[:, ci * 512 + tt4 * 128: ci * 512 + tt4 * 128 + 128],
                            A_W[:, ci * 1024 + nh * 512: ci * 1024 + nh * 512 + 512],
                            start=(ci == 0), stop=(ci == 7)))
                    s.op("scalar", lambda e: e.activation(VSv, ps_b[:], AF.Copy))
                    s.op("vector", lambda e, tt=tt, nh=nh: e.tensor_tensor(
                        A_V[:, tt * 1040 + nh * 520: tt * 1040 + nh * 520 + 520].rearrange("p (h d) -> p h d", d=65)[:, :, 0:64],
                        VSv.rearrange("p (h d) -> p h d", d=64),
                        BVRB[:, nh * 512:(nh + 1) * 512].rearrange("p (h d) -> p h d", d=64), op=ALU.add))

        load_w(wq_d)
        for tg in range(4):
            make_h(xTo, tg, 2, 3)
            for oc in range(8):
                for ci in range(8):
                    s.op("tensor", lambda e, oc=oc, ci=ci: e.matmul(
                        ps_s[:, 0:512], A_W[:, ci * 1024 + oc * 128: ci * 1024 + oc * 128 + 128],
                        HBt[:, ci * 512:(ci + 1) * 512], start=(ci == 0), stop=(ci == 7)))
                s.op("vector", lambda e, oc=oc: e.tensor_scalar(QS[:], ps_s[:, 0:512], BIA[:, oc:oc + 1], None, op0=ALU.add))
                s.dma("sync", qT_d[oc * 128:(oc + 1) * 128, tg * 512:(tg + 1) * 512], QS[:])

        MSKA = WS[:]
        for qg in range(4):
            for ci in range(8):
                s.dma("sync", A_W[:, ci * 512:(ci + 1) * 512], qT_d[ci * 128:(ci + 1) * 128, qg * 512:(qg + 1) * 512])
            s.dma("sync", QIF[:], qidx_d[0:1, qg * 512:(qg + 1) * 512].partition_broadcast(128))
            for kc in range(32):
                s.op("vector", lambda e, kc=kc: e.tensor_scalar(
                    MSKA[:, kc * 512:(kc + 1) * 512], QIF[:], KIX[:, kc:kc + 1], None, op0=ALU.is_ge))
            for h in range(16):
                hr = (h % 2) * 64
                hc = (h // 2)
                for kb in range(8):
                    for j in range(4):
                        kc = kb * 4 + j
                        s.op("tensor", lambda e, hr=hr, hc=hc, kc=kc, j=j: e.matmul(
                            ps_s[:, j * 512:(j + 1) * 512],
                            A_K[hr:hr + 64, hc * S + kc * 128: hc * S + kc * 128 + 128],
                            A_W[hr:hr + 64, hc * 512: hc * 512 + 512],
                            start=True, stop=True, skip_group_check=True))
                    s.op("scalar", lambda e: e.activation(A_P[:, 0:2048], ps_s[:], AF.Exp, scale=0.03125))
                    eng = "vector" if kb % 2 == 0 else "gpsimd"
                    s.op(eng, lambda e, kb=kb: e.tensor_tensor(
                        A_P[:, 0:2048], A_P[:, 0:2048], MSKA[:, kb * 2048:(kb + 1) * 2048], op=ALU.mult))
                    for j in range(4):
                        kc = kb * 4 + j
                        s.op("tensor", lambda e, kc=kc, h=h, j=j, kb=kb: e.matmul(
                            ps_o[:], A_V[:, kc * 1040 + 65 * h: kc * 1040 + 65 * h + 65],
                            A_P[:, j * 512:(j + 1) * 512],
                            start=(kb == 0 and j == 0), stop=(kb == 7 and j == 3), skip_group_check=True))
                s.op("vector", lambda e: e.reciprocal(RL[:], ps_o[64:65, :]))
                s.op("tensor", lambda e: e.matmul(ps_b[0:64, :], ONER[:, 0:64], RL[:], start=True, stop=True))
                s.op("scalar", lambda e: e.activation(RLB[:], ps_b[0:64, :], AF.Copy))
                s.op("vector", lambda e: e.tensor_tensor(CTXS[:], ps_o[0:64, :], RLB[:], op=ALU.mult))
                s.dma("sync", ctx_d[h * 64:(h + 1) * 64, qg * 512:(qg + 1) * 512], CTXS[:])

        load_w(wo_d)
        for tg in range(4):
            for ci in range(8):
                s.dma("sync", A_P[:, ci * 512:(ci + 1) * 512], ctx_d[ci * 128:(ci + 1) * 128, tg * 512:(tg + 1) * 512])
            for oc in range(8):
                for ci in range(8):
                    s.op("tensor", lambda e, oc=oc, ci=ci: e.matmul(
                        ps_s[:, 0:512], A_W[:, ci * 1024 + oc * 128: ci * 1024 + oc * 128 + 128],
                        A_P[:, ci * 512:(ci + 1) * 512], start=(ci == 0), stop=(ci == 7)))
                s.op("vector", lambda e, oc=oc: e.tensor_scalar(OA[:, oc * 512:(oc + 1) * 512], ps_s[:, 0:512],
                                                                BIA[:, 16 + oc:17 + oc], None, op0=ALU.add))
            for ci in range(8):
                s.dma("sync", XA[:, ci * 512:(ci + 1) * 512], xTo[ci * 128:(ci + 1) * 128, tg * 512:(tg + 1) * 512])
                s.op("vector", lambda e, ci=ci: e.tensor_tensor(
                    YA[:, ci * 512:(ci + 1) * 512], XA[:, ci * 512:(ci + 1) * 512], OA[:, ci * 512:(ci + 1) * 512], op=ALU.add))
                s.dma("sync", x3T_d[ci * 128:(ci + 1) * 128, tg * 512:(tg + 1) * 512], YA[:, ci * 512:(ci + 1) * 512])
                s.op("scalar", lambda e, ci=ci: e.activation(SQv, YA[:, ci * 512:(ci + 1) * 512], AF.Square))
                s.op("tensor", lambda e, ci=ci: e.matmul(
                    ps_s[0:1, 0:512], ONE1[:], YA[:, ci * 512:(ci + 1) * 512], start=(ci == 0), stop=(ci == 7), skip_group_check=True))
                s.op("tensor", lambda e, ci=ci: e.matmul(
                    ps_s[0:1, 512:1024], ONE1[:], SQv, start=(ci == 0), stop=(ci == 7), skip_group_check=True))
            s.op("vector", lambda e: e.tensor_scalar(R0[:], ps_s[0:1, 0:512], 1.0 / D, None, op0=ALU.mult))
            s.op("vector", lambda e: e.tensor_scalar(R1[:], ps_s[0:1, 512:1024], 1.0 / D, None, op0=ALU.mult))
            s.op("vector", lambda e: e.tensor_tensor(R2[:], R0[:], R0[:], op=ALU.mult))
            s.op("vector", lambda e: e.tensor_tensor(R2[:], R1[:], R2[:], op=ALU.subtract))
            s.op("vector", lambda e: e.tensor_scalar(R2[:], R2[:], 1e-5, None, op0=ALU.add))
            s.op("scalar", lambda e: e.activation(R2[:], R2[:], AF.Sqrt))
            s.op("vector", lambda e: e.reciprocal(R2[:], R2[:]))
            s.op("tensor", lambda e: e.matmul(ps_b[:], ONER[:], R0[:], start=True, stop=True))
            s.op("tensor", lambda e: e.matmul(ps_c[:], ONER[:], R2[:], start=True, stop=True))
            for ci in range(8):
                s.op("vector", lambda e, ci=ci: e.tensor_tensor(HFv, YA[:, ci * 512:(ci + 1) * 512], ps_b[:], op=ALU.subtract))
                s.op("vector", lambda e: e.tensor_tensor(HFv, HFv, ps_c[:], op=ALU.mult))
                s.dma("sync", z0T_d[ci * 128:(ci + 1) * 128, tg * 512:(tg + 1) * 512], HFv)

        s.build(blk)
    return nc


def run_l1_attn(d, x2):
    nc = build_l1_attn()
    x2T_all = t2d(x2)
    kidx = (np.arange(32)[None, :] * 128 + np.arange(128)[:, None]).astype(np.float32)
    in_maps = []
    for c in range(8):
        b, half = c // 2, c % 2
        own = slice(half * TOWN, (half + 1) * TOWN)
        in_maps.append({
            "xT": x2T_all[b],
            "xTo": np.ascontiguousarray(x2T_all[b][:, own]),
            "wq": t2d(d["L1_attn_Wq_w"]), "wk": t2d(d["L1_attn_Wk_w"]),
            "wv": t2d(d["L1_attn_Wv_w"]), "wo": t2d(d["L1_attn_Wo_w"]),
            "bq": bcol(d["L1_attn_Wq_b"]), "bk": bcol(d["L1_attn_Wk_b"]), "bo": bcol(d["L1_attn_Wo_b"]),
            "bvr": d["L1_attn_Wv_b"].reshape(1, D),
            "qidx": np.arange(half * TOWN, (half + 1) * TOWN, dtype=np.float32).reshape(1, TOWN),
            "kidx": kidx,
        })
    res = run_bass_kernel_spmd(nc, in_maps, core_ids=list(range(8)))
    x3 = np.empty((B, S, D), np.float32)
    z0 = np.empty((B, S, D), np.float32)
    for c in range(8):
        b, half = c // 2, c % 2
        x3[b, half * TOWN:(half + 1) * TOWN] = res.results[c]["x3T"].T
        z0[b, half * TOWN:(half + 1) * TOWN] = res.results[c]["z0T"].T
    return x3, z0



HALO = 384
NBUF = HALO + TOWN
NQ = TOWN + 128
DI = 2730
ACH = [128] * 21 + [42]
GW = [(0, 512), (512, 512), (1024, 512), (1536, 512), (2048, NBUF - 2048)]
GQ = [(0, 512), (512, 512), (1024, 512), (1536, 512), (2048, NQ - 2048)]


def build_l1f(last):
    nc = bass.Bass("TRN2", target_bir_lowering=False)
    zH_d = nc.dram_tensor("zH", [D, 1 + NBUF], F32, kind="ExternalInput").ap()
    wqkv_d = nc.dram_tensor("wqkv", [D, 3 * D], F32, kind="ExternalInput").ap()
    wo2_d = nc.dram_tensor("wo2", [D, D], F32, kind="ExternalInput").ap()
    w1_d = nc.dram_tensor("w1", [D, 2 * DI], F32, kind="ExternalInput").ap()
    w2_d = nc.dram_tensor("w2", [DI, D], F32, kind="ExternalInput").ap()
    b1a_d = nc.dram_tensor("b1a", [128, 22], F32, kind="ExternalInput").ap()
    b1g_d = nc.dram_tensor("b1g", [128, 22], F32, kind="ExternalInput").ap()
    b2_d = nc.dram_tensor("b2c", [128, 8], F32, kind="ExternalInput").ap()
    lng_d = nc.dram_tensor("lng", [128, 8], F32, kind="ExternalInput").ap()
    lnb_d = nc.dram_tensor("lnb", [128, 8], F32, kind="ExternalInput").ap()
    qsc_d = nc.dram_tensor("qsc", [128, 1], F32, kind="ExternalInput").ap()
    ksc_d = nc.dram_tensor("ksc", [128, 1], F32, kind="ExternalInput").ap()
    csq_d = nc.dram_tensor("csq", [128, NQ], F32, kind="ExternalInput").ap()
    snq_d = nc.dram_tensor("snq", [128, NQ], F32, kind="ExternalInput").ap()
    csk_d = nc.dram_tensor("csk", [128, NBUF], F32, kind="ExternalInput").ap()
    snk_d = nc.dram_tensor("snk", [128, NBUF], F32, kind="ExternalInput").ap()
    rotm_d = nc.dram_tensor("rotm", [128, 128], BF16, kind="ExternalInput").ap()
    sel2_d = nc.dram_tensor("sel2", [128, 2], F32, kind="ExternalInput").ap()
    bb2_d = nc.dram_tensor("bb2", [2, 128], F32, kind="ExternalInput").ap()
    qidx_d = nc.dram_tensor("qidx", [1, NQ], F32, kind="ExternalInput").ap()
    kix_d = nc.dram_tensor("kix", [128, 19], F32, kind="ExternalInput").ap()
    kix2_d = nc.dram_tensor("kix2", [128, 19], F32, kind="ExternalInput").ap()
    if last:
        xres_d = nc.dram_tensor("xres", [D, TOWN], F32, kind="ExternalInput").ap()
    zo_d = nc.dram_tensor("zo", [D, TOWN], F32, kind="ExternalOutput").ap()
    ctx_d = nc.dram_tensor("ctxh", [D, NQ], BF16).ap()
    z2_d = nc.dram_tensor("z2h", [D, NQ], F32).ap()
    w1b_d = nc.dram_tensor("w1bh", [D, 2 * DI], BF16).ap()
    w2b_d = nc.dram_tensor("w2bh", [DI, D], BF16).ap()

    ex = contextlib.ExitStack()
    with ex:
        sem = ex.enter_context(nc.semaphore("ser"))
        SH = ex.enter_context(nc.sbuf_tensor("SH", [128, 36864], BF16))
        ZB = ex.enter_context(nc.sbuf_tensor("ZB", [128, 8 * NBUF], BF16))
        TB1 = ex.enter_context(nc.sbuf_tensor("TB1", [128, NBUF], F32))
        TB2 = ex.enter_context(nc.sbuf_tensor("TB2", [128, NBUF], F32))
        WOB = ex.enter_context(nc.sbuf_tensor("WOB", [128, 8192], BF16))
        MSKA = ex.enter_context(nc.sbuf_tensor("MSKA", [128, 17 * 384], BF16))
        QIF = ex.enter_context(nc.sbuf_tensor("QIF", [128, 128], F32))
        KIX = ex.enter_context(nc.sbuf_tensor("KIX", [128, 19], F32))
        KIX2 = ex.enter_context(nc.sbuf_tensor("KIX2", [128, 19], F32))
        PP = ex.enter_context(nc.sbuf_tensor("PP", [128, 384], BF16))
        N2 = ex.enter_context(nc.sbuf_tensor("N2", [2, NBUF], F32))
        RL = ex.enter_context(nc.sbuf_tensor("RL", [1, 128], F32))
        RLB = ex.enter_context(nc.sbuf_tensor("RLB", [64, 128], F32))
        CTXS = ex.enter_context(nc.sbuf_tensor("CTXS", [64, 128], BF16))
        ROTM = ex.enter_context(nc.sbuf_tensor("ROTM", [128, 128], BF16))
        SEL2 = ex.enter_context(nc.sbuf_tensor("SEL2", [128, 2], F32))
        BB2 = ex.enter_context(nc.sbuf_tensor("BB2", [2, 128], F32))
        QSC = ex.enter_context(nc.sbuf_tensor("QSC", [128, 2], F32))
        ONER = ex.enter_context(nc.sbuf_tensor("ONER", [1, 128], F32))
        ONE1 = ex.enter_context(nc.sbuf_tensor("ONE1", [128, 1], F32))
        BIA = ex.enter_context(nc.sbuf_tensor("BIA", [128, 68], F32))
        WSL = ex.enter_context(nc.sbuf_tensor("WSL", [128, 512], F32))
        W2S = ex.enter_context(nc.sbuf_tensor("W2S", [128, 128], BF16))
        R0 = ex.enter_context(nc.sbuf_tensor("R0", [1, 512], F32))
        R1 = ex.enter_context(nc.sbuf_tensor("R1", [1, 512], F32))
        R2 = ex.enter_context(nc.sbuf_tensor("R2", [1, 512], F32))
        psA = ex.enter_context(nc.psum_tensor("psA", [128, 512], F32))
        psB = ex.enter_context(nc.psum_tensor("psB", [128, 512], F32))
        psC = ex.enter_context(nc.psum_tensor("psC", [128, 512], F32))
        psD = ex.enter_context(nc.psum_tensor("psD", [65, 128], F32))
        psN = ex.enter_context(nc.psum_tensor("psN", [2, 512], F32))
        blk = ex.enter_context(nc.Block())

        QR = SH[:, 0:4864].bitcast(F32)
        QH = SH[:, 4864:9728].bitcast(F32)
        SQ2 = SH[:, 9728:14592].bitcast(F32)
        QHB = SH[:, 14592:17024]
        Q2 = SH[:, 17024:19200]
        K2 = SH[:, 19200:21632]
        VT = SH[:, 21632:24102]
        WQS = SH[:, 24102:27174]
        XF = SH[:, 0:8192].bitcast(F32)
        XN = SH[:, 8192:12288]
        H2T = SH[:, 12288:23552]
        WSLB = SH[:, 23552:27648]
        GEV = SH[:, 27648:28672].bitcast(F32)
        AEV = SH[:, 28672:29696].bitcast(F32)
        OUV = SH[:, 29696:30720].bitcast(F32)
        CXS = SH[:, 30720:34816]

        s = Ser(nc, sem)
        s.op("vector", lambda e: e.memset(ONER[:], 1.0))
        s.op("vector", lambda e: e.memset(ONE1[:], 1.0))
        s.dma("sync", ROTM[:], rotm_d)
        s.dma("sync", SEL2[:], sel2_d)
        s.dma("sync", BB2[:], bb2_d)
        s.dma("sync", QSC[:, 0:1], qsc_d)
        s.dma("sync", QSC[:, 1:2], ksc_d)
        s.dma("sync", KIX[:], kix_d)
        s.dma("sync", KIX2[:], kix2_d)
        s.dma("sync", BIA[:, 0:22], b1a_d)
        s.dma("sync", BIA[:, 22:44], b1g_d)
        s.dma("sync", BIA[:, 44:52], b2_d)
        s.dma("sync", BIA[:, 52:60], lng_d)
        s.dma("sync", BIA[:, 60:68], lnb_d)

        for ti in range(17):
            s.dma("sync", QIF[:], qidx_d[0:1, ti * 128:(ti + 1) * 128].partition_broadcast(128))
            for c3 in range(3):
                ck = ti + c3
                ms = MSKA[:, (ti * 3 + c3) * 128: (ti * 3 + c3) * 128 + 128]
                s.op("vector", lambda e, ms=ms, ck=ck: e.tensor_scalar(
                    ms, QIF[:], KIX[:, ck:ck + 1], None, op0=ALU.is_ge))
                s.op("vector", lambda e, ms=ms, ck=ck: e.scalar_tensor_tensor(
                    ms, QIF[:], KIX2[:, ck:ck + 1], ms, op0=ALU.is_le, op1=ALU.mult))

        W1G = [(0, 512)] * 0 + [(g * 512, min(512, 5460 - g * 512)) for g in range(11)]
        for ci in range(8):
            for (go, gw) in W1G:
                s.dma("sync", WSL[:, 0:gw], w1_d[ci * 128:(ci + 1) * 128, go:go + gw])
                s.op("vector", lambda e, gw=gw: e.tensor_copy(W2S[:, 0:0] if False else PP[:, 0:gw] if gw <= 384 else QHB[:, 0:gw], WSL[:, 0:gw]))
                s.dma("sync", w1b_d[ci * 128:(ci + 1) * 128, go:go + gw], PP[:, 0:gw] if gw <= 384 else QHB[:, 0:gw])
        for ri in range(22):
            rh = ACH[ri]
            for g2 in range(2):
                s.dma("sync", WSL[0:rh, 0:512], w2_d[ri * 128: ri * 128 + rh, g2 * 512:(g2 + 1) * 512])
                s.op("vector", lambda e, rh=rh: e.tensor_copy(QHB[0:rh, 0:512], WSL[0:rh, 0:512]))
                s.dma("sync", w2b_d[ri * 128: ri * 128 + rh, g2 * 512:(g2 + 1) * 512], QHB[0:rh, 0:512])

        for ci in range(8):
            off = 1 if ci < 4 else 0
            for g in range(5):
                w = 512 if g < 4 else NBUF - 2048
                s.dma("sync", WSL[:, 0:w], zH_d[ci * 128:(ci + 1) * 128, off + g * 512: off + g * 512 + w])
                s.op("vector", lambda e, ci=ci, g=g, w=w: e.tensor_copy(
                    ZB[:, ci * NBUF + g * 512: ci * NBUF + g * 512 + w], WSL[:, 0:w]))

        def rotary(kind):
            boff = 0
            groups = GQ if kind == "q" else GW
            scol = 0 if kind == "q" else 1
            out = Q2 if kind == "q" else K2
            n = NQ if kind == "q" else NBUF
            s.op("scalar", lambda e, n=n: e.activation(SQ2[:, 0:n], QR[:, 0:n], AF.Square))
            for (go, gw) in groups:
                s.op("tensor", lambda e, go=go, gw=gw: e.matmul(
                    psN[:, 0:gw], SEL2[:], SQ2[:, go:go + gw], start=True, stop=True))
                s.op("vector", lambda e, go=go, gw=gw: e.tensor_copy(N2[:, go:go + gw], psN[:, 0:gw]))
            s.op("scalar", lambda e, n=n: e.activation(N2[:, 0:n], N2[:, 0:n], AF.Sqrt))
            s.op("vector", lambda e, n=n: e.tensor_scalar(N2[:, 0:n], N2[:, 0:n], 1e-12, None, op0=ALU.max))
            s.op("vector", lambda e, n=n: e.reciprocal(N2[:, 0:n], N2[:, 0:n]))
            for (go, gw) in groups:
                s.op("tensor", lambda e, go=go, gw=gw: e.matmul(
                    psB[:, 0:gw], BB2[:], N2[:, go:go + gw], start=True, stop=True))
                s.op("vector", lambda e, go=go, gw=gw, scol=scol: e.scalar_tensor_tensor(
                    QH[:, go:go + gw], QR[:, go:go + gw], QSC[:, scol:scol + 1], psB[:, 0:gw],
                    op0=ALU.mult, op1=ALU.mult))
                s.op("vector", lambda e, go=go, gw=gw: e.tensor_copy(QHB[:, go:go + gw], QH[:, go:go + gw]))
                s.op("tensor", lambda e, go=go, gw=gw: e.matmul(
                    psC[:, 0:gw], ROTM[:], QHB[:, go:go + gw], start=True, stop=True))
                s.op("vector", lambda e, go=go, gw=gw, boff=boff: e.tensor_tensor(
                    QH[:, go:go + gw], QH[:, go:go + gw], TB1[:, boff + go: boff + go + gw], op=ALU.mult))
                s.op("vector", lambda e, go=go, gw=gw, boff=boff: e.tensor_tensor(
                    SQ2[:, go:go + gw], psC[:, 0:gw], TB2[:, boff + go: boff + go + gw], op=ALU.mult))
                s.op("vector", lambda e, go=go, gw=gw, out=out: e.tensor_tensor(
                    out[:, go:go + gw], QH[:, go:go + gw], SQ2[:, go:go + gw], op=ALU.add))

        for st in range(8):
            for part in range(3):
                for ci in range(8):
                    s.dma("sync", WSL[:, 0:128], wqkv_d[ci * 128:(ci + 1) * 128, part * D + st * 128: part * D + st * 128 + 128])
                    s.op("vector", lambda e, part=part, ci=ci: e.tensor_copy(
                        WQS[:, (part * 8 + ci) * 128:(part * 8 + ci) * 128 + 128], WSL[:, 0:128]))
            for tt in range(19):
                s.op("gpsimd", lambda e, tt=tt: e.memset(
                    VT[:, tt * 130:(tt + 1) * 130].rearrange("p (h d) -> p h d", d=65)[:, :, 64:65], 1.0))
                for ci in range(8):
                    s.op("tensor", lambda e, tt=tt, ci=ci: e.matmul(
                        psA[:, 0:128], ZB[:, ci * NBUF + tt * 128: ci * NBUF + tt * 128 + 128],
                        WQS[:, (2 * 8 + ci) * 128:(2 * 8 + ci) * 128 + 128],
                        start=(ci == 0), stop=(ci == 7)))
                s.op("vector", lambda e, tt=tt: e.tensor_copy(
                    VT[:, tt * 130:(tt + 1) * 130].rearrange("p (h d) -> p h d", d=65)[:, :, 0:64],
                    psA[:, 0:128].rearrange("p (h d) -> p h d", d=64)))
            s.dma("sync", TB1[:, 0:NBUF], csk_d)
            s.dma("sync", TB2[:, 0:NBUF], snk_d)
            for (go, gw) in GW:
                for ci in range(8):
                    s.op("tensor", lambda e, go=go, gw=gw, ci=ci: e.matmul(
                        psA[:, 0:gw], WQS[:, (1 * 8 + ci) * 128:(1 * 8 + ci) * 128 + 128],
                        ZB[:, ci * NBUF + go: ci * NBUF + go + gw],
                        start=(ci == 0), stop=(ci == 7)))
                s.op("scalar", lambda e, go=go, gw=gw: e.activation(QR[:, go:go + gw], psA[:, 0:gw], AF.Copy))
            rotary("k")
            s.dma("sync", TB1[:, 0:NQ], csq_d)
            s.dma("sync", TB2[:, 0:NQ], snq_d)
            for (go, gw) in GQ:
                for ci in range(8):
                    s.op("tensor", lambda e, go=go, gw=gw, ci=ci: e.matmul(
                        psA[:, 0:gw], WQS[:, (0 * 8 + ci) * 128:(0 * 8 + ci) * 128 + 128],
                        ZB[:, ci * NBUF + 256 + go: ci * NBUF + 256 + go + gw],
                        start=(ci == 0), stop=(ci == 7)))
                s.op("scalar", lambda e, go=go, gw=gw: e.activation(QR[:, go:go + gw], psA[:, 0:gw], AF.Copy))
            rotary("q")
            for ti in range(17):
                for h2 in range(2):
                    hr = h2 * 64
                    for c3 in range(3):
                        ck = ti + c3
                        s.op("tensor", lambda e, hr=hr, ck=ck, c3=c3, ti=ti: e.matmul(
                            psA[:, c3 * 128:(c3 + 1) * 128],
                            K2[hr:hr + 64, ck * 128: ck * 128 + 128],
                            Q2[hr:hr + 64, ti * 128: ti * 128 + 128],
                            start=True, stop=True, skip_group_check=True))
                    s.op("scalar", lambda e: e.activation(PP[:], psA[:, 0:384], AF.Exp, scale=8.0))
                    s.op("vector", lambda e, ti=ti: e.tensor_tensor(
                        PP[:], PP[:], MSKA[:, ti * 384:(ti + 1) * 384], op=ALU.mult))
                    for c3 in range(3):
                        ck = ti + c3
                        s.op("tensor", lambda e, ck=ck, h2=h2, c3=c3: e.matmul(
                            psD[:], VT[:, ck * 130 + h2 * 65: ck * 130 + h2 * 65 + 65],
                            PP[:, c3 * 128:(c3 + 1) * 128],
                            start=(c3 == 0), stop=(c3 == 2), skip_group_check=True))
                    s.op("vector", lambda e: e.tensor_scalar(RL[:], psD[64:65, :], 1e-20, None, op0=ALU.add))
                    s.op("vector", lambda e: e.reciprocal(RL[:], RL[:]))
                    s.op("tensor", lambda e: e.matmul(psB[0:64, 0:128], ONER[:, 0:64], RL[:], start=True, stop=True))
                    s.op("scalar", lambda e: e.activation(RLB[:], psB[0:64, 0:128], AF.Copy))
                    s.op("vector", lambda e: e.tensor_tensor(CTXS[:], psD[0:64, :], RLB[:], op=ALU.mult))
                    s.dma("sync", ctx_d[(st * 2 + h2) * 64:(st * 2 + h2) * 64 + 64, ti * 128:(ti + 1) * 128], CTXS[:])

        for ci in range(8):
            s.dma("sync", WSL[:, 0:512], wo2_d[ci * 128:(ci + 1) * 128, 0:512])
            s.op("vector", lambda e, ci=ci: e.tensor_copy(WOB[:, ci * 1024: ci * 1024 + 512], WSL[:, 0:512]))
            s.dma("sync", WSL[:, 0:512], wo2_d[ci * 128:(ci + 1) * 128, 512:1024])
            s.op("vector", lambda e, ci=ci: e.tensor_copy(WOB[:, ci * 1024 + 512: ci * 1024 + 1024], WSL[:, 0:512]))
        for (go, gw) in GQ:
            for ci in range(8):
                s.dma("sync", CXS[:, ci * 512: ci * 512 + gw], ctx_d[ci * 128:(ci + 1) * 128, go:go + gw])
            for oc in range(8):
                for ci in range(8):
                    s.op("tensor", lambda e, oc=oc, ci=ci, gw=gw: e.matmul(
                        psA[:, 0:gw], WOB[:, ci * 1024 + oc * 128: ci * 1024 + oc * 128 + 128],
                        CXS[:, ci * 512: ci * 512 + gw], start=(ci == 0), stop=(ci == 7)))
                s.dma("sync", WSL[:, 0:gw], zH_d[oc * 128:(oc + 1) * 128, 1 + 256 + go: 1 + 256 + go + gw])
                s.op("vector", lambda e, gw=gw: e.tensor_tensor(OUV[:, 0:gw], psA[:, 0:gw], WSL[:, 0:gw], op=ALU.add))
                s.dma("sync", z2_d[oc * 128:(oc + 1) * 128, go:go + gw], OUV[:, 0:gw])

        for tg in range(4):
            for ci in range(8):
                off = 128 if ci < 4 else 127
                s.dma("sync", XF[:, ci * 512:(ci + 1) * 512], z2_d[ci * 128:(ci + 1) * 128, off + tg * 512: off + tg * 512 + 512])
            for ci in range(8):
                s.op("scalar", lambda e, ci=ci: e.activation(GEV[:], XF[:, ci * 512:(ci + 1) * 512], AF.Square))
                s.op("tensor", lambda e, ci=ci: e.matmul(
                    psB[0:1, 0:512], ONE1[:], XF[:, ci * 512:(ci + 1) * 512], start=(ci == 0), stop=(ci == 7), skip_group_check=True))
                s.op("tensor", lambda e, ci=ci: e.matmul(
                    psC[0:1, 0:512], ONE1[:], GEV[:], start=(ci == 0), stop=(ci == 7), skip_group_check=True))
            s.op("vector", lambda e: e.tensor_scalar(R0[:], psB[0:1, 0:512], 1.0 / D, None, op0=ALU.mult))
            s.op("vector", lambda e: e.tensor_scalar(R1[:], psC[0:1, 0:512], 1.0 / D, None, op0=ALU.mult))
            s.op("vector", lambda e: e.tensor_tensor(R2[:], R0[:], R0[:], op=ALU.mult))
            s.op("vector", lambda e: e.tensor_tensor(R2[:], R1[:], R2[:], op=ALU.subtract))
            s.op("vector", lambda e: e.tensor_scalar(R2[:], R2[:], 1e-5, None, op0=ALU.add))
            s.op("scalar", lambda e: e.activation(R2[:], R2[:], AF.Sqrt))
            s.op("vector", lambda e: e.reciprocal(R2[:], R2[:]))
            s.op("tensor", lambda e: e.matmul(psB[:], ONER[:], R0[:], start=True, stop=True))
            s.op("tensor", lambda e: e.matmul(psC[:], ONER[:], R2[:], start=True, stop=True))
            for ci in range(8):
                s.op("vector", lambda e, ci=ci: e.tensor_tensor(GEV[:], XF[:, ci * 512:(ci + 1) * 512], psB[:], op=ALU.subtract))
                s.op("vector", lambda e: e.tensor_tensor(GEV[:], GEV[:], psC[:], op=ALU.mult))
                s.op("vector", lambda e, ci=ci: e.tensor_scalar(
                    XN[:, ci * 512:(ci + 1) * 512], GEV[:], BIA[:, 52 + ci:53 + ci], BIA[:, 60 + ci:61 + ci], op0=ALU.mult, op1=ALU.add))
            for ac in range(22):
                aw = ACH[ac]
                for ci in range(8):
                    s.dma("sync", WSLB[:, ci * 128: ci * 128 + aw], w1b_d[ci * 128:(ci + 1) * 128, ac * 128: ac * 128 + aw])
                    s.dma("sync", WSLB[:, 1024 + ci * 128: 1024 + ci * 128 + aw], w1b_d[ci * 128:(ci + 1) * 128, DI + ac * 128: DI + ac * 128 + aw])
                for ci in range(8):
                    s.op("tensor", lambda e, ac=ac, aw=aw, ci=ci: e.matmul(
                        psA[0:aw, :] if aw < 128 else psA[:], WSLB[:, ci * 128: ci * 128 + aw],
                        XN[:, ci * 512:(ci + 1) * 512], start=(ci == 0), stop=(ci == 7), skip_group_check=True))
                for ci in range(8):
                    s.op("tensor", lambda e, ac=ac, aw=aw, ci=ci: e.matmul(
                        psB[0:aw, :] if aw < 128 else psB[:], WSLB[:, 1024 + ci * 128: 1024 + ci * 128 + aw],
                        XN[:, ci * 512:(ci + 1) * 512], start=(ci == 0), stop=(ci == 7), skip_group_check=True))
                s.op("vector", lambda e, ac=ac, aw=aw: e.tensor_scalar(
                    AEV[0:aw, :] if aw < 128 else AEV[:], psA[0:aw, :] if aw < 128 else psA[:],
                    BIA[0:aw, ac:ac + 1] if aw < 128 else BIA[:, ac:ac + 1], None, op0=ALU.add))
                s.op("scalar", lambda e, ac=ac, aw=aw: e.activation(
                    GEV[0:aw, :] if aw < 128 else GEV[:], psB[0:aw, :] if aw < 128 else psB[:],
                    AF.Gelu, bias=BIA[0:aw, 22 + ac:23 + ac] if aw < 128 else BIA[:, 22 + ac:23 + ac]))
                s.op("vector", lambda e, ac=ac, aw=aw: e.tensor_tensor(
                    H2T[0:aw, ac * 512:(ac + 1) * 512] if aw < 128 else H2T[:, ac * 512:(ac + 1) * 512],
                    AEV[0:aw, :] if aw < 128 else AEV[:], GEV[0:aw, :] if aw < 128 else GEV[:], op=ALU.mult))
            for oc in range(8):
                for ri in range(22):
                    rh = ACH[ri]
                    s.dma("sync", W2S[0:rh, :], w2b_d[ri * 128: ri * 128 + rh, oc * 128:(oc + 1) * 128])
                    s.op("tensor", lambda e, ri=ri, rh=rh: e.matmul(
                        psA[:], W2S[0:rh, :], H2T[0:rh, ri * 512:(ri + 1) * 512],
                        start=(ri == 0), stop=(ri == 21), skip_group_check=True))
                s.op("vector", lambda e, oc=oc: e.tensor_scalar(OUV[:], psA[:], BIA[:, 44 + oc:45 + oc], None, op0=ALU.add))
                s.dma("sync", WSL[:], z2_d[oc * 128:(oc + 1) * 128, 128 + tg * 512: 128 + tg * 512 + 512])
                s.op("vector", lambda e: e.tensor_tensor(OUV[:], OUV[:], WSL[:], op=ALU.add))
                if last:
                    s.dma("sync", WSL[:], xres_d[oc * 128:(oc + 1) * 128, tg * 512:(tg + 1) * 512])
                    s.op("vector", lambda e: e.tensor_tensor(OUV[:], OUV[:], WSL[:], op=ALU.add))
                s.dma("sync", zo_d[oc * 128:(oc + 1) * 128, tg * 512:(tg + 1) * 512], OUV[:])

        s.build(blk)
    return nc


def rot_tables_host(own0):
    half = np.arange(0, DH, 2, dtype=np.float32)
    inv_freq = (1.0 / 10000.0 ** (half / DH)).astype(np.float32)
    base_scale = ((half + 0.4 * DH) / (1.4 * DH)).astype(np.float32)

    def tabs(pos):
        fr = pos[:, None] * inv_freq[None, :]
        fr64 = np.concatenate([fr, fr], 1)
        sc = base_scale[None, :] ** ((pos - S // 2) / 128.0)[:, None]
        sc64 = np.concatenate([sc, sc], 1).astype(np.float32)
        cos, sin = np.cos(fr64), np.sin(fr64)
        csq = (cos * sc64).T
        snq = (sin * sc64).T
        csk = (cos / sc64).T
        snk = (sin / sc64).T
        stack = lambda a: np.ascontiguousarray(np.concatenate([a, a], 0), dtype=np.float32)
        return stack(csq), stack(snq), stack(csk), stack(snk)
    pk = (own0 - HALO + np.arange(NBUF)).astype(np.float32)
    pq = (own0 - 128 + np.arange(NQ)).astype(np.float32)
    csq, snq, _, _ = tabs(pq)
    _, _, csk, snk = tabs(pk)
    return csq, snq, csk, snk


def bchunk(v, widths):
    out = np.zeros((128, len(widths)), np.float32)
    o = 0
    for i, w in enumerate(widths):
        out[0:w, i] = v[o:o + w]
        o += w
    return out


def l1f_inmaps(d, i, z, last, x3=None):
    rotm = np.zeros((128, 128), np.float32)
    for dd in range(128):
        m = dd % 64
        if m < 32:
            rotm[dd, (dd // 64) * 64 + m + 32] = -1.0
        else:
            rotm[dd, (dd // 64) * 64 + m - 32] = 1.0
    rotmT = np.ascontiguousarray(rotm.T).astype(BF16NP)
    sel2 = np.zeros((128, 2), np.float32); sel2[0:64, 0] = 1; sel2[64:128, 1] = 1
    bb2 = np.zeros((2, 128), np.float32); bb2[0, 0:64] = 1; bb2[1, 64:128] = 1
    zT = t2d(z)
    maps = []
    for c in range(8):
        b, halfc = c // 2, c % 2
        own0 = halfc * TOWN
        zH = np.zeros((D, 1 + NBUF), np.float32)
        lo = own0 - HALO - 1
        src_lo = max(0, lo)
        zH[:, 1 + NBUF - (own0 + TOWN - src_lo) - 0 - 0:] = 0
        seg = zT[b][:, src_lo: own0 + TOWN]
        zH[:, 1 + NBUF - seg.shape[1]:] = seg
        csq, snq, csk, snk = rot_tables_host(own0)
        qidx = (own0 - 128 + np.arange(NQ)).astype(np.float32)
        qidx[qidx < 0] = -1e6
        kbuf = (own0 - HALO + np.arange(NBUF)).astype(np.float32)
        kbuf[kbuf < 0] = -1e6
        kix = np.zeros((128, 19), np.float32)
        for ck in range(19):
            kix[:, ck] = kbuf[ck * 128:(ck + 1) * 128]
        kix2 = kix + 256.0
        mp = {
            "zH": zH, "wqkv": d["L1_ffn_Wqkv"][i], "wo2": d["L1_ffn_Wo"][i],
            "w1": d["L1_ffn_W1"][i], "w2": d["L1_ffn_W2"][i],
            "b1a": bchunk(d["L1_ffn_b1"][i][:DI], ACH), "b1g": bchunk(d["L1_ffn_b1"][i][DI:], ACH),
            "b2c": bcol(d["L1_ffn_b2"][i]), "lng": bcol(d["L1_ffn_ln_g"][i]), "lnb": bcol(d["L1_ffn_ln_b"][i]),
            "qsc": np.tile(d["L1_ffn_q_scale"][i], 2).reshape(128, 1).astype(np.float32),
            "ksc": np.tile(d["L1_ffn_k_scale"][i], 2).reshape(128, 1).astype(np.float32),
            "csq": csq[:, 0:NQ], "snq": snq[:, 0:NQ], "csk": csk[:, 0:NBUF], "snk": snk[:, 0:NBUF],
            "rotm": rotmT, "sel2": sel2, "bb2": bb2,
            "qidx": qidx.reshape(1, NQ), "kix": kix, "kix2": kix2,
        }
        if last:
            mp["xres"] = np.ascontiguousarray(t2d(x3[b])[:, own0:own0 + TOWN])
        maps.append(mp)
    return maps


def run_l1f(nc, maps):
    res = run_bass_kernel_spmd(nc, maps, core_ids=list(range(8)))
    z = np.empty((B, S, D), np.float32)
    for c in range(8):
        b, halfc = c // 2, c % 2
        z[b, halfc * TOWN:(halfc + 1) * TOWN] = res.results[c]["zo"].T
    return z




def kernel(**inputs):
    d = {k: np.asarray(v, dtype=np.float32) for k, v in inputs.items()}
    x1 = run_l0_attn(d)
    x2 = run_l0_ffn(d, x1)
    x3, z = run_l1_attn(d, x2)
    nc_mid = build_l1f(False)
    for i in range(5):
        z = run_l1f(nc_mid, l1f_inmaps(d, i, z, False))
    nc_last = build_l1f(True)
    x4 = run_l1f(nc_last, l1f_inmaps(d, 5, z, True, x3))
    return x4




# version 05
# speedup vs baseline: 1.4820x; 1.1019x over previous
import numpy as np
import ml_dtypes
import contextlib
import concourse.bass as bass
import concourse.mybir as mybir
from concourse.bass_utils import run_bass_kernel_spmd

F32 = mybir.dt.float32
BF16 = mybir.dt.bfloat16
AF = mybir.ActivationFunctionType
ALU = mybir.AluOpType
AX = mybir.AxisListType
BF16NP = ml_dtypes.bfloat16

ENGINES = ["sync", "scalar", "vector", "tensor", "gpsimd"]

B, S, D, H, DH = 4, 4096, 1024, 16, 64
TOWN = S // 2


class Ser:

    def __init__(self, nc, sem):
        self.nc = nc
        self.sem = sem
        self.ops = []

    def op(self, engine, fn, inc=1):
        self.ops.append((engine, fn, inc))

    def dma(self, engine, out, in_):
        self.op(engine, lambda e: e.dma_start(out, in_), 16)

    RELAX = ("tensor", "sync")

    def build(self, block):
        ops, sem = self.ops, self.sem
        waits = []
        cum = 0
        prev_e = None
        run_wait = 0
        for (e, fn, inc) in ops:
            if e in self.RELAX and e == prev_e:
                waits.append(run_wait)
            else:
                waits.append(cum)
                run_wait = cum
            prev_e = e
            cum += inc
        total = cum

        def mk(ename):
            def body(eng):
                cum = 0
                for idx, (e, fn, inc) in enumerate(ops):
                    if e == ename:
                        eng.wait_ge(sem, waits[idx])
                        fn(eng).then_inc(sem, inc)
                    cum += inc
                eng.wait_ge(sem, total)
            return body

        for ename in ENGINES:
            getattr(block, ename)(mk(ename))


def t2d(x):
    return np.ascontiguousarray(np.swapaxes(x, -1, -2))


def bcol(v):
    return np.ascontiguousarray(v.reshape(8, 128).T)


def pe_table():
    pos = np.arange(S, dtype=np.float32)[:, None]
    div = np.exp(np.arange(0, D, 2, dtype=np.float32) * (-np.log(10000.0) / D))
    pe = np.zeros((S, D), np.float32)
    pe[:, 0::2] = np.sin(pos * div)
    pe[:, 1::2] = np.cos(pos * div)
    return pe



def build_l0_attn():
    nc = bass.Bass("TRN2", target_bir_lowering=False)
    xT = nc.dram_tensor("xT", [D, S], F32, kind="ExternalInput").ap()
    xTo = nc.dram_tensor("xTo", [D, TOWN], F32, kind="ExternalInput").ap()
    peT_d = nc.dram_tensor("peT", [D, S], F32, kind="ExternalInput").ap()
    peTo_d = nc.dram_tensor("peTo", [D, TOWN], F32, kind="ExternalInput").ap()
    wq_d = nc.dram_tensor("wq", [D, D], F32, kind="ExternalInput").ap()
    wk_d = nc.dram_tensor("wk", [D, D], F32, kind="ExternalInput").ap()
    wv_d = nc.dram_tensor("wv", [D, D], F32, kind="ExternalInput").ap()
    wo_d = nc.dram_tensor("wo", [D, D], F32, kind="ExternalInput").ap()
    bq_d = nc.dram_tensor("bq", [128, 8], F32, kind="ExternalInput").ap()
    bk_d = nc.dram_tensor("bk", [128, 8], F32, kind="ExternalInput").ap()
    bo_d = nc.dram_tensor("bo", [128, 8], F32, kind="ExternalInput").ap()
    gam_d = nc.dram_tensor("gam", [128, 8], F32, kind="ExternalInput").ap()
    bet_d = nc.dram_tensor("bet", [128, 8], F32, kind="ExternalInput").ap()
    bvr_d = nc.dram_tensor("bvr", [1, D], F32, kind="ExternalInput").ap()
    x1T_d = nc.dram_tensor("x1T", [D, TOWN], F32, kind="ExternalOutput").ap()
    ctx_d = nc.dram_tensor("ctxh", [D, TOWN], BF16).ap()
    qT_d = nc.dram_tensor("qTh", [D, TOWN], BF16).ap()
    st_d = nc.dram_tensor("sth", [4, S], F32).ap()

    ex = contextlib.ExitStack()
    with ex:
        sem = ex.enter_context(nc.semaphore("ser"))
        A_K = ex.enter_context(nc.sbuf_tensor("A_K", [128, 8 * S], BF16))
        A_V = ex.enter_context(nc.sbuf_tensor("A_V", [128, 32 * 1040], BF16))
        A_W = ex.enter_context(nc.sbuf_tensor("A_W", [128, 8192], BF16))
        A_P = ex.enter_context(nc.sbuf_tensor("A_P", [128, 4096], BF16))
        WL = ex.enter_context(nc.sbuf_tensor("WL", [128, 1024], F32))
        XS = ex.enter_context(nc.sbuf_tensor("XS", [128, 512], F32))
        SQ = ex.enter_context(nc.sbuf_tensor("SQ", [128, 512], F32))
        HF = ex.enter_context(nc.sbuf_tensor("HF", [128, 512], F32))
        PEs = ex.enter_context(nc.sbuf_tensor("PEs", [128, 512], F32))
        HBt = ex.enter_context(nc.sbuf_tensor("HBt", [128, 4096], BF16))
        QS = ex.enter_context(nc.sbuf_tensor("QS", [128, 512], BF16))
        R0 = ex.enter_context(nc.sbuf_tensor("R0", [1, 512], F32))
        R1 = ex.enter_context(nc.sbuf_tensor("R1", [1, 512], F32))
        R2 = ex.enter_context(nc.sbuf_tensor("R2", [1, 512], F32))
        RL = ex.enter_context(nc.sbuf_tensor("RL", [1, 512], F32))
        RLB = ex.enter_context(nc.sbuf_tensor("RLB", [64, 512], F32))
        CTXS = ex.enter_context(nc.sbuf_tensor("CTXS", [64, 512], BF16))
        ONE1 = ex.enter_context(nc.sbuf_tensor("ONE1", [128, 1], F32))
        ONER = ex.enter_context(nc.sbuf_tensor("ONER", [1, 128], F32))
        BIA = ex.enter_context(nc.sbuf_tensor("BIA", [128, 40], F32))
        BVRB = ex.enter_context(nc.sbuf_tensor("BVRB", [128, D], F32))
        VS = ex.enter_context(nc.sbuf_tensor("VS", [128, 512], F32))
        ps_s = ex.enter_context(nc.psum_tensor("ps_s", [128, 2048], F32))
        ps_o = ex.enter_context(nc.psum_tensor("ps_o", [65, 512], F32))
        ps_b = ex.enter_context(nc.psum_tensor("ps_b", [128, 512], F32))
        ps_c = ex.enter_context(nc.psum_tensor("ps_c", [128, 512], F32))
        blk = ex.enter_context(nc.Block())

        OA = A_V[:, 0:4096]
        YA = A_V[:, 8192:16384].bitcast(F32)
        XA = A_V[:, 16384:24576].bitcast(F32)

        s = Ser(nc, sem)
        s.op("vector", lambda e: e.memset(ONE1[:], 1.0))
        s.op("vector", lambda e: e.memset(ONER[:], 1.0))
        s.dma("sync", BIA[:, 0:8], bq_d)
        s.dma("sync", BIA[:, 8:16], bk_d)
        s.dma("sync", BIA[:, 16:24], bo_d)
        s.dma("sync", BIA[:, 24:32], gam_d)
        s.dma("sync", BIA[:, 32:40], bet_d)
        s.dma("sync", BVRB[:], bvr_d.partition_broadcast(128))
        for tt in range(32):
            s.op("gpsimd", lambda e, tt=tt: e.memset(
                A_V[:, tt * 1040:(tt + 1) * 1040].rearrange("p (h d) -> p h d", d=65)[:, :, 64:65], 1.0))

        def stats_sweep(src, ntg, mu_row, rstd_row):
            for tg in range(ntg):
                for ci in range(8):
                    s.dma("sync", XS[:], src[ci * 128:(ci + 1) * 128, tg * 512:(tg + 1) * 512])
                    s.op("scalar", lambda e: e.activation(SQ[:], XS[:], AF.Square))
                    s.op("tensor", lambda e, ci=ci: e.matmul(
                        ps_s[0:1, 0:512], ONE1[:], XS[:], start=(ci == 0), stop=(ci == 7), skip_group_check=True))
                    s.op("tensor", lambda e, ci=ci: e.matmul(
                        ps_s[0:1, 512:1024], ONE1[:], SQ[:], start=(ci == 0), stop=(ci == 7), skip_group_check=True))
                s.op("vector", lambda e: e.tensor_scalar(R0[:], ps_s[0:1, 0:512], 1.0 / D, None, op0=ALU.mult))
                s.op("vector", lambda e: e.tensor_scalar(R1[:], ps_s[0:1, 512:1024], 1.0 / D, None, op0=ALU.mult))
                s.op("vector", lambda e: e.tensor_tensor(R2[:], R0[:], R0[:], op=ALU.mult))
                s.op("vector", lambda e: e.tensor_tensor(R2[:], R1[:], R2[:], op=ALU.subtract))
                s.op("vector", lambda e: e.tensor_scalar(R2[:], R2[:], 1e-5, None, op0=ALU.add))
                s.op("scalar", lambda e: e.activation(R2[:], R2[:], AF.Sqrt))
                s.op("vector", lambda e: e.reciprocal(R2[:], R2[:]))
                s.dma("sync", st_d[mu_row:mu_row + 1, tg * 512:(tg + 1) * 512], R0[:])
                s.dma("sync", st_d[rstd_row:rstd_row + 1, tg * 512:(tg + 1) * 512], R2[:])

        def load_w(w_d):
            for ci in range(8):
                s.dma("sync", WL[:], w_d[ci * 128:(ci + 1) * 128, :])
                s.op("vector", lambda e, ci=ci: e.tensor_copy(A_W[:, ci * 1024:(ci + 1) * 1024], WL[:]))

        def make_h(src, pe_src, tg, mu_row, rstd_row, add_pe=True):
            s.dma("sync", R0[:], st_d[mu_row:mu_row + 1, tg * 512:(tg + 1) * 512])
            s.dma("sync", R2[:], st_d[rstd_row:rstd_row + 1, tg * 512:(tg + 1) * 512])
            s.op("tensor", lambda e: e.matmul(ps_b[:], ONER[:], R0[:], start=True, stop=True))
            s.op("tensor", lambda e: e.matmul(ps_c[:], ONER[:], R2[:], start=True, stop=True))
            for ci in range(8):
                s.dma("sync", XS[:], src[ci * 128:(ci + 1) * 128, tg * 512:(tg + 1) * 512])
                s.op("vector", lambda e: e.tensor_tensor(HF[:], XS[:], ps_b[:], op=ALU.subtract))
                if add_pe:
                    s.dma("sync", PEs[:], pe_src[ci * 128:(ci + 1) * 128, tg * 512:(tg + 1) * 512])
                    s.op("vector", lambda e: e.tensor_tensor(HF[:], HF[:], ps_c[:], op=ALU.mult))
                    s.op("vector", lambda e, ci=ci: e.tensor_tensor(HBt[:, ci * 512:(ci + 1) * 512], HF[:], PEs[:], op=ALU.add))
                else:
                    s.op("vector", lambda e, ci=ci: e.tensor_tensor(HBt[:, ci * 512:(ci + 1) * 512], HF[:], ps_c[:], op=ALU.mult))

        stats_sweep(xT, 8, 0, 1)
        stats_sweep(xTo, 4, 2, 3)

        load_w(wk_d)
        for tg in range(8):
            make_h(xT, peT_d, tg, 0, 1)
            for oc in range(8):
                for ci in range(8):
                    s.op("tensor", lambda e, oc=oc, ci=ci: e.matmul(
                        ps_s[:, 0:512], A_W[:, ci * 1024 + oc * 128: ci * 1024 + oc * 128 + 128],
                        HBt[:, ci * 512:(ci + 1) * 512], start=(ci == 0), stop=(ci == 7)))
                s.op("vector", lambda e, oc=oc, tg=tg: e.tensor_scalar(
                    A_K[:, oc * S + tg * 512: oc * S + tg * 512 + 512], ps_s[:, 0:512],
                    BIA[:, 8 + oc:9 + oc], None, op0=ALU.add))

        load_w(wv_d)
        for tg in range(8):
            make_h(xT, peT_d, tg, 0, 1)
            for tt4 in range(4):
                tt = tg * 4 + tt4
                for nh in range(2):
                    for ci in range(8):
                        s.op("tensor", lambda e, tt4=tt4, nh=nh, ci=ci: e.matmul(
                            ps_b[:], HBt[:, ci * 512 + tt4 * 128: ci * 512 + tt4 * 128 + 128],
                            A_W[:, ci * 1024 + nh * 512: ci * 1024 + nh * 512 + 512],
                            start=(ci == 0), stop=(ci == 7)))
                    s.op("scalar", lambda e: e.activation(VS[:], ps_b[:], AF.Copy))
                    s.op("vector", lambda e, tt=tt, nh=nh: e.tensor_tensor(
                        A_V[:, tt * 1040 + nh * 520: tt * 1040 + nh * 520 + 520].rearrange("p (h d) -> p h d", d=65)[:, :, 0:64],
                        VS[:].rearrange("p (h d) -> p h d", d=64),
                        BVRB[:, nh * 512:(nh + 1) * 512].rearrange("p (h d) -> p h d", d=64), op=ALU.add))

        load_w(wq_d)
        for tg in range(4):
            make_h(xTo, peTo_d, tg, 2, 3)
            for oc in range(8):
                for ci in range(8):
                    s.op("tensor", lambda e, oc=oc, ci=ci: e.matmul(
                        ps_s[:, 0:512], A_W[:, ci * 1024 + oc * 128: ci * 1024 + oc * 128 + 128],
                        HBt[:, ci * 512:(ci + 1) * 512], start=(ci == 0), stop=(ci == 7)))
                s.op("vector", lambda e, oc=oc: e.tensor_scalar(QS[:], ps_s[:, 0:512], BIA[:, oc:oc + 1], None, op0=ALU.add))
                s.dma("sync", qT_d[oc * 128:(oc + 1) * 128, tg * 512:(tg + 1) * 512], QS[:])

        for qg in range(4):
            for ci in range(8):
                s.dma("sync", A_W[:, ci * 512:(ci + 1) * 512], qT_d[ci * 128:(ci + 1) * 128, qg * 512:(qg + 1) * 512])
            for h in range(16):
                hr = (h % 2) * 64
                hc = (h // 2)
                for kb in range(8):
                    for j in range(4):
                        kc = kb * 4 + j
                        s.op("tensor", lambda e, hr=hr, hc=hc, kc=kc, j=j: e.matmul(
                            ps_s[:, j * 512:(j + 1) * 512],
                            A_K[hr:hr + 64, hc * S + kc * 128: hc * S + kc * 128 + 128],
                            A_W[hr:hr + 64, hc * 512: hc * 512 + 512],
                            start=True, stop=True, skip_group_check=True))
                    s.op("scalar", lambda e: e.activation(A_P[:, 0:2048], ps_s[:], AF.Exp, scale=0.125))
                    for j in range(4):
                        kc = kb * 4 + j
                        s.op("tensor", lambda e, kc=kc, h=h, j=j, kb=kb: e.matmul(
                            ps_o[:], A_V[:, kc * 1040 + 65 * h: kc * 1040 + 65 * h + 65],
                            A_P[:, j * 512:(j + 1) * 512],
                            start=(kb == 0 and j == 0), stop=(kb == 7 and j == 3), skip_group_check=True))
                s.op("vector", lambda e: e.reciprocal(RL[:], ps_o[64:65, :]))
                s.op("tensor", lambda e: e.matmul(ps_b[0:64, :], ONER[:, 0:64], RL[:], start=True, stop=True))
                s.op("scalar", lambda e: e.activation(RLB[:], ps_b[0:64, :], AF.Copy))
                s.op("vector", lambda e: e.tensor_tensor(CTXS[:], ps_o[0:64, :], RLB[:], op=ALU.mult))
                s.dma("sync", ctx_d[h * 64:(h + 1) * 64, qg * 512:(qg + 1) * 512], CTXS[:])

        load_w(wo_d)
        for tg in range(4):
            for ci in range(8):
                s.dma("sync", A_P[:, ci * 512:(ci + 1) * 512], ctx_d[ci * 128:(ci + 1) * 128, tg * 512:(tg + 1) * 512])
            for oc in range(8):
                for ci in range(8):
                    s.op("tensor", lambda e, oc=oc, ci=ci: e.matmul(
                        ps_s[:, 0:512], A_W[:, ci * 1024 + oc * 128: ci * 1024 + oc * 128 + 128],
                        A_P[:, ci * 512:(ci + 1) * 512], start=(ci == 0), stop=(ci == 7)))
                s.op("vector", lambda e, oc=oc: e.tensor_scalar(OA[:, oc * 512:(oc + 1) * 512], ps_s[:, 0:512],
                                                                BIA[:, 16 + oc:17 + oc], None, op0=ALU.add))
            s.dma("sync", R0[:], st_d[2:3, tg * 512:(tg + 1) * 512])
            s.dma("sync", R2[:], st_d[3:4, tg * 512:(tg + 1) * 512])
            s.op("tensor", lambda e: e.matmul(ps_b[:], ONER[:], R0[:], start=True, stop=True))
            s.op("tensor", lambda e: e.matmul(ps_c[:], ONER[:], R2[:], start=True, stop=True))
            for ci in range(8):
                s.dma("sync", XA[:, ci * 512:(ci + 1) * 512], xTo[ci * 128:(ci + 1) * 128, tg * 512:(tg + 1) * 512])
                s.dma("sync", PEs[:], peTo_d[ci * 128:(ci + 1) * 128, tg * 512:(tg + 1) * 512])
                s.op("vector", lambda e, ci=ci: e.tensor_tensor(HF[:], XA[:, ci * 512:(ci + 1) * 512], ps_b[:], op=ALU.subtract))
                s.op("vector", lambda e: e.tensor_tensor(HF[:], HF[:], ps_c[:], op=ALU.mult))
                s.op("vector", lambda e: e.tensor_tensor(HF[:], HF[:], PEs[:], op=ALU.add))
                s.op("vector", lambda e, ci=ci: e.tensor_tensor(
                    YA[:, ci * 512:(ci + 1) * 512], HF[:], OA[:, ci * 512:(ci + 1) * 512], op=ALU.add))
                s.op("scalar", lambda e, ci=ci: e.activation(SQ[:], YA[:, ci * 512:(ci + 1) * 512], AF.Square))
                s.op("tensor", lambda e, ci=ci: e.matmul(
                    ps_s[0:1, 0:512], ONE1[:], YA[:, ci * 512:(ci + 1) * 512], start=(ci == 0), stop=(ci == 7), skip_group_check=True))
                s.op("tensor", lambda e, ci=ci: e.matmul(
                    ps_s[0:1, 512:1024], ONE1[:], SQ[:], start=(ci == 0), stop=(ci == 7), skip_group_check=True))
            s.op("vector", lambda e: e.tensor_scalar(R0[:], ps_s[0:1, 0:512], 1.0 / D, None, op0=ALU.mult))
            s.op("vector", lambda e: e.tensor_scalar(R1[:], ps_s[0:1, 512:1024], 1.0 / D, None, op0=ALU.mult))
            s.op("vector", lambda e: e.tensor_tensor(R2[:], R0[:], R0[:], op=ALU.mult))
            s.op("vector", lambda e: e.tensor_tensor(R2[:], R1[:], R2[:], op=ALU.subtract))
            s.op("vector", lambda e: e.tensor_scalar(R2[:], R2[:], 1e-5, None, op0=ALU.add))
            s.op("scalar", lambda e: e.activation(R2[:], R2[:], AF.Sqrt))
            s.op("vector", lambda e: e.reciprocal(R2[:], R2[:]))
            s.op("tensor", lambda e: e.matmul(ps_b[:], ONER[:], R0[:], start=True, stop=True))
            s.op("tensor", lambda e: e.matmul(ps_c[:], ONER[:], R2[:], start=True, stop=True))
            for ci in range(8):
                s.op("vector", lambda e, ci=ci: e.tensor_tensor(HF[:], YA[:, ci * 512:(ci + 1) * 512], ps_b[:], op=ALU.subtract))
                s.op("vector", lambda e: e.tensor_tensor(HF[:], HF[:], ps_c[:], op=ALU.mult))
                s.op("vector", lambda e, ci=ci: e.tensor_scalar(HF[:], HF[:], BIA[:, 24 + ci:25 + ci], BIA[:, 32 + ci:33 + ci], op0=ALU.mult, op1=ALU.add))
                s.op("vector", lambda e, ci=ci: e.tensor_tensor(HF[:], HF[:], XA[:, ci * 512:(ci + 1) * 512], op=ALU.add))
                s.dma("sync", x1T_d[ci * 128:(ci + 1) * 128, tg * 512:(tg + 1) * 512], HF[:])

        s.build(blk)
    return nc


def run_l0_attn(d):
    nc = build_l0_attn()
    pe = pe_table()
    peT = t2d(pe)
    xT_all = t2d(d["L0_attn_x"])
    in_maps = []
    for c in range(8):
        b, half = c // 2, c % 2
        own = slice(half * TOWN, (half + 1) * TOWN)
        in_maps.append({
            "xT": xT_all[b],
            "xTo": np.ascontiguousarray(xT_all[b][:, own]),
            "peT": peT,
            "peTo": np.ascontiguousarray(peT[:, own]),
            "wq": d["L0_attn_Wq"], "wk": d["L0_attn_Wk"], "wv": d["L0_attn_Wv"], "wo": d["L0_attn_Wo"],
            "bq": bcol(d["L0_attn_bq"]), "bk": bcol(d["L0_attn_bk"]), "bo": bcol(d["L0_attn_bo"]),
            "gam": bcol(d["L0_attn_gamma"]), "bet": bcol(d["L0_attn_beta"]),
            "bvr": d["L0_attn_bv"].reshape(1, D),
        })
    res = run_bass_kernel_spmd(nc, in_maps, core_ids=list(range(8)))
    x1 = np.empty((B, S, D), np.float32)
    for c in range(8):
        b, half = c // 2, c % 2
        x1[b, half * TOWN:(half + 1) * TOWN] = res.results[c]["x1T"].T
    return x1




def build_l0_ffn():
    nc = bass.Bass("TRN2", target_bir_lowering=False)
    IN2 = 2816
    x1o_d = nc.dram_tensor("x1o", [TOWN, D], F32, kind="ExternalInput").ap()
    wg_d = nc.dram_tensor("wg", [IN2, D], F32, kind="ExternalInput").ap()
    wu_d = nc.dram_tensor("wu", [IN2, D], F32, kind="ExternalInput").ap()
    wd_d = nc.dram_tensor("wd", [D, IN2], F32, kind="ExternalInput").ap()
    idn_d = nc.dram_tensor("idn", [128, 128], BF16, kind="ExternalInput").ap()
    x2o_d = nc.dram_tensor("x2o", [TOWN, D], F32, kind="ExternalOutput").ap()

    ex = contextlib.ExitStack()
    with ex:
        sem = ex.enter_context(nc.semaphore("ser"))
        WGT = ex.enter_context(nc.sbuf_tensor("WGT", [128, 8 * IN2], BF16))
        WUT = ex.enter_context(nc.sbuf_tensor("WUT", [128, 8 * IN2], BF16))
        WDT = ex.enter_context(nc.sbuf_tensor("WDT", [128, 22 * D], BF16))
        XT_ = ex.enter_context(nc.sbuf_tensor("XT_", [128, 1408], F32))
        ZT_ = ex.enter_context(nc.sbuf_tensor("ZT_", [128, 1024], F32))
        T_A = ex.enter_context(nc.sbuf_tensor("T_A", [128, 1408], F32))
        T_CA = ex.enter_context(nc.sbuf_tensor("T_CA", [128, 1408], F32))
        T_CB = ex.enter_context(nc.sbuf_tensor("T_CB", [128, 1408], F32))
        MB = ex.enter_context(nc.sbuf_tensor("MB", [128, 88], F32))
        SB = ex.enter_context(nc.sbuf_tensor("SB", [128, 88], F32))
        RB = ex.enter_context(nc.sbuf_tensor("RB", [128, 88], F32))
        QB = ex.enter_context(nc.sbuf_tensor("QB", [128, 1408], BF16))
        ZTT = ex.enter_context(nc.sbuf_tensor("ZTT", [128, 1024], BF16))
        HT = ex.enter_context(nc.sbuf_tensor("HT", [128, IN2], BF16))
        GA = ex.enter_context(nc.sbuf_tensor("GA", [128, 512], F32))
        UA = ex.enter_context(nc.sbuf_tensor("UA", [128, 512], F32))
        HH = ex.enter_context(nc.sbuf_tensor("HH", [128, 512], F32))
        OUTT = ex.enter_context(nc.sbuf_tensor("OUTT", [128, 1024], F32))
        IDN = ex.enter_context(nc.sbuf_tensor("IDN", [128, 128], BF16))
        SQA = ex.enter_context(nc.sbuf_tensor("SQA", [128, 1024], F32))
        MU = ex.enter_context(nc.sbuf_tensor("MU", [128, 1], F32))
        MSQ = ex.enter_context(nc.sbuf_tensor("MSQ", [128, 1], F32))
        RSTD = ex.enter_context(nc.sbuf_tensor("RSTD", [128, 1], F32))
        ps_g = ex.enter_context(nc.psum_tensor("ps_g", [128, 512], F32))
        ps_u = ex.enter_context(nc.psum_tensor("ps_u", [128, 512], F32))
        ps_t = ex.enter_context(nc.psum_tensor("ps_t", [128, 128], BF16))
        blk = ex.enter_context(nc.Block())

        s = Ser(nc, sem)
        s.dma("sync", IDN[:], idn_d)

        def quant(src, out, F):
            G = F // 16
            s.op("vector", lambda e: e.tensor_reduce(
                MB[:, 0:G], src.rearrange("p (g s) -> p g s", s=16), axis=AX.X,
                op=ALU.max, apply_absolute_value=True))
            s.op("vector", lambda e: e.tensor_scalar(SB[:, 0:G], MB[:, 0:G], 1.0 / 6.0, 1e-12, op0=ALU.mult, op1=ALU.max))
            s.op("vector", lambda e: e.reciprocal(RB[:, 0:G], SB[:, 0:G]))
            s.op("vector", lambda e: e.tensor_tensor(
                src.rearrange("p (g s) -> p g s", s=16), src.rearrange("p (g s) -> p g s", s=16),
                RB[:, 0:G].unsqueeze(2).broadcast_to([128, G, 16]), op=ALU.mult))
            s.op("scalar", lambda e: e.activation(T_A[:, 0:F], src, AF.Abs))
            s.op("vector", lambda e: e.tensor_scalar(T_CB[:, 0:F], T_A[:, 0:F], 5.0, 2.0, op0=ALU.is_gt, op1=ALU.mult))
            s.op("vector", lambda e: e.scalar_tensor_tensor(T_CB[:, 0:F], T_A[:, 0:F], 2.5, T_CB[:, 0:F], op0=ALU.is_gt, op1=ALU.add))
            s.op("vector", lambda e: e.scalar_tensor_tensor(T_CB[:, 0:F], T_A[:, 0:F], 3.5, T_CB[:, 0:F], op0=ALU.is_gt, op1=ALU.add))
            s.op("vector", lambda e: e.tensor_scalar(T_CA[:, 0:F], T_A[:, 0:F], 0.25, None, op0=ALU.is_gt))
            s.op("vector", lambda e: e.scalar_tensor_tensor(T_CA[:, 0:F], T_A[:, 0:F], 0.75, T_CA[:, 0:F], op0=ALU.is_gt, op1=ALU.add))
            s.op("vector", lambda e: e.scalar_tensor_tensor(T_CA[:, 0:F], T_A[:, 0:F], 1.25, T_CA[:, 0:F], op0=ALU.is_gt, op1=ALU.add))
            s.op("vector", lambda e: e.scalar_tensor_tensor(T_CA[:, 0:F], T_A[:, 0:F], 1.75, T_CA[:, 0:F], op0=ALU.is_gt, op1=ALU.add))
            s.op("vector", lambda e: e.scalar_tensor_tensor(T_CA[:, 0:F], T_CA[:, 0:F], 0.5, T_CB[:, 0:F], op0=ALU.mult, op1=ALU.add))
            s.op("scalar", lambda e: e.activation(T_CB[:, 0:F], src, AF.Sign))
            s.op("vector", lambda e: e.tensor_tensor(
                T_CB[:, 0:F].rearrange("p (g s) -> p g s", s=16), T_CB[:, 0:F].rearrange("p (g s) -> p g s", s=16),
                SB[:, 0:G].unsqueeze(2).broadcast_to([128, G, 16]), op=ALU.mult))
            s.op("gpsimd", lambda e: e.tensor_tensor(out, T_CA[:, 0:F], T_CB[:, 0:F], op=ALU.mult))

        def transpose_to(src_bf16, dst):
            s.op("tensor", lambda e: e.transpose(ps_t[:], src_bf16, IDN[:]))
            s.op("vector", lambda e: e.tensor_copy(dst, ps_t[:]))

        for r in range(22):
            s.dma("sync", XT_[:, 0:1024], wg_d[r * 128:(r + 1) * 128, :])
            quant(XT_[:, 0:1024], QB[:, 0:1024], 1024)
            for ci in range(8):
                transpose_to(QB[:, ci * 128:(ci + 1) * 128], WGT[:, ci * IN2 + r * 128: ci * IN2 + r * 128 + 128])
        for r in range(22):
            s.dma("sync", XT_[:, 0:1024], wu_d[r * 128:(r + 1) * 128, :])
            quant(XT_[:, 0:1024], QB[:, 0:1024], 1024)
            for ci in range(8):
                transpose_to(QB[:, ci * 128:(ci + 1) * 128], WUT[:, ci * IN2 + r * 128: ci * IN2 + r * 128 + 128])
        for r in range(8):
            for hh in range(2):
                s.dma("sync", XT_[:, 0:1408], wd_d[r * 128:(r + 1) * 128, hh * 1408:(hh + 1) * 1408])
                quant(XT_[:, 0:1408], QB[:, 0:1408], 1408)
                for cj in range(11):
                    ri = hh * 11 + cj
                    transpose_to(QB[:, cj * 128:(cj + 1) * 128], WDT[:, ri * D + r * 128: ri * D + r * 128 + 128])

        OCW = [(0, 512), (512, 512), (1024, 512), (1536, 512), (2048, 512), (2560, 256)]
        for t in range(16):
            s.dma("sync", XT_[:, 0:1024], x1o_d[t * 128:(t + 1) * 128, :])
            s.op("vector", lambda e: e.tensor_reduce(MU[:], XT_[:, 0:1024], axis=AX.X, op=ALU.add))
            s.op("vector", lambda e: e.tensor_scalar(MU[:], MU[:], 1.0 / D, None, op0=ALU.mult))
            s.op("scalar", lambda e: e.activation(SQA[:], XT_[:, 0:1024], AF.Square))
            s.op("vector", lambda e: e.tensor_reduce(MSQ[:], SQA[:], axis=AX.X, op=ALU.add))
            s.op("vector", lambda e: e.tensor_scalar(MSQ[:], MSQ[:], 1.0 / D, None, op0=ALU.mult))
            s.op("vector", lambda e: e.tensor_tensor(RSTD[:], MU[:], MU[:], op=ALU.mult))
            s.op("vector", lambda e: e.tensor_tensor(RSTD[:], MSQ[:], RSTD[:], op=ALU.subtract))
            s.op("vector", lambda e: e.tensor_scalar(RSTD[:], RSTD[:], 1e-5, None, op0=ALU.add))
            s.op("scalar", lambda e: e.activation(RSTD[:], RSTD[:], AF.Sqrt))
            s.op("vector", lambda e: e.reciprocal(RSTD[:], RSTD[:]))
            s.op("vector", lambda e: e.tensor_scalar(ZT_[:], XT_[:, 0:1024], MU[:], RSTD[:], op0=ALU.subtract, op1=ALU.mult))
            quant(ZT_[:], QB[:, 0:1024], 1024)
            for ci in range(8):
                transpose_to(QB[:, ci * 128:(ci + 1) * 128], ZTT[:, ci * 128:(ci + 1) * 128])
            for (ooff, ow) in OCW:
                for ci in range(8):
                    s.op("tensor", lambda e, ci=ci, ooff=ooff, ow=ow: e.matmul(
                        ps_g[:, 0:ow], ZTT[:, ci * 128:(ci + 1) * 128],
                        WGT[:, ci * IN2 + ooff: ci * IN2 + ooff + ow], start=(ci == 0), stop=(ci == 7), skip_group_check=True))
                for ci in range(8):
                    s.op("tensor", lambda e, ci=ci, ooff=ooff, ow=ow: e.matmul(
                        ps_u[:, 0:ow], ZTT[:, ci * 128:(ci + 1) * 128],
                        WUT[:, ci * IN2 + ooff: ci * IN2 + ooff + ow], start=(ci == 0), stop=(ci == 7), skip_group_check=True))
                s.op("scalar", lambda e, ow=ow: e.activation(GA[:, 0:ow], ps_g[:, 0:ow], AF.Silu))
                s.op("scalar", lambda e, ow=ow: e.activation(UA[:, 0:ow], ps_u[:, 0:ow], AF.Copy))
                s.op("vector", lambda e, ow=ow: e.tensor_tensor(HH[:, 0:ow], GA[:, 0:ow], UA[:, 0:ow], op=ALU.mult))
                quant(HH[:, 0:ow], QB[:, 0:ow], ow)
                for k in range(ow // 128):
                    transpose_to(QB[:, k * 128:(k + 1) * 128], HT[:, ooff + k * 128: ooff + (k + 1) * 128])
            for oc2 in range(2):
                for ri in range(22):
                    s.op("tensor", lambda e, ri=ri, oc2=oc2: e.matmul(
                        ps_g[:], HT[:, ri * 128:(ri + 1) * 128],
                        WDT[:, ri * D + oc2 * 512: ri * D + oc2 * 512 + 512], start=(ri == 0), stop=(ri == 21), skip_group_check=True))
                s.op("vector", lambda e, oc2=oc2: e.tensor_tensor(
                    OUTT[:, oc2 * 512:(oc2 + 1) * 512], ps_g[:], XT_[:, oc2 * 512: oc2 * 512 + 512], op=ALU.add))
            s.dma("sync", x2o_d[t * 128:(t + 1) * 128, :], OUTT[:])

        s.build(blk)
    return nc


def run_l0_ffn(d, x1):
    nc = build_l0_ffn()
    idn = np.eye(128).astype(BF16NP)
    in_maps = []
    for c in range(8):
        b, half = c // 2, c % 2
        in_maps.append({
            "x1o": np.ascontiguousarray(x1[b, half * TOWN:(half + 1) * TOWN]),
            "wg": d["L0_ffn_w_gate"], "wu": d["L0_ffn_w_up"], "wd": d["L0_ffn_w_down"],
            "idn": idn,
        })
    res = run_bass_kernel_spmd(nc, in_maps, core_ids=list(range(8)))
    x2 = np.empty((B, S, D), np.float32)
    for c in range(8):
        b, half = c // 2, c % 2
        x2[b, half * TOWN:(half + 1) * TOWN] = res.results[c]["x2o"]
    return x2




def build_l1_attn():
    nc = bass.Bass("TRN2", target_bir_lowering=False)
    xT = nc.dram_tensor("xT", [D, S], F32, kind="ExternalInput").ap()
    xTo = nc.dram_tensor("xTo", [D, TOWN], F32, kind="ExternalInput").ap()
    wq_d = nc.dram_tensor("wq", [D, D], F32, kind="ExternalInput").ap()
    wk_d = nc.dram_tensor("wk", [D, D], F32, kind="ExternalInput").ap()
    wv_d = nc.dram_tensor("wv", [D, D], F32, kind="ExternalInput").ap()
    wo_d = nc.dram_tensor("wo", [D, D], F32, kind="ExternalInput").ap()
    bq_d = nc.dram_tensor("bq", [128, 8], F32, kind="ExternalInput").ap()
    bk_d = nc.dram_tensor("bk", [128, 8], F32, kind="ExternalInput").ap()
    bo_d = nc.dram_tensor("bo", [128, 8], F32, kind="ExternalInput").ap()
    bvr_d = nc.dram_tensor("bvr", [1, D], F32, kind="ExternalInput").ap()
    qidx_d = nc.dram_tensor("qidx", [1, TOWN], F32, kind="ExternalInput").ap()
    kidx_d = nc.dram_tensor("kidx", [128, 32], F32, kind="ExternalInput").ap()
    x3T_d = nc.dram_tensor("x3T", [D, TOWN], F32, kind="ExternalOutput").ap()
    z0T_d = nc.dram_tensor("z0T", [D, TOWN], F32, kind="ExternalOutput").ap()
    ctx_d = nc.dram_tensor("ctxh", [D, TOWN], BF16).ap()
    qT_d = nc.dram_tensor("qTh", [D, TOWN], BF16).ap()
    st_d = nc.dram_tensor("sth", [4, S], F32).ap()

    ex = contextlib.ExitStack()
    with ex:
        sem = ex.enter_context(nc.semaphore("ser"))
        A_K = ex.enter_context(nc.sbuf_tensor("A_K", [128, 8 * S], BF16))
        A_V = ex.enter_context(nc.sbuf_tensor("A_V", [128, 32 * 1040], BF16))
        A_W = ex.enter_context(nc.sbuf_tensor("A_W", [128, 8192], BF16))
        A_P = ex.enter_context(nc.sbuf_tensor("A_P", [128, 4096], BF16))
        WS = ex.enter_context(nc.sbuf_tensor("WS", [128, 16384], BF16))
        QS = ex.enter_context(nc.sbuf_tensor("QS", [128, 512], BF16))
        QIF = ex.enter_context(nc.sbuf_tensor("QIF", [128, 512], F32))
        KIX = ex.enter_context(nc.sbuf_tensor("KIX", [128, 32], F32))
        R0 = ex.enter_context(nc.sbuf_tensor("R0", [1, 512], F32))
        R1 = ex.enter_context(nc.sbuf_tensor("R1", [1, 512], F32))
        R2 = ex.enter_context(nc.sbuf_tensor("R2", [1, 512], F32))
        RL = ex.enter_context(nc.sbuf_tensor("RL", [1, 512], F32))
        RLB = ex.enter_context(nc.sbuf_tensor("RLB", [64, 512], F32))
        CTXS = ex.enter_context(nc.sbuf_tensor("CTXS", [64, 512], BF16))
        ONE1 = ex.enter_context(nc.sbuf_tensor("ONE1", [128, 1], F32))
        ONER = ex.enter_context(nc.sbuf_tensor("ONER", [1, 128], F32))
        BIA = ex.enter_context(nc.sbuf_tensor("BIA", [128, 24], F32))
        BVRB = ex.enter_context(nc.sbuf_tensor("BVRB", [128, D], F32))
        ps_s = ex.enter_context(nc.psum_tensor("ps_s", [128, 2048], F32))
        ps_o = ex.enter_context(nc.psum_tensor("ps_o", [65, 512], F32))
        ps_b = ex.enter_context(nc.psum_tensor("ps_b", [128, 512], F32))
        ps_c = ex.enter_context(nc.psum_tensor("ps_c", [128, 512], F32))
        blk = ex.enter_context(nc.Block())

        HBt = WS[:, 0:4096]
        WLv = WS[:, 4096:6144].bitcast(F32)
        XSv = WS[:, 6144:7168].bitcast(F32)
        SQv = WS[:, 7168:8192].bitcast(F32)
        HFv = WS[:, 8192:9216].bitcast(F32)
        VSv = WS[:, 9216:10240].bitcast(F32)
        OA = A_V[:, 0:4096]
        YA = A_V[:, 8192:16384].bitcast(F32)
        XA = A_V[:, 16384:24576].bitcast(F32)

        s = Ser(nc, sem)
        s.op("vector", lambda e: e.memset(ONE1[:], 1.0))
        s.op("vector", lambda e: e.memset(ONER[:], 1.0))
        s.dma("sync", BIA[:, 0:8], bq_d)
        s.dma("sync", BIA[:, 8:16], bk_d)
        s.dma("sync", BIA[:, 16:24], bo_d)
        s.dma("sync", BVRB[:], bvr_d.partition_broadcast(128))
        s.dma("sync", KIX[:], kidx_d)
        for tt in range(32):
            s.op("gpsimd", lambda e, tt=tt: e.memset(
                A_V[:, tt * 1040:(tt + 1) * 1040].rearrange("p (h d) -> p h d", d=65)[:, :, 64:65], 1.0))

        def stats_sweep(src, ntg, mu_row, rstd_row):
            for tg in range(ntg):
                for ci in range(8):
                    s.dma("sync", XSv, src[ci * 128:(ci + 1) * 128, tg * 512:(tg + 1) * 512])
                    s.op("scalar", lambda e: e.activation(SQv, XSv, AF.Square))
                    s.op("tensor", lambda e, ci=ci: e.matmul(
                        ps_s[0:1, 0:512], ONE1[:], XSv, start=(ci == 0), stop=(ci == 7), skip_group_check=True))
                    s.op("tensor", lambda e, ci=ci: e.matmul(
                        ps_s[0:1, 512:1024], ONE1[:], SQv, start=(ci == 0), stop=(ci == 7), skip_group_check=True))
                s.op("vector", lambda e: e.tensor_scalar(R0[:], ps_s[0:1, 0:512], 1.0 / D, None, op0=ALU.mult))
                s.op("vector", lambda e: e.tensor_scalar(R1[:], ps_s[0:1, 512:1024], 1.0 / D, None, op0=ALU.mult))
                s.op("vector", lambda e: e.tensor_tensor(R2[:], R0[:], R0[:], op=ALU.mult))
                s.op("vector", lambda e: e.tensor_tensor(R2[:], R1[:], R2[:], op=ALU.subtract))
                s.op("vector", lambda e: e.tensor_scalar(R2[:], R2[:], 1e-5, None, op0=ALU.add))
                s.op("scalar", lambda e: e.activation(R2[:], R2[:], AF.Sqrt))
                s.op("vector", lambda e: e.reciprocal(R2[:], R2[:]))
                s.dma("sync", st_d[mu_row:mu_row + 1, tg * 512:(tg + 1) * 512], R0[:])
                s.dma("sync", st_d[rstd_row:rstd_row + 1, tg * 512:(tg + 1) * 512], R2[:])

        def load_w(w_d):
            for ci in range(8):
                s.dma("sync", WLv, w_d[ci * 128:(ci + 1) * 128, :])
                s.op("vector", lambda e, ci=ci: e.tensor_copy(A_W[:, ci * 1024:(ci + 1) * 1024], WLv))

        def make_h(src, tg, mu_row, rstd_row):
            s.dma("sync", R0[:], st_d[mu_row:mu_row + 1, tg * 512:(tg + 1) * 512])
            s.dma("sync", R2[:], st_d[rstd_row:rstd_row + 1, tg * 512:(tg + 1) * 512])
            s.op("tensor", lambda e: e.matmul(ps_b[:], ONER[:], R0[:], start=True, stop=True))
            s.op("tensor", lambda e: e.matmul(ps_c[:], ONER[:], R2[:], start=True, stop=True))
            for ci in range(8):
                s.dma("sync", XSv, src[ci * 128:(ci + 1) * 128, tg * 512:(tg + 1) * 512])
                s.op("vector", lambda e: e.tensor_tensor(HFv, XSv, ps_b[:], op=ALU.subtract))
                s.op("vector", lambda e, ci=ci: e.tensor_tensor(HBt[:, ci * 512:(ci + 1) * 512], HFv, ps_c[:], op=ALU.mult))

        stats_sweep(xT, 8, 0, 1)
        stats_sweep(xTo, 4, 2, 3)

        load_w(wk_d)
        for tg in range(8):
            make_h(xT, tg, 0, 1)
            for oc in range(8):
                for ci in range(8):
                    s.op("tensor", lambda e, oc=oc, ci=ci: e.matmul(
                        ps_s[:, 0:512], A_W[:, ci * 1024 + oc * 128: ci * 1024 + oc * 128 + 128],
                        HBt[:, ci * 512:(ci + 1) * 512], start=(ci == 0), stop=(ci == 7)))
                s.op("vector", lambda e, oc=oc, tg=tg: e.tensor_scalar(
                    A_K[:, oc * S + tg * 512: oc * S + tg * 512 + 512], ps_s[:, 0:512],
                    BIA[:, 8 + oc:9 + oc], None, op0=ALU.add))

        load_w(wv_d)
        for tg in range(8):
            make_h(xT, tg, 0, 1)
            for tt4 in range(4):
                tt = tg * 4 + tt4
                for nh in range(2):
                    for ci in range(8):
                        s.op("tensor", lambda e, tt4=tt4, nh=nh, ci=ci: e.matmul(
                            ps_b[:], HBt[:, ci * 512 + tt4 * 128: ci * 512 + tt4 * 128 + 128],
                            A_W[:, ci * 1024 + nh * 512: ci * 1024 + nh * 512 + 512],
                            start=(ci == 0), stop=(ci == 7)))
                    s.op("scalar", lambda e: e.activation(VSv, ps_b[:], AF.Copy))
                    s.op("vector", lambda e, tt=tt, nh=nh: e.tensor_tensor(
                        A_V[:, tt * 1040 + nh * 520: tt * 1040 + nh * 520 + 520].rearrange("p (h d) -> p h d", d=65)[:, :, 0:64],
                        VSv.rearrange("p (h d) -> p h d", d=64),
                        BVRB[:, nh * 512:(nh + 1) * 512].rearrange("p (h d) -> p h d", d=64), op=ALU.add))

        load_w(wq_d)
        for tg in range(4):
            make_h(xTo, tg, 2, 3)
            for oc in range(8):
                for ci in range(8):
                    s.op("tensor", lambda e, oc=oc, ci=ci: e.matmul(
                        ps_s[:, 0:512], A_W[:, ci * 1024 + oc * 128: ci * 1024 + oc * 128 + 128],
                        HBt[:, ci * 512:(ci + 1) * 512], start=(ci == 0), stop=(ci == 7)))
                s.op("vector", lambda e, oc=oc: e.tensor_scalar(QS[:], ps_s[:, 0:512], BIA[:, oc:oc + 1], None, op0=ALU.add))
                s.dma("sync", qT_d[oc * 128:(oc + 1) * 128, tg * 512:(tg + 1) * 512], QS[:])

        MSKA = WS[:]
        for qg in range(4):
            for ci in range(8):
                s.dma("sync", A_W[:, ci * 512:(ci + 1) * 512], qT_d[ci * 128:(ci + 1) * 128, qg * 512:(qg + 1) * 512])
            s.dma("sync", QIF[:], qidx_d[0:1, qg * 512:(qg + 1) * 512].partition_broadcast(128))
            for kc in range(32):
                s.op("vector", lambda e, kc=kc: e.tensor_scalar(
                    MSKA[:, kc * 512:(kc + 1) * 512], QIF[:], KIX[:, kc:kc + 1], None, op0=ALU.is_ge))
            for h in range(16):
                hr = (h % 2) * 64
                hc = (h // 2)
                for kb in range(8):
                    for j in range(4):
                        kc = kb * 4 + j
                        s.op("tensor", lambda e, hr=hr, hc=hc, kc=kc, j=j: e.matmul(
                            ps_s[:, j * 512:(j + 1) * 512],
                            A_K[hr:hr + 64, hc * S + kc * 128: hc * S + kc * 128 + 128],
                            A_W[hr:hr + 64, hc * 512: hc * 512 + 512],
                            start=True, stop=True, skip_group_check=True))
                    s.op("scalar", lambda e: e.activation(A_P[:, 0:2048], ps_s[:], AF.Exp, scale=0.03125))
                    eng = "vector" if kb % 2 == 0 else "gpsimd"
                    s.op(eng, lambda e, kb=kb: e.tensor_tensor(
                        A_P[:, 0:2048], A_P[:, 0:2048], MSKA[:, kb * 2048:(kb + 1) * 2048], op=ALU.mult))
                    for j in range(4):
                        kc = kb * 4 + j
                        s.op("tensor", lambda e, kc=kc, h=h, j=j, kb=kb: e.matmul(
                            ps_o[:], A_V[:, kc * 1040 + 65 * h: kc * 1040 + 65 * h + 65],
                            A_P[:, j * 512:(j + 1) * 512],
                            start=(kb == 0 and j == 0), stop=(kb == 7 and j == 3), skip_group_check=True))
                s.op("vector", lambda e: e.reciprocal(RL[:], ps_o[64:65, :]))
                s.op("tensor", lambda e: e.matmul(ps_b[0:64, :], ONER[:, 0:64], RL[:], start=True, stop=True))
                s.op("scalar", lambda e: e.activation(RLB[:], ps_b[0:64, :], AF.Copy))
                s.op("vector", lambda e: e.tensor_tensor(CTXS[:], ps_o[0:64, :], RLB[:], op=ALU.mult))
                s.dma("sync", ctx_d[h * 64:(h + 1) * 64, qg * 512:(qg + 1) * 512], CTXS[:])

        load_w(wo_d)
        for tg in range(4):
            for ci in range(8):
                s.dma("sync", A_P[:, ci * 512:(ci + 1) * 512], ctx_d[ci * 128:(ci + 1) * 128, tg * 512:(tg + 1) * 512])
            for oc in range(8):
                for ci in range(8):
                    s.op("tensor", lambda e, oc=oc, ci=ci: e.matmul(
                        ps_s[:, 0:512], A_W[:, ci * 1024 + oc * 128: ci * 1024 + oc * 128 + 128],
                        A_P[:, ci * 512:(ci + 1) * 512], start=(ci == 0), stop=(ci == 7)))
                s.op("vector", lambda e, oc=oc: e.tensor_scalar(OA[:, oc * 512:(oc + 1) * 512], ps_s[:, 0:512],
                                                                BIA[:, 16 + oc:17 + oc], None, op0=ALU.add))
            for ci in range(8):
                s.dma("sync", XA[:, ci * 512:(ci + 1) * 512], xTo[ci * 128:(ci + 1) * 128, tg * 512:(tg + 1) * 512])
                s.op("vector", lambda e, ci=ci: e.tensor_tensor(
                    YA[:, ci * 512:(ci + 1) * 512], XA[:, ci * 512:(ci + 1) * 512], OA[:, ci * 512:(ci + 1) * 512], op=ALU.add))
                s.dma("sync", x3T_d[ci * 128:(ci + 1) * 128, tg * 512:(tg + 1) * 512], YA[:, ci * 512:(ci + 1) * 512])
                s.op("scalar", lambda e, ci=ci: e.activation(SQv, YA[:, ci * 512:(ci + 1) * 512], AF.Square))
                s.op("tensor", lambda e, ci=ci: e.matmul(
                    ps_s[0:1, 0:512], ONE1[:], YA[:, ci * 512:(ci + 1) * 512], start=(ci == 0), stop=(ci == 7), skip_group_check=True))
                s.op("tensor", lambda e, ci=ci: e.matmul(
                    ps_s[0:1, 512:1024], ONE1[:], SQv, start=(ci == 0), stop=(ci == 7), skip_group_check=True))
            s.op("vector", lambda e: e.tensor_scalar(R0[:], ps_s[0:1, 0:512], 1.0 / D, None, op0=ALU.mult))
            s.op("vector", lambda e: e.tensor_scalar(R1[:], ps_s[0:1, 512:1024], 1.0 / D, None, op0=ALU.mult))
            s.op("vector", lambda e: e.tensor_tensor(R2[:], R0[:], R0[:], op=ALU.mult))
            s.op("vector", lambda e: e.tensor_tensor(R2[:], R1[:], R2[:], op=ALU.subtract))
            s.op("vector", lambda e: e.tensor_scalar(R2[:], R2[:], 1e-5, None, op0=ALU.add))
            s.op("scalar", lambda e: e.activation(R2[:], R2[:], AF.Sqrt))
            s.op("vector", lambda e: e.reciprocal(R2[:], R2[:]))
            s.op("tensor", lambda e: e.matmul(ps_b[:], ONER[:], R0[:], start=True, stop=True))
            s.op("tensor", lambda e: e.matmul(ps_c[:], ONER[:], R2[:], start=True, stop=True))
            for ci in range(8):
                s.op("vector", lambda e, ci=ci: e.tensor_tensor(HFv, YA[:, ci * 512:(ci + 1) * 512], ps_b[:], op=ALU.subtract))
                s.op("vector", lambda e: e.tensor_tensor(HFv, HFv, ps_c[:], op=ALU.mult))
                s.dma("sync", z0T_d[ci * 128:(ci + 1) * 128, tg * 512:(tg + 1) * 512], HFv)

        s.build(blk)
    return nc


def run_l1_attn(d, x2):
    nc = build_l1_attn()
    x2T_all = t2d(x2)
    kidx = (np.arange(32)[None, :] * 128 + np.arange(128)[:, None]).astype(np.float32)
    in_maps = []
    for c in range(8):
        b, half = c // 2, c % 2
        own = slice(half * TOWN, (half + 1) * TOWN)
        in_maps.append({
            "xT": x2T_all[b],
            "xTo": np.ascontiguousarray(x2T_all[b][:, own]),
            "wq": t2d(d["L1_attn_Wq_w"]), "wk": t2d(d["L1_attn_Wk_w"]),
            "wv": t2d(d["L1_attn_Wv_w"]), "wo": t2d(d["L1_attn_Wo_w"]),
            "bq": bcol(d["L1_attn_Wq_b"]), "bk": bcol(d["L1_attn_Wk_b"]), "bo": bcol(d["L1_attn_Wo_b"]),
            "bvr": d["L1_attn_Wv_b"].reshape(1, D),
            "qidx": np.arange(half * TOWN, (half + 1) * TOWN, dtype=np.float32).reshape(1, TOWN),
            "kidx": kidx,
        })
    res = run_bass_kernel_spmd(nc, in_maps, core_ids=list(range(8)))
    x3 = np.empty((B, S, D), np.float32)
    z0 = np.empty((B, S, D), np.float32)
    for c in range(8):
        b, half = c // 2, c % 2
        x3[b, half * TOWN:(half + 1) * TOWN] = res.results[c]["x3T"].T
        z0[b, half * TOWN:(half + 1) * TOWN] = res.results[c]["z0T"].T
    return x3, z0



HALO = 384
NBUF = HALO + TOWN
NQ = TOWN + 128
DI = 2730
ACH = [128] * 21 + [42]
GW = [(0, 512), (512, 512), (1024, 512), (1536, 512), (2048, NBUF - 2048)]
GQ = [(0, 512), (512, 512), (1024, 512), (1536, 512), (2048, NQ - 2048)]


def build_l1f(last):
    nc = bass.Bass("TRN2", target_bir_lowering=False)
    zH_d = nc.dram_tensor("zH", [D, 1 + NBUF], F32, kind="ExternalInput").ap()
    wqkv_d = nc.dram_tensor("wqkv", [D, 3 * D], F32, kind="ExternalInput").ap()
    wo2_d = nc.dram_tensor("wo2", [D, D], F32, kind="ExternalInput").ap()
    w1_d = nc.dram_tensor("w1", [D, 2 * 2816], F32, kind="ExternalInput").ap()
    w2_d = nc.dram_tensor("w2", [2816, D], F32, kind="ExternalInput").ap()
    b1a_d = nc.dram_tensor("b1a", [128, 22], F32, kind="ExternalInput").ap()
    b1g_d = nc.dram_tensor("b1g", [128, 22], F32, kind="ExternalInput").ap()
    b2_d = nc.dram_tensor("b2c", [128, 8], F32, kind="ExternalInput").ap()
    lng_d = nc.dram_tensor("lng", [128, 8], F32, kind="ExternalInput").ap()
    lnb_d = nc.dram_tensor("lnb", [128, 8], F32, kind="ExternalInput").ap()
    qsc_d = nc.dram_tensor("qsc", [128, 1], F32, kind="ExternalInput").ap()
    ksc_d = nc.dram_tensor("ksc", [128, 1], F32, kind="ExternalInput").ap()
    csq_d = nc.dram_tensor("csq", [128, NQ], F32, kind="ExternalInput").ap()
    snq_d = nc.dram_tensor("snq", [128, NQ], F32, kind="ExternalInput").ap()
    csk_d = nc.dram_tensor("csk", [128, NBUF], F32, kind="ExternalInput").ap()
    snk_d = nc.dram_tensor("snk", [128, NBUF], F32, kind="ExternalInput").ap()
    rotm_d = nc.dram_tensor("rotm", [128, 128], BF16, kind="ExternalInput").ap()
    sel2_d = nc.dram_tensor("sel2", [128, 2], F32, kind="ExternalInput").ap()
    bb2_d = nc.dram_tensor("bb2", [2, 128], F32, kind="ExternalInput").ap()
    qidx_d = nc.dram_tensor("qidx", [1, NQ], F32, kind="ExternalInput").ap()
    kix_d = nc.dram_tensor("kix", [128, 19], F32, kind="ExternalInput").ap()
    kix2_d = nc.dram_tensor("kix2", [128, 19], F32, kind="ExternalInput").ap()
    if last:
        xres_d = nc.dram_tensor("xres", [D, TOWN], F32, kind="ExternalInput").ap()
    zo_d = nc.dram_tensor("zo", [D, TOWN], F32, kind="ExternalOutput").ap()
    ctx_d = nc.dram_tensor("ctxh", [D, NQ], BF16).ap()
    z2_d = nc.dram_tensor("z2h", [D, NQ], F32).ap()
    w1b_d = nc.dram_tensor("w1bh", [128, 8 * 5632], BF16).ap()
    w2b_d = nc.dram_tensor("w2bh", [128, 22 * D], BF16).ap()

    ex = contextlib.ExitStack()
    with ex:
        sem = ex.enter_context(nc.semaphore("ser"))
        SH = ex.enter_context(nc.sbuf_tensor("SH", [128, 36864], BF16))
        ZB = ex.enter_context(nc.sbuf_tensor("ZB", [128, 8 * NBUF], BF16))
        TB1 = ex.enter_context(nc.sbuf_tensor("TB1", [128, NBUF], F32))
        TB2 = ex.enter_context(nc.sbuf_tensor("TB2", [128, NBUF], F32))
        TB3 = ex.enter_context(nc.sbuf_tensor("TB3", [128, NQ], F32))
        TB4 = ex.enter_context(nc.sbuf_tensor("TB4", [128, NQ], F32))
        W2A = ex.enter_context(nc.sbuf_tensor("W2A", [128, 2816], BF16))
        CTXB = ex.enter_context(nc.sbuf_tensor("CTXB", [128, NQ], BF16))
        WOB = ex.enter_context(nc.sbuf_tensor("WOB", [128, 8192], BF16))
        MSKA = ex.enter_context(nc.sbuf_tensor("MSKA", [128, 17 * 384], BF16))
        QIF = ex.enter_context(nc.sbuf_tensor("QIF", [128, 128], F32))
        KIX = ex.enter_context(nc.sbuf_tensor("KIX", [128, 19], F32))
        KIX2 = ex.enter_context(nc.sbuf_tensor("KIX2", [128, 19], F32))
        PP = ex.enter_context(nc.sbuf_tensor("PP", [128, 384], BF16))
        N2 = ex.enter_context(nc.sbuf_tensor("N2", [2, NBUF], F32))
        RL = ex.enter_context(nc.sbuf_tensor("RL", [1, 128], F32))
        RLB = ex.enter_context(nc.sbuf_tensor("RLB", [64, 128], F32))
        CTXS = ex.enter_context(nc.sbuf_tensor("CTXS", [64, 128], BF16))
        ROTM = ex.enter_context(nc.sbuf_tensor("ROTM", [128, 128], BF16))
        SEL2 = ex.enter_context(nc.sbuf_tensor("SEL2", [128, 2], F32))
        BB2 = ex.enter_context(nc.sbuf_tensor("BB2", [2, 128], F32))
        QSC = ex.enter_context(nc.sbuf_tensor("QSC", [128, 2], F32))
        ONER = ex.enter_context(nc.sbuf_tensor("ONER", [1, 128], F32))
        ONE1 = ex.enter_context(nc.sbuf_tensor("ONE1", [128, 1], F32))
        BIA = ex.enter_context(nc.sbuf_tensor("BIA", [128, 68], F32))
        WSL = ex.enter_context(nc.sbuf_tensor("WSL", [128, 512], F32))
        W2S = ex.enter_context(nc.sbuf_tensor("W2S", [128, 128], BF16))
        R0 = ex.enter_context(nc.sbuf_tensor("R0", [1, 512], F32))
        R1 = ex.enter_context(nc.sbuf_tensor("R1", [1, 512], F32))
        R2 = ex.enter_context(nc.sbuf_tensor("R2", [1, 512], F32))
        psA = ex.enter_context(nc.psum_tensor("psA", [128, 512], F32))
        psB = ex.enter_context(nc.psum_tensor("psB", [128, 512], F32))
        psC = ex.enter_context(nc.psum_tensor("psC", [128, 512], F32))
        psD = ex.enter_context(nc.psum_tensor("psD", [65, 128], F32))
        psN = ex.enter_context(nc.psum_tensor("psN", [2, 512], F32))
        blk = ex.enter_context(nc.Block())

        QR = SH[:, 0:4864].bitcast(F32)
        QH = SH[:, 4864:9728].bitcast(F32)
        SQ2 = SH[:, 9728:14592].bitcast(F32)
        QHB = SH[:, 14592:17024]
        Q2 = SH[:, 17024:19200]
        K2 = SH[:, 19200:21632]
        VT = SH[:, 21632:24102]
        WQS = SH[:, 24102:27174]
        XF = SH[:, 0:8192].bitcast(F32)
        XN = SH[:, 8192:12288]
        H2T = SH[:, 12288:23552]
        WSLB = SH[:, 23552:27648]
        GEV = SH[:, 27648:28672].bitcast(F32)
        AEV = SH[:, 28672:29696].bitcast(F32)
        OUV = SH[:, 29696:30720].bitcast(F32)
        CXS = SH[:, 30720:34816]

        s = Ser(nc, sem)
        s.op("vector", lambda e: e.memset(ONER[:], 1.0))
        s.op("vector", lambda e: e.memset(ONE1[:], 1.0))
        s.dma("sync", ROTM[:], rotm_d)
        s.dma("sync", SEL2[:], sel2_d)
        s.dma("sync", BB2[:], bb2_d)
        s.dma("sync", QSC[:, 0:1], qsc_d)
        s.dma("sync", QSC[:, 1:2], ksc_d)
        s.dma("sync", KIX[:], kix_d)
        s.dma("sync", KIX2[:], kix2_d)
        s.dma("sync", BIA[:, 0:22], b1a_d)
        s.dma("sync", BIA[:, 22:44], b1g_d)
        s.dma("sync", BIA[:, 44:52], b2_d)
        s.dma("sync", BIA[:, 52:60], lng_d)
        s.dma("sync", BIA[:, 60:68], lnb_d)

        for ti in range(17):
            s.dma("sync", QIF[:], qidx_d[0:1, ti * 128:(ti + 1) * 128].partition_broadcast(128))
            for c3 in range(3):
                ck = ti + c3
                ms = MSKA[:, (ti * 3 + c3) * 128: (ti * 3 + c3) * 128 + 128]
                s.op("vector", lambda e, ms=ms, ck=ck: e.tensor_scalar(
                    ms, QIF[:], KIX[:, ck:ck + 1], None, op0=ALU.is_ge))
                s.op("vector", lambda e, ms=ms, ck=ck: e.scalar_tensor_tensor(
                    ms, QIF[:], KIX2[:, ck:ck + 1], ms, op0=ALU.is_le, op1=ALU.mult))

        W1G = [(g * 512, 512) for g in range(5)] + [(2560, 256)]
        for ci in range(8):
            for part in range(2):
                for (go, gw) in W1G:
                    s.dma("sync", WSL[:, 0:gw], w1_d[ci * 128:(ci + 1) * 128, part * 2816 + go: part * 2816 + go + gw])
                    s.op("vector", lambda e, gw=gw: e.tensor_copy(QHB[:, 0:gw], WSL[:, 0:gw]))
                    s.dma("sync", w1b_d[:, ci * 5632 + part * 2816 + go: ci * 5632 + part * 2816 + go + gw], QHB[:, 0:gw])
        for ri in range(22):
            for g2 in range(2):
                s.dma("sync", WSL[:, 0:512], w2_d[ri * 128:(ri + 1) * 128, g2 * 512:(g2 + 1) * 512])
                s.op("vector", lambda e: e.tensor_copy(QHB[:, 0:512], WSL[:, 0:512]))
                s.dma("sync", w2b_d[:, ri * 1024 + g2 * 512: ri * 1024 + g2 * 512 + 512], QHB[:, 0:512])

        for ci in range(8):
            off = 1 if ci < 4 else 0
            for g in range(5):
                w = 512 if g < 4 else NBUF - 2048
                s.dma("sync", WSL[:, 0:w], zH_d[ci * 128:(ci + 1) * 128, off + g * 512: off + g * 512 + w])
                s.op("vector", lambda e, ci=ci, g=g, w=w: e.tensor_copy(
                    ZB[:, ci * NBUF + g * 512: ci * NBUF + g * 512 + w], WSL[:, 0:w]))

        def rotary(kind):
            boff = 0
            groups = GQ if kind == "q" else GW
            scol = 0 if kind == "q" else 1
            out = Q2 if kind == "q" else K2
            n = NQ if kind == "q" else NBUF
            Ta = TB3 if kind == "q" else TB1
            Tb = TB4 if kind == "q" else TB2
            s.op("scalar", lambda e, n=n: e.activation(SQ2[:, 0:n], QR[:, 0:n], AF.Square))
            for (go, gw) in groups:
                s.op("tensor", lambda e, go=go, gw=gw: e.matmul(
                    psN[:, 0:gw], SEL2[:], SQ2[:, go:go + gw], start=True, stop=True))
                s.op("vector", lambda e, go=go, gw=gw: e.tensor_copy(N2[:, go:go + gw], psN[:, 0:gw]))
            s.op("scalar", lambda e, n=n: e.activation(N2[:, 0:n], N2[:, 0:n], AF.Sqrt))
            s.op("vector", lambda e, n=n: e.tensor_scalar(N2[:, 0:n], N2[:, 0:n], 1e-12, None, op0=ALU.max))
            s.op("vector", lambda e, n=n: e.reciprocal(N2[:, 0:n], N2[:, 0:n]))
            for (go, gw) in groups:
                s.op("tensor", lambda e, go=go, gw=gw: e.matmul(
                    psB[:, 0:gw], BB2[:], N2[:, go:go + gw], start=True, stop=True))
                s.op("vector", lambda e, go=go, gw=gw, scol=scol: e.scalar_tensor_tensor(
                    QH[:, go:go + gw], QR[:, go:go + gw], QSC[:, scol:scol + 1], psB[:, 0:gw],
                    op0=ALU.mult, op1=ALU.mult))
                s.op("vector", lambda e, go=go, gw=gw: e.tensor_copy(QHB[:, go:go + gw], QH[:, go:go + gw]))
                s.op("tensor", lambda e, go=go, gw=gw: e.matmul(
                    psC[:, 0:gw], ROTM[:], QHB[:, go:go + gw], start=True, stop=True))
                s.op("vector", lambda e, go=go, gw=gw, boff=boff, Ta=Ta: e.tensor_tensor(
                    QH[:, go:go + gw], QH[:, go:go + gw], Ta[:, boff + go: boff + go + gw], op=ALU.mult))
                s.op("vector", lambda e, go=go, gw=gw, boff=boff, Tb=Tb: e.tensor_tensor(
                    SQ2[:, go:go + gw], psC[:, 0:gw], Tb[:, boff + go: boff + go + gw], op=ALU.mult))
                s.op("vector", lambda e, go=go, gw=gw, out=out: e.tensor_tensor(
                    out[:, go:go + gw], QH[:, go:go + gw], SQ2[:, go:go + gw], op=ALU.add))

        s.dma("sync", TB1[:, 0:NBUF], csk_d)
        s.dma("sync", TB2[:, 0:NBUF], snk_d)
        s.dma("sync", TB3[:, 0:NQ], csq_d)
        s.dma("sync", TB4[:, 0:NQ], snq_d)
        for st in range(8):
            for part in range(3):
                for ci in range(8):
                    s.dma("sync", WSL[:, 0:128], wqkv_d[ci * 128:(ci + 1) * 128, part * D + st * 128: part * D + st * 128 + 128])
                    s.op("vector", lambda e, part=part, ci=ci: e.tensor_copy(
                        WQS[:, (part * 8 + ci) * 128:(part * 8 + ci) * 128 + 128], WSL[:, 0:128]))
            for tt in range(19):
                s.op("gpsimd", lambda e, tt=tt: e.memset(
                    VT[:, tt * 130:(tt + 1) * 130].rearrange("p (h d) -> p h d", d=65)[:, :, 64:65], 1.0))
                for ci in range(8):
                    s.op("tensor", lambda e, tt=tt, ci=ci: e.matmul(
                        psA[:, 0:128], ZB[:, ci * NBUF + tt * 128: ci * NBUF + tt * 128 + 128],
                        WQS[:, (2 * 8 + ci) * 128:(2 * 8 + ci) * 128 + 128],
                        start=(ci == 0), stop=(ci == 7)))
                s.op("vector", lambda e, tt=tt: e.tensor_copy(
                    VT[:, tt * 130:(tt + 1) * 130].rearrange("p (h d) -> p h d", d=65)[:, :, 0:64],
                    psA[:, 0:128].rearrange("p (h d) -> p h d", d=64)))
            for (go, gw) in GW:
                for ci in range(8):
                    s.op("tensor", lambda e, go=go, gw=gw, ci=ci: e.matmul(
                        psA[:, 0:gw], WQS[:, (1 * 8 + ci) * 128:(1 * 8 + ci) * 128 + 128],
                        ZB[:, ci * NBUF + go: ci * NBUF + go + gw],
                        start=(ci == 0), stop=(ci == 7)))
                s.op("scalar", lambda e, go=go, gw=gw: e.activation(QR[:, go:go + gw], psA[:, 0:gw], AF.Copy))
            rotary("k")
            for (go, gw) in GQ:
                for ci in range(8):
                    s.op("tensor", lambda e, go=go, gw=gw, ci=ci: e.matmul(
                        psA[:, 0:gw], WQS[:, (0 * 8 + ci) * 128:(0 * 8 + ci) * 128 + 128],
                        ZB[:, ci * NBUF + 256 + go: ci * NBUF + 256 + go + gw],
                        start=(ci == 0), stop=(ci == 7)))
                s.op("scalar", lambda e, go=go, gw=gw: e.activation(QR[:, go:go + gw], psA[:, 0:gw], AF.Copy))
            rotary("q")
            for ti in range(17):
                for h2 in range(2):
                    hr = h2 * 64
                    for c3 in range(3):
                        ck = ti + c3
                        s.op("tensor", lambda e, hr=hr, ck=ck, c3=c3, ti=ti: e.matmul(
                            psA[:, c3 * 128:(c3 + 1) * 128],
                            K2[hr:hr + 64, ck * 128: ck * 128 + 128],
                            Q2[hr:hr + 64, ti * 128: ti * 128 + 128],
                            start=True, stop=True, skip_group_check=True))
                    s.op("scalar", lambda e: e.activation(PP[:], psA[:, 0:384], AF.Exp, scale=8.0))
                    s.op("vector", lambda e, ti=ti: e.tensor_tensor(
                        PP[:], PP[:], MSKA[:, ti * 384:(ti + 1) * 384], op=ALU.mult))
                    for c3 in range(3):
                        ck = ti + c3
                        s.op("tensor", lambda e, ck=ck, h2=h2, c3=c3: e.matmul(
                            psD[:], VT[:, ck * 130 + h2 * 65: ck * 130 + h2 * 65 + 65],
                            PP[:, c3 * 128:(c3 + 1) * 128],
                            start=(c3 == 0), stop=(c3 == 2), skip_group_check=True))
                    s.op("vector", lambda e: e.tensor_scalar(RL[:], psD[64:65, :], 1e-20, None, op0=ALU.add))
                    s.op("vector", lambda e: e.reciprocal(RL[:], RL[:]))
                    s.op("tensor", lambda e: e.matmul(psB[0:64, 0:128], ONER[:, 0:64], RL[:], start=True, stop=True))
                    s.op("scalar", lambda e: e.activation(RLB[:], psB[0:64, 0:128], AF.Copy))
                    s.op("vector", lambda e, h2=h2, ti=ti: e.tensor_tensor(
                        CTXB[h2 * 64:(h2 + 1) * 64, ti * 128:(ti + 1) * 128], psD[0:64, :], RLB[:], op=ALU.mult))
            s.dma("sync", ctx_d[st * 128:(st + 1) * 128, :], CTXB[:])

        for ci in range(8):
            s.dma("sync", WSL[:, 0:512], wo2_d[ci * 128:(ci + 1) * 128, 0:512])
            s.op("vector", lambda e, ci=ci: e.tensor_copy(WOB[:, ci * 1024: ci * 1024 + 512], WSL[:, 0:512]))
            s.dma("sync", WSL[:, 0:512], wo2_d[ci * 128:(ci + 1) * 128, 512:1024])
            s.op("vector", lambda e, ci=ci: e.tensor_copy(WOB[:, ci * 1024 + 512: ci * 1024 + 1024], WSL[:, 0:512]))
        for (go, gw) in GQ:
            for ci in range(8):
                s.dma("sync", CXS[:, ci * 512: ci * 512 + gw], ctx_d[ci * 128:(ci + 1) * 128, go:go + gw])
            for oc in range(8):
                for ci in range(8):
                    s.op("tensor", lambda e, oc=oc, ci=ci, gw=gw: e.matmul(
                        psA[:, 0:gw], WOB[:, ci * 1024 + oc * 128: ci * 1024 + oc * 128 + 128],
                        CXS[:, ci * 512: ci * 512 + gw], start=(ci == 0), stop=(ci == 7)))
                s.dma("sync", WSL[:, 0:gw], zH_d[oc * 128:(oc + 1) * 128, 1 + 256 + go: 1 + 256 + go + gw])
                s.op("vector", lambda e, gw=gw: e.tensor_tensor(OUV[:, 0:gw], psA[:, 0:gw], WSL[:, 0:gw], op=ALU.add))
                s.dma("sync", z2_d[oc * 128:(oc + 1) * 128, go:go + gw], OUV[:, 0:gw])

        for tg in range(4):
            for ci in range(8):
                off = 128 if ci < 4 else 127
                s.dma("sync", XF[:, ci * 512:(ci + 1) * 512], z2_d[ci * 128:(ci + 1) * 128, off + tg * 512: off + tg * 512 + 512])
            for ci in range(8):
                s.op("scalar", lambda e, ci=ci: e.activation(GEV[:], XF[:, ci * 512:(ci + 1) * 512], AF.Square))
                s.op("tensor", lambda e, ci=ci: e.matmul(
                    psB[0:1, 0:512], ONE1[:], XF[:, ci * 512:(ci + 1) * 512], start=(ci == 0), stop=(ci == 7), skip_group_check=True))
                s.op("tensor", lambda e, ci=ci: e.matmul(
                    psC[0:1, 0:512], ONE1[:], GEV[:], start=(ci == 0), stop=(ci == 7), skip_group_check=True))
            s.op("vector", lambda e: e.tensor_scalar(R0[:], psB[0:1, 0:512], 1.0 / D, None, op0=ALU.mult))
            s.op("vector", lambda e: e.tensor_scalar(R1[:], psC[0:1, 0:512], 1.0 / D, None, op0=ALU.mult))
            s.op("vector", lambda e: e.tensor_tensor(R2[:], R0[:], R0[:], op=ALU.mult))
            s.op("vector", lambda e: e.tensor_tensor(R2[:], R1[:], R2[:], op=ALU.subtract))
            s.op("vector", lambda e: e.tensor_scalar(R2[:], R2[:], 1e-5, None, op0=ALU.add))
            s.op("scalar", lambda e: e.activation(R2[:], R2[:], AF.Sqrt))
            s.op("vector", lambda e: e.reciprocal(R2[:], R2[:]))
            s.op("tensor", lambda e: e.matmul(psB[:], ONER[:], R0[:], start=True, stop=True))
            s.op("tensor", lambda e: e.matmul(psC[:], ONER[:], R2[:], start=True, stop=True))
            for ci in range(8):
                s.op("vector", lambda e, ci=ci: e.tensor_tensor(GEV[:], XF[:, ci * 512:(ci + 1) * 512], psB[:], op=ALU.subtract))
                s.op("vector", lambda e: e.tensor_tensor(GEV[:], GEV[:], psC[:], op=ALU.mult))
                s.op("vector", lambda e, ci=ci: e.tensor_scalar(
                    XN[:, ci * 512:(ci + 1) * 512], GEV[:], BIA[:, 52 + ci:53 + ci], BIA[:, 60 + ci:61 + ci], op0=ALU.mult, op1=ALU.add))
            w1r = w1b_d.rearrange("p (c w) -> p c w", w=5632)
            for ac in range(22):
                s.dma("sync", WSLB[:, 0:1024].rearrange("p (c w) -> p c w", w=128),
                      w1r[:, :, ac * 128: ac * 128 + 128])
                s.dma("sync", WSLB[:, 1024:2048].rearrange("p (c w) -> p c w", w=128),
                      w1r[:, :, 2816 + ac * 128: 2816 + ac * 128 + 128])
                for ci in range(8):
                    s.op("tensor", lambda e, ci=ci: e.matmul(
                        psA[:], WSLB[:, ci * 128: ci * 128 + 128],
                        XN[:, ci * 512:(ci + 1) * 512], start=(ci == 0), stop=(ci == 7), skip_group_check=True))
                for ci in range(8):
                    s.op("tensor", lambda e, ci=ci: e.matmul(
                        psB[:], WSLB[:, 1024 + ci * 128: 1024 + ci * 128 + 128],
                        XN[:, ci * 512:(ci + 1) * 512], start=(ci == 0), stop=(ci == 7), skip_group_check=True))
                s.op("vector", lambda e, ac=ac: e.tensor_scalar(
                    AEV[:], psA[:], BIA[:, ac:ac + 1], None, op0=ALU.add))
                s.op("scalar", lambda e, ac=ac: e.activation(
                    GEV[:], psB[:], AF.Gelu, bias=BIA[:, 22 + ac:23 + ac]))
                s.op("vector", lambda e, ac=ac: e.tensor_tensor(
                    H2T[:, ac * 512:(ac + 1) * 512], AEV[:], GEV[:], op=ALU.mult))
            w2r = w2b_d.rearrange("p (r w) -> p r w", w=1024)
            for oc in range(8):
                s.dma("sync", W2A[:].rearrange("p (r w) -> p r w", w=128),
                      w2r[:, :, oc * 128:(oc + 1) * 128])
                for ri in range(22):
                    s.op("tensor", lambda e, ri=ri: e.matmul(
                        psA[:], W2A[:, ri * 128:(ri + 1) * 128], H2T[:, ri * 512:(ri + 1) * 512],
                        start=(ri == 0), stop=(ri == 21), skip_group_check=True))
                s.op("vector", lambda e, oc=oc: e.tensor_scalar(OUV[:], psA[:], BIA[:, 44 + oc:45 + oc], None, op0=ALU.add))
                s.dma("sync", WSL[:], z2_d[oc * 128:(oc + 1) * 128, 128 + tg * 512: 128 + tg * 512 + 512])
                s.op("vector", lambda e: e.tensor_tensor(OUV[:], OUV[:], WSL[:], op=ALU.add))
                if last:
                    s.dma("sync", WSL[:], xres_d[oc * 128:(oc + 1) * 128, tg * 512:(tg + 1) * 512])
                    s.op("vector", lambda e: e.tensor_tensor(OUV[:], OUV[:], WSL[:], op=ALU.add))
                s.dma("sync", zo_d[oc * 128:(oc + 1) * 128, tg * 512:(tg + 1) * 512], OUV[:])

        s.build(blk)
    return nc


def rot_tables_host(own0):
    half = np.arange(0, DH, 2, dtype=np.float32)
    inv_freq = (1.0 / 10000.0 ** (half / DH)).astype(np.float32)
    base_scale = ((half + 0.4 * DH) / (1.4 * DH)).astype(np.float32)

    def tabs(pos):
        fr = pos[:, None] * inv_freq[None, :]
        fr64 = np.concatenate([fr, fr], 1)
        sc = base_scale[None, :] ** ((pos - S // 2) / 128.0)[:, None]
        sc64 = np.concatenate([sc, sc], 1).astype(np.float32)
        cos, sin = np.cos(fr64), np.sin(fr64)
        csq = (cos * sc64).T
        snq = (sin * sc64).T
        csk = (cos / sc64).T
        snk = (sin / sc64).T
        stack = lambda a: np.ascontiguousarray(np.concatenate([a, a], 0), dtype=np.float32)
        return stack(csq), stack(snq), stack(csk), stack(snk)
    pk = (own0 - HALO + np.arange(NBUF)).astype(np.float32)
    pq = (own0 - 128 + np.arange(NQ)).astype(np.float32)
    csq, snq, _, _ = tabs(pq)
    _, _, csk, snk = tabs(pk)
    return csq, snq, csk, snk


def bchunk(v, widths):
    out = np.zeros((128, len(widths)), np.float32)
    o = 0
    for i, w in enumerate(widths):
        out[0:w, i] = v[o:o + w]
        o += w
    return out


def l1f_inmaps(d, i, z, last, x3=None):
    rotm = np.zeros((128, 128), np.float32)
    for dd in range(128):
        m = dd % 64
        if m < 32:
            rotm[dd, (dd // 64) * 64 + m + 32] = -1.0
        else:
            rotm[dd, (dd // 64) * 64 + m - 32] = 1.0
    rotmT = np.ascontiguousarray(rotm.T).astype(BF16NP)
    sel2 = np.zeros((128, 2), np.float32); sel2[0:64, 0] = 1; sel2[64:128, 1] = 1
    bb2 = np.zeros((2, 128), np.float32); bb2[0, 0:64] = 1; bb2[1, 64:128] = 1
    zT = t2d(z)
    maps = []
    for c in range(8):
        b, halfc = c // 2, c % 2
        own0 = halfc * TOWN
        zH = np.zeros((D, 1 + NBUF), np.float32)
        lo = own0 - HALO - 1
        src_lo = max(0, lo)
        zH[:, 1 + NBUF - (own0 + TOWN - src_lo) - 0 - 0:] = 0
        seg = zT[b][:, src_lo: own0 + TOWN]
        zH[:, 1 + NBUF - seg.shape[1]:] = seg
        csq, snq, csk, snk = rot_tables_host(own0)
        qidx = (own0 - 128 + np.arange(NQ)).astype(np.float32)
        qidx[qidx < 0] = -1e6
        kbuf = (own0 - HALO + np.arange(NBUF)).astype(np.float32)
        kbuf[kbuf < 0] = -1e6
        kix = np.zeros((128, 19), np.float32)
        for ck in range(19):
            kix[:, ck] = kbuf[ck * 128:(ck + 1) * 128]
        kix2 = kix + 256.0
        mp = {
            "zH": zH, "wqkv": d["L1_ffn_Wqkv"][i], "wo2": d["L1_ffn_Wo"][i],
            "w1": np.ascontiguousarray(np.concatenate(
                [np.pad(d["L1_ffn_W1"][i][:, :DI], ((0, 0), (0, 86))),
                 np.pad(d["L1_ffn_W1"][i][:, DI:], ((0, 0), (0, 86)))], axis=1)),
            "w2": np.pad(d["L1_ffn_W2"][i], ((0, 86), (0, 0))),
            "b1a": bchunk(np.pad(d["L1_ffn_b1"][i][:DI], (0, 86)), [128] * 22),
            "b1g": bchunk(np.pad(d["L1_ffn_b1"][i][DI:], (0, 86)), [128] * 22),
            "b2c": bcol(d["L1_ffn_b2"][i]), "lng": bcol(d["L1_ffn_ln_g"][i]), "lnb": bcol(d["L1_ffn_ln_b"][i]),
            "qsc": np.tile(d["L1_ffn_q_scale"][i], 2).reshape(128, 1).astype(np.float32),
            "ksc": np.tile(d["L1_ffn_k_scale"][i], 2).reshape(128, 1).astype(np.float32),
            "csq": csq[:, 0:NQ], "snq": snq[:, 0:NQ], "csk": csk[:, 0:NBUF], "snk": snk[:, 0:NBUF],
            "rotm": rotmT, "sel2": sel2, "bb2": bb2,
            "qidx": qidx.reshape(1, NQ), "kix": kix, "kix2": kix2,
        }
        if last:
            mp["xres"] = np.ascontiguousarray(t2d(x3[b])[:, own0:own0 + TOWN])
        maps.append(mp)
    return maps


def run_l1f(nc, maps):
    res = run_bass_kernel_spmd(nc, maps, core_ids=list(range(8)))
    z = np.empty((B, S, D), np.float32)
    for c in range(8):
        b, halfc = c // 2, c % 2
        z[b, halfc * TOWN:(halfc + 1) * TOWN] = res.results[c]["zo"].T
    return z




def kernel(**inputs):
    d = {k: np.asarray(v, dtype=np.float32) for k, v in inputs.items()}
    x1 = run_l0_attn(d)
    x2 = run_l0_ffn(d, x1)
    x3, z = run_l1_attn(d, x2)
    nc_mid = build_l1f(False)
    for i in range(5):
        z = run_l1f(nc_mid, l1f_inmaps(d, i, z, False))
    nc_last = build_l1f(True)
    x4 = run_l1f(nc_last, l1f_inmaps(d, 5, z, True, x3))
    return x4




# version 07 (final)
# speedup vs baseline: 1.7569x; 1.1855x over previous
import numpy as np
import ml_dtypes
import contextlib
import concourse.bass as bass
import concourse.mybir as mybir
from concourse.bass_utils import run_bass_kernel_spmd

F32 = mybir.dt.float32
BF16 = mybir.dt.bfloat16
AF = mybir.ActivationFunctionType
ALU = mybir.AluOpType
AX = mybir.AxisListType
BF16NP = ml_dtypes.bfloat16

ENGINES = ["sync", "scalar", "vector", "tensor", "gpsimd"]

B, S, D, H, DH = 4, 4096, 1024, 16, 64
TOWN = S // 2


class Ser:

    def __init__(self, nc, sem):
        self.nc = nc
        self.sem = sem
        self.ops = []

    def op(self, engine, fn, inc=1):
        self.ops.append((engine, fn, inc))

    def dma(self, engine, out, in_):
        self.op(engine, lambda e: e.dma_start(out, in_), 16)

    RELAX = ("tensor", "sync")

    def build(self, block):
        ops, sem = self.ops, self.sem
        waits = []
        cum = 0
        prev_e = None
        run_wait = 0
        for (e, fn, inc) in ops:
            if e in self.RELAX and e == prev_e:
                waits.append(run_wait)
            else:
                waits.append(cum)
                run_wait = cum
            prev_e = e
            cum += inc
        total = cum

        def mk(ename):
            def body(eng):
                cum = 0
                for idx, (e, fn, inc) in enumerate(ops):
                    if e == ename:
                        eng.wait_ge(sem, waits[idx])
                        fn(eng).then_inc(sem, inc)
                    cum += inc
                eng.wait_ge(sem, total)
            return body

        for ename in ENGINES:
            getattr(block, ename)(mk(ename))


def t2d(x):
    return np.ascontiguousarray(np.swapaxes(x, -1, -2))


def bcol(v):
    return np.ascontiguousarray(v.reshape(8, 128).T)


def pe_table():
    pos = np.arange(S, dtype=np.float32)[:, None]
    div = np.exp(np.arange(0, D, 2, dtype=np.float32) * (-np.log(10000.0) / D))
    pe = np.zeros((S, D), np.float32)
    pe[:, 0::2] = np.sin(pos * div)
    pe[:, 1::2] = np.cos(pos * div)
    return pe



def build_l0_attn():
    nc = bass.Bass("TRN2", target_bir_lowering=False)
    xT = nc.dram_tensor("xT", [D, S], F32, kind="ExternalInput").ap()
    xTo = nc.dram_tensor("xTo", [D, TOWN], F32, kind="ExternalInput").ap()
    peT_d = nc.dram_tensor("peT", [D, S], F32, kind="ExternalInput").ap()
    peTo_d = nc.dram_tensor("peTo", [D, TOWN], F32, kind="ExternalInput").ap()
    wq_d = nc.dram_tensor("wq", [D, D], F32, kind="ExternalInput").ap()
    wk_d = nc.dram_tensor("wk", [D, D], F32, kind="ExternalInput").ap()
    wv_d = nc.dram_tensor("wv", [D, D], F32, kind="ExternalInput").ap()
    wo_d = nc.dram_tensor("wo", [D, D], F32, kind="ExternalInput").ap()
    bq_d = nc.dram_tensor("bq", [128, 8], F32, kind="ExternalInput").ap()
    bk_d = nc.dram_tensor("bk", [128, 8], F32, kind="ExternalInput").ap()
    bo_d = nc.dram_tensor("bo", [128, 8], F32, kind="ExternalInput").ap()
    gam_d = nc.dram_tensor("gam", [128, 8], F32, kind="ExternalInput").ap()
    bet_d = nc.dram_tensor("bet", [128, 8], F32, kind="ExternalInput").ap()
    bvr_d = nc.dram_tensor("bvr", [1, D], F32, kind="ExternalInput").ap()
    x1T_d = nc.dram_tensor("x1T", [D, TOWN], F32, kind="ExternalOutput").ap()
    ctx_d = nc.dram_tensor("ctxh", [D, TOWN], BF16).ap()
    qT_d = nc.dram_tensor("qTh", [D, TOWN], BF16).ap()
    st_d = nc.dram_tensor("sth", [4, S], F32).ap()

    ex = contextlib.ExitStack()
    with ex:
        sem = ex.enter_context(nc.semaphore("ser"))
        A_K = ex.enter_context(nc.sbuf_tensor("A_K", [128, 8 * S], BF16))
        A_V = ex.enter_context(nc.sbuf_tensor("A_V", [128, 32 * 1040], BF16))
        A_W = ex.enter_context(nc.sbuf_tensor("A_W", [128, 8192], BF16))
        A_P = ex.enter_context(nc.sbuf_tensor("A_P", [128, 4096], BF16))
        WL = ex.enter_context(nc.sbuf_tensor("WL", [128, 1024], F32))
        XS = ex.enter_context(nc.sbuf_tensor("XS", [128, 512], F32))
        SQ = ex.enter_context(nc.sbuf_tensor("SQ", [128, 512], F32))
        HF = ex.enter_context(nc.sbuf_tensor("HF", [128, 512], F32))
        PEs = ex.enter_context(nc.sbuf_tensor("PEs", [128, 512], F32))
        HBt = ex.enter_context(nc.sbuf_tensor("HBt", [128, 4096], BF16))
        QS = ex.enter_context(nc.sbuf_tensor("QS", [128, 512], BF16))
        R0 = ex.enter_context(nc.sbuf_tensor("R0", [1, 512], F32))
        R1 = ex.enter_context(nc.sbuf_tensor("R1", [1, 512], F32))
        R2 = ex.enter_context(nc.sbuf_tensor("R2", [1, 512], F32))
        RL = ex.enter_context(nc.sbuf_tensor("RL", [1, 512], F32))
        RLB = ex.enter_context(nc.sbuf_tensor("RLB", [64, 512], F32))
        CTXS = ex.enter_context(nc.sbuf_tensor("CTXS", [64, 512], BF16))
        ONE1 = ex.enter_context(nc.sbuf_tensor("ONE1", [128, 1], F32))
        ONER = ex.enter_context(nc.sbuf_tensor("ONER", [1, 128], F32))
        BIA = ex.enter_context(nc.sbuf_tensor("BIA", [128, 40], F32))
        BVRB = ex.enter_context(nc.sbuf_tensor("BVRB", [128, D], F32))
        VS = ex.enter_context(nc.sbuf_tensor("VS", [128, 512], F32))
        ps_s = ex.enter_context(nc.psum_tensor("ps_s", [128, 2048], F32))
        ps_o = ex.enter_context(nc.psum_tensor("ps_o", [65, 512], F32))
        ps_b = ex.enter_context(nc.psum_tensor("ps_b", [128, 512], F32))
        ps_c = ex.enter_context(nc.psum_tensor("ps_c", [128, 512], F32))
        blk = ex.enter_context(nc.Block())

        OA = A_V[:, 0:4096]
        YA = A_V[:, 8192:16384].bitcast(F32)
        XA = A_V[:, 16384:24576].bitcast(F32)

        s = Ser(nc, sem)
        s.op("vector", lambda e: e.memset(ONE1[:], 1.0))
        s.op("vector", lambda e: e.memset(ONER[:], 1.0))
        s.dma("sync", BIA[:, 0:8], bq_d)
        s.dma("sync", BIA[:, 8:16], bk_d)
        s.dma("sync", BIA[:, 16:24], bo_d)
        s.dma("sync", BIA[:, 24:32], gam_d)
        s.dma("sync", BIA[:, 32:40], bet_d)
        s.dma("sync", BVRB[:], bvr_d.partition_broadcast(128))
        for tt in range(32):
            s.op("gpsimd", lambda e, tt=tt: e.memset(
                A_V[:, tt * 1040:(tt + 1) * 1040].rearrange("p (h d) -> p h d", d=65)[:, :, 64:65], 1.0))

        def stats_sweep(src, ntg, mu_row, rstd_row):
            for tg in range(ntg):
                for ci in range(8):
                    s.dma("sync", XS[:], src[ci * 128:(ci + 1) * 128, tg * 512:(tg + 1) * 512])
                    s.op("scalar", lambda e: e.activation(SQ[:], XS[:], AF.Square))
                    s.op("tensor", lambda e, ci=ci: e.matmul(
                        ps_s[0:1, 0:512], ONE1[:], XS[:], start=(ci == 0), stop=(ci == 7), skip_group_check=True))
                    s.op("tensor", lambda e, ci=ci: e.matmul(
                        ps_s[0:1, 512:1024], ONE1[:], SQ[:], start=(ci == 0), stop=(ci == 7), skip_group_check=True))
                s.op("vector", lambda e: e.tensor_scalar(R0[:], ps_s[0:1, 0:512], 1.0 / D, None, op0=ALU.mult))
                s.op("vector", lambda e: e.tensor_scalar(R1[:], ps_s[0:1, 512:1024], 1.0 / D, None, op0=ALU.mult))
                s.op("vector", lambda e: e.tensor_tensor(R2[:], R0[:], R0[:], op=ALU.mult))
                s.op("vector", lambda e: e.tensor_tensor(R2[:], R1[:], R2[:], op=ALU.subtract))
                s.op("vector", lambda e: e.tensor_scalar(R2[:], R2[:], 1e-5, None, op0=ALU.add))
                s.op("scalar", lambda e: e.activation(R2[:], R2[:], AF.Sqrt))
                s.op("vector", lambda e: e.reciprocal(R2[:], R2[:]))
                s.dma("sync", st_d[mu_row:mu_row + 1, tg * 512:(tg + 1) * 512], R0[:])
                s.dma("sync", st_d[rstd_row:rstd_row + 1, tg * 512:(tg + 1) * 512], R2[:])

        def load_w(w_d):
            for ci in range(8):
                s.dma("sync", WL[:], w_d[ci * 128:(ci + 1) * 128, :])
                s.op("vector", lambda e, ci=ci: e.tensor_copy(A_W[:, ci * 1024:(ci + 1) * 1024], WL[:]))

        def make_h(src, pe_src, tg, mu_row, rstd_row, add_pe=True):
            s.dma("sync", R0[:], st_d[mu_row:mu_row + 1, tg * 512:(tg + 1) * 512])
            s.dma("sync", R2[:], st_d[rstd_row:rstd_row + 1, tg * 512:(tg + 1) * 512])
            s.op("tensor", lambda e: e.matmul(ps_b[:], ONER[:], R0[:], start=True, stop=True))
            s.op("tensor", lambda e: e.matmul(ps_c[:], ONER[:], R2[:], start=True, stop=True))
            for ci in range(8):
                s.dma("sync", XS[:], src[ci * 128:(ci + 1) * 128, tg * 512:(tg + 1) * 512])
                s.op("vector", lambda e: e.tensor_tensor(HF[:], XS[:], ps_b[:], op=ALU.subtract))
                if add_pe:
                    s.dma("sync", PEs[:], pe_src[ci * 128:(ci + 1) * 128, tg * 512:(tg + 1) * 512])
                    s.op("vector", lambda e: e.tensor_tensor(HF[:], HF[:], ps_c[:], op=ALU.mult))
                    s.op("vector", lambda e, ci=ci: e.tensor_tensor(HBt[:, ci * 512:(ci + 1) * 512], HF[:], PEs[:], op=ALU.add))
                else:
                    s.op("vector", lambda e, ci=ci: e.tensor_tensor(HBt[:, ci * 512:(ci + 1) * 512], HF[:], ps_c[:], op=ALU.mult))

        stats_sweep(xT, 8, 0, 1)
        stats_sweep(xTo, 4, 2, 3)

        load_w(wk_d)
        for tg in range(8):
            make_h(xT, peT_d, tg, 0, 1)
            for oc in range(8):
                for ci in range(8):
                    s.op("tensor", lambda e, oc=oc, ci=ci: e.matmul(
                        ps_s[:, 0:512], A_W[:, ci * 1024 + oc * 128: ci * 1024 + oc * 128 + 128],
                        HBt[:, ci * 512:(ci + 1) * 512], start=(ci == 0), stop=(ci == 7)))
                s.op("vector", lambda e, oc=oc, tg=tg: e.tensor_scalar(
                    A_K[:, oc * S + tg * 512: oc * S + tg * 512 + 512], ps_s[:, 0:512],
                    BIA[:, 8 + oc:9 + oc], None, op0=ALU.add))

        load_w(wv_d)
        for tg in range(8):
            make_h(xT, peT_d, tg, 0, 1)
            for tt4 in range(4):
                tt = tg * 4 + tt4
                for nh in range(2):
                    for ci in range(8):
                        s.op("tensor", lambda e, tt4=tt4, nh=nh, ci=ci: e.matmul(
                            ps_b[:], HBt[:, ci * 512 + tt4 * 128: ci * 512 + tt4 * 128 + 128],
                            A_W[:, ci * 1024 + nh * 512: ci * 1024 + nh * 512 + 512],
                            start=(ci == 0), stop=(ci == 7)))
                    s.op("scalar", lambda e: e.activation(VS[:], ps_b[:], AF.Copy))
                    s.op("vector", lambda e, tt=tt, nh=nh: e.tensor_tensor(
                        A_V[:, tt * 1040 + nh * 520: tt * 1040 + nh * 520 + 520].rearrange("p (h d) -> p h d", d=65)[:, :, 0:64],
                        VS[:].rearrange("p (h d) -> p h d", d=64),
                        BVRB[:, nh * 512:(nh + 1) * 512].rearrange("p (h d) -> p h d", d=64), op=ALU.add))

        load_w(wq_d)
        for tg in range(4):
            make_h(xTo, peTo_d, tg, 2, 3)
            for oc in range(8):
                for ci in range(8):
                    s.op("tensor", lambda e, oc=oc, ci=ci: e.matmul(
                        ps_s[:, 0:512], A_W[:, ci * 1024 + oc * 128: ci * 1024 + oc * 128 + 128],
                        HBt[:, ci * 512:(ci + 1) * 512], start=(ci == 0), stop=(ci == 7)))
                s.op("vector", lambda e, oc=oc: e.tensor_scalar(QS[:], ps_s[:, 0:512], BIA[:, oc:oc + 1], None, op0=ALU.add))
                s.dma("sync", qT_d[oc * 128:(oc + 1) * 128, tg * 512:(tg + 1) * 512], QS[:])

        for qg in range(4):
            for ci in range(8):
                s.dma("sync", A_W[:, ci * 512:(ci + 1) * 512], qT_d[ci * 128:(ci + 1) * 128, qg * 512:(qg + 1) * 512])
            for h in range(16):
                hr = (h % 2) * 64
                hc = (h // 2)
                for kb in range(8):
                    for j in range(4):
                        kc = kb * 4 + j
                        s.op("tensor", lambda e, hr=hr, hc=hc, kc=kc, j=j: e.matmul(
                            ps_s[:, j * 512:(j + 1) * 512],
                            A_K[hr:hr + 64, hc * S + kc * 128: hc * S + kc * 128 + 128],
                            A_W[hr:hr + 64, hc * 512: hc * 512 + 512],
                            start=True, stop=True, skip_group_check=True))
                    s.op("scalar", lambda e: e.activation(A_P[:, 0:2048], ps_s[:], AF.Exp, scale=0.125))
                    for j in range(4):
                        kc = kb * 4 + j
                        s.op("tensor", lambda e, kc=kc, h=h, j=j, kb=kb: e.matmul(
                            ps_o[:], A_V[:, kc * 1040 + 65 * h: kc * 1040 + 65 * h + 65],
                            A_P[:, j * 512:(j + 1) * 512],
                            start=(kb == 0 and j == 0), stop=(kb == 7 and j == 3), skip_group_check=True))
                s.op("vector", lambda e: e.reciprocal(RL[:], ps_o[64:65, :]))
                s.op("tensor", lambda e: e.matmul(ps_b[0:64, :], ONER[:, 0:64], RL[:], start=True, stop=True))
                s.op("scalar", lambda e: e.activation(RLB[:], ps_b[0:64, :], AF.Copy))
                s.op("vector", lambda e: e.tensor_tensor(CTXS[:], ps_o[0:64, :], RLB[:], op=ALU.mult))
                s.dma("sync", ctx_d[h * 64:(h + 1) * 64, qg * 512:(qg + 1) * 512], CTXS[:])

        load_w(wo_d)
        for tg in range(4):
            for ci in range(8):
                s.dma("sync", A_P[:, ci * 512:(ci + 1) * 512], ctx_d[ci * 128:(ci + 1) * 128, tg * 512:(tg + 1) * 512])
            for oc in range(8):
                for ci in range(8):
                    s.op("tensor", lambda e, oc=oc, ci=ci: e.matmul(
                        ps_s[:, 0:512], A_W[:, ci * 1024 + oc * 128: ci * 1024 + oc * 128 + 128],
                        A_P[:, ci * 512:(ci + 1) * 512], start=(ci == 0), stop=(ci == 7)))
                s.op("vector", lambda e, oc=oc: e.tensor_scalar(OA[:, oc * 512:(oc + 1) * 512], ps_s[:, 0:512],
                                                                BIA[:, 16 + oc:17 + oc], None, op0=ALU.add))
            s.dma("sync", R0[:], st_d[2:3, tg * 512:(tg + 1) * 512])
            s.dma("sync", R2[:], st_d[3:4, tg * 512:(tg + 1) * 512])
            s.op("tensor", lambda e: e.matmul(ps_b[:], ONER[:], R0[:], start=True, stop=True))
            s.op("tensor", lambda e: e.matmul(ps_c[:], ONER[:], R2[:], start=True, stop=True))
            for ci in range(8):
                s.dma("sync", XA[:, ci * 512:(ci + 1) * 512], xTo[ci * 128:(ci + 1) * 128, tg * 512:(tg + 1) * 512])
                s.dma("sync", PEs[:], peTo_d[ci * 128:(ci + 1) * 128, tg * 512:(tg + 1) * 512])
                s.op("vector", lambda e, ci=ci: e.tensor_tensor(HF[:], XA[:, ci * 512:(ci + 1) * 512], ps_b[:], op=ALU.subtract))
                s.op("vector", lambda e: e.tensor_tensor(HF[:], HF[:], ps_c[:], op=ALU.mult))
                s.op("vector", lambda e: e.tensor_tensor(HF[:], HF[:], PEs[:], op=ALU.add))
                s.op("vector", lambda e, ci=ci: e.tensor_tensor(
                    YA[:, ci * 512:(ci + 1) * 512], HF[:], OA[:, ci * 512:(ci + 1) * 512], op=ALU.add))
                s.op("scalar", lambda e, ci=ci: e.activation(SQ[:], YA[:, ci * 512:(ci + 1) * 512], AF.Square))
                s.op("tensor", lambda e, ci=ci: e.matmul(
                    ps_s[0:1, 0:512], ONE1[:], YA[:, ci * 512:(ci + 1) * 512], start=(ci == 0), stop=(ci == 7), skip_group_check=True))
                s.op("tensor", lambda e, ci=ci: e.matmul(
                    ps_s[0:1, 512:1024], ONE1[:], SQ[:], start=(ci == 0), stop=(ci == 7), skip_group_check=True))
            s.op("vector", lambda e: e.tensor_scalar(R0[:], ps_s[0:1, 0:512], 1.0 / D, None, op0=ALU.mult))
            s.op("vector", lambda e: e.tensor_scalar(R1[:], ps_s[0:1, 512:1024], 1.0 / D, None, op0=ALU.mult))
            s.op("vector", lambda e: e.tensor_tensor(R2[:], R0[:], R0[:], op=ALU.mult))
            s.op("vector", lambda e: e.tensor_tensor(R2[:], R1[:], R2[:], op=ALU.subtract))
            s.op("vector", lambda e: e.tensor_scalar(R2[:], R2[:], 1e-5, None, op0=ALU.add))
            s.op("scalar", lambda e: e.activation(R2[:], R2[:], AF.Sqrt))
            s.op("vector", lambda e: e.reciprocal(R2[:], R2[:]))
            s.op("tensor", lambda e: e.matmul(ps_b[:], ONER[:], R0[:], start=True, stop=True))
            s.op("tensor", lambda e: e.matmul(ps_c[:], ONER[:], R2[:], start=True, stop=True))
            for ci in range(8):
                s.op("vector", lambda e, ci=ci: e.tensor_tensor(HF[:], YA[:, ci * 512:(ci + 1) * 512], ps_b[:], op=ALU.subtract))
                s.op("vector", lambda e: e.tensor_tensor(HF[:], HF[:], ps_c[:], op=ALU.mult))
                s.op("vector", lambda e, ci=ci: e.tensor_scalar(HF[:], HF[:], BIA[:, 24 + ci:25 + ci], BIA[:, 32 + ci:33 + ci], op0=ALU.mult, op1=ALU.add))
                s.op("vector", lambda e, ci=ci: e.tensor_tensor(HF[:], HF[:], XA[:, ci * 512:(ci + 1) * 512], op=ALU.add))
                s.dma("sync", x1T_d[ci * 128:(ci + 1) * 128, tg * 512:(tg + 1) * 512], HF[:])

        s.build(blk)
    return nc


def run_l0_attn(d):
    nc = build_l0_attn()
    pe = pe_table()
    peT = t2d(pe)
    xT_all = t2d(d["L0_attn_x"])
    in_maps = []
    for c in range(8):
        b, half = c // 2, c % 2
        own = slice(half * TOWN, (half + 1) * TOWN)
        in_maps.append({
            "xT": xT_all[b],
            "xTo": np.ascontiguousarray(xT_all[b][:, own]),
            "peT": peT,
            "peTo": np.ascontiguousarray(peT[:, own]),
            "wq": d["L0_attn_Wq"], "wk": d["L0_attn_Wk"], "wv": d["L0_attn_Wv"], "wo": d["L0_attn_Wo"],
            "bq": bcol(d["L0_attn_bq"]), "bk": bcol(d["L0_attn_bk"]), "bo": bcol(d["L0_attn_bo"]),
            "gam": bcol(d["L0_attn_gamma"]), "bet": bcol(d["L0_attn_beta"]),
            "bvr": d["L0_attn_bv"].reshape(1, D),
        })
    res = run_bass_kernel_spmd(nc, in_maps, core_ids=list(range(8)))
    x1 = np.empty((B, S, D), np.float32)
    for c in range(8):
        b, half = c // 2, c % 2
        x1[b, half * TOWN:(half + 1) * TOWN] = res.results[c]["x1T"].T
    return x1




def build_l0_ffn():
    nc = bass.Bass("TRN2", target_bir_lowering=False)
    IN2 = 2816
    x1o_d = nc.dram_tensor("x1o", [TOWN, D], F32, kind="ExternalInput").ap()
    wg_d = nc.dram_tensor("wg", [IN2, D], F32, kind="ExternalInput").ap()
    wu_d = nc.dram_tensor("wu", [IN2, D], F32, kind="ExternalInput").ap()
    wd_d = nc.dram_tensor("wd", [D, IN2], F32, kind="ExternalInput").ap()
    idn_d = nc.dram_tensor("idn", [128, 128], BF16, kind="ExternalInput").ap()
    x2o_d = nc.dram_tensor("x2o", [TOWN, D], F32, kind="ExternalOutput").ap()

    ex = contextlib.ExitStack()
    with ex:
        sem = ex.enter_context(nc.semaphore("ser"))
        WGT = ex.enter_context(nc.sbuf_tensor("WGT", [128, 8 * IN2], BF16))
        WUT = ex.enter_context(nc.sbuf_tensor("WUT", [128, 8 * IN2], BF16))
        WDT = ex.enter_context(nc.sbuf_tensor("WDT", [128, 22 * D], BF16))
        XT_ = ex.enter_context(nc.sbuf_tensor("XT_", [128, 1408], F32))
        ZT_ = ex.enter_context(nc.sbuf_tensor("ZT_", [128, 1024], F32))
        T_A = ex.enter_context(nc.sbuf_tensor("T_A", [128, 1408], F32))
        T_CA = ex.enter_context(nc.sbuf_tensor("T_CA", [128, 1408], F32))
        T_CB = ex.enter_context(nc.sbuf_tensor("T_CB", [128, 1408], F32))
        MB = ex.enter_context(nc.sbuf_tensor("MB", [128, 88], F32))
        SB = ex.enter_context(nc.sbuf_tensor("SB", [128, 88], F32))
        RB = ex.enter_context(nc.sbuf_tensor("RB", [128, 88], F32))
        QB = ex.enter_context(nc.sbuf_tensor("QB", [128, 1408], BF16))
        ZTT = ex.enter_context(nc.sbuf_tensor("ZTT", [128, 1024], BF16))
        HT = ex.enter_context(nc.sbuf_tensor("HT", [128, IN2], BF16))
        GA = ex.enter_context(nc.sbuf_tensor("GA", [128, 512], F32))
        UA = ex.enter_context(nc.sbuf_tensor("UA", [128, 512], F32))
        HH = ex.enter_context(nc.sbuf_tensor("HH", [128, 512], F32))
        OUTT = ex.enter_context(nc.sbuf_tensor("OUTT", [128, 1024], F32))
        IDN = ex.enter_context(nc.sbuf_tensor("IDN", [128, 128], BF16))
        SQA = ex.enter_context(nc.sbuf_tensor("SQA", [128, 1024], F32))
        MU = ex.enter_context(nc.sbuf_tensor("MU", [128, 1], F32))
        MSQ = ex.enter_context(nc.sbuf_tensor("MSQ", [128, 1], F32))
        RSTD = ex.enter_context(nc.sbuf_tensor("RSTD", [128, 1], F32))
        ps_g = ex.enter_context(nc.psum_tensor("ps_g", [128, 512], F32))
        ps_u = ex.enter_context(nc.psum_tensor("ps_u", [128, 512], F32))
        ps_t = ex.enter_context(nc.psum_tensor("ps_t", [128, 128], BF16))
        blk = ex.enter_context(nc.Block())

        s = Ser(nc, sem)
        s.dma("sync", IDN[:], idn_d)

        def quant(src, out, F):
            G = F // 16
            s.op("vector", lambda e: e.tensor_reduce(
                MB[:, 0:G], src.rearrange("p (g s) -> p g s", s=16), axis=AX.X,
                op=ALU.max, apply_absolute_value=True))
            s.op("vector", lambda e: e.tensor_scalar(SB[:, 0:G], MB[:, 0:G], 1.0 / 6.0, 1e-12, op0=ALU.mult, op1=ALU.max))
            s.op("vector", lambda e: e.reciprocal(RB[:, 0:G], SB[:, 0:G]))
            s.op("vector", lambda e: e.tensor_tensor(
                src.rearrange("p (g s) -> p g s", s=16), src.rearrange("p (g s) -> p g s", s=16),
                RB[:, 0:G].unsqueeze(2).broadcast_to([128, G, 16]), op=ALU.mult))
            s.op("scalar", lambda e: e.activation(T_A[:, 0:F], src, AF.Abs))
            s.op("vector", lambda e: e.tensor_scalar(T_CB[:, 0:F], T_A[:, 0:F], 5.0, 2.0, op0=ALU.is_gt, op1=ALU.mult))
            s.op("vector", lambda e: e.scalar_tensor_tensor(T_CB[:, 0:F], T_A[:, 0:F], 2.5, T_CB[:, 0:F], op0=ALU.is_gt, op1=ALU.add))
            s.op("vector", lambda e: e.scalar_tensor_tensor(T_CB[:, 0:F], T_A[:, 0:F], 3.5, T_CB[:, 0:F], op0=ALU.is_gt, op1=ALU.add))
            s.op("vector", lambda e: e.tensor_scalar(T_CA[:, 0:F], T_A[:, 0:F], 0.25, None, op0=ALU.is_gt))
            s.op("vector", lambda e: e.scalar_tensor_tensor(T_CA[:, 0:F], T_A[:, 0:F], 0.75, T_CA[:, 0:F], op0=ALU.is_gt, op1=ALU.add))
            s.op("vector", lambda e: e.scalar_tensor_tensor(T_CA[:, 0:F], T_A[:, 0:F], 1.25, T_CA[:, 0:F], op0=ALU.is_gt, op1=ALU.add))
            s.op("vector", lambda e: e.scalar_tensor_tensor(T_CA[:, 0:F], T_A[:, 0:F], 1.75, T_CA[:, 0:F], op0=ALU.is_gt, op1=ALU.add))
            s.op("vector", lambda e: e.scalar_tensor_tensor(T_CA[:, 0:F], T_CA[:, 0:F], 0.5, T_CB[:, 0:F], op0=ALU.mult, op1=ALU.add))
            s.op("scalar", lambda e: e.activation(T_CB[:, 0:F], src, AF.Sign))
            s.op("vector", lambda e: e.tensor_tensor(
                T_CB[:, 0:F].rearrange("p (g s) -> p g s", s=16), T_CB[:, 0:F].rearrange("p (g s) -> p g s", s=16),
                SB[:, 0:G].unsqueeze(2).broadcast_to([128, G, 16]), op=ALU.mult))
            s.op("gpsimd", lambda e: e.tensor_tensor(out, T_CA[:, 0:F], T_CB[:, 0:F], op=ALU.mult))

        def transpose_to(src_bf16, dst):
            s.op("tensor", lambda e: e.transpose(ps_t[:], src_bf16, IDN[:]))
            s.op("vector", lambda e: e.tensor_copy(dst, ps_t[:]))

        for r in range(22):
            s.dma("sync", XT_[:, 0:1024], wg_d[r * 128:(r + 1) * 128, :])
            quant(XT_[:, 0:1024], QB[:, 0:1024], 1024)
            for ci in range(8):
                transpose_to(QB[:, ci * 128:(ci + 1) * 128], WGT[:, ci * IN2 + r * 128: ci * IN2 + r * 128 + 128])
        for r in range(22):
            s.dma("sync", XT_[:, 0:1024], wu_d[r * 128:(r + 1) * 128, :])
            quant(XT_[:, 0:1024], QB[:, 0:1024], 1024)
            for ci in range(8):
                transpose_to(QB[:, ci * 128:(ci + 1) * 128], WUT[:, ci * IN2 + r * 128: ci * IN2 + r * 128 + 128])
        for r in range(8):
            for hh in range(2):
                s.dma("sync", XT_[:, 0:1408], wd_d[r * 128:(r + 1) * 128, hh * 1408:(hh + 1) * 1408])
                quant(XT_[:, 0:1408], QB[:, 0:1408], 1408)
                for cj in range(11):
                    ri = hh * 11 + cj
                    transpose_to(QB[:, cj * 128:(cj + 1) * 128], WDT[:, ri * D + r * 128: ri * D + r * 128 + 128])

        OCW = [(0, 512), (512, 512), (1024, 512), (1536, 512), (2048, 512), (2560, 256)]
        for t in range(16):
            s.dma("sync", XT_[:, 0:1024], x1o_d[t * 128:(t + 1) * 128, :])
            s.op("vector", lambda e: e.tensor_reduce(MU[:], XT_[:, 0:1024], axis=AX.X, op=ALU.add))
            s.op("vector", lambda e: e.tensor_scalar(MU[:], MU[:], 1.0 / D, None, op0=ALU.mult))
            s.op("scalar", lambda e: e.activation(SQA[:], XT_[:, 0:1024], AF.Square))
            s.op("vector", lambda e: e.tensor_reduce(MSQ[:], SQA[:], axis=AX.X, op=ALU.add))
            s.op("vector", lambda e: e.tensor_scalar(MSQ[:], MSQ[:], 1.0 / D, None, op0=ALU.mult))
            s.op("vector", lambda e: e.tensor_tensor(RSTD[:], MU[:], MU[:], op=ALU.mult))
            s.op("vector", lambda e: e.tensor_tensor(RSTD[:], MSQ[:], RSTD[:], op=ALU.subtract))
            s.op("vector", lambda e: e.tensor_scalar(RSTD[:], RSTD[:], 1e-5, None, op0=ALU.add))
            s.op("scalar", lambda e: e.activation(RSTD[:], RSTD[:], AF.Sqrt))
            s.op("vector", lambda e: e.reciprocal(RSTD[:], RSTD[:]))
            s.op("vector", lambda e: e.tensor_scalar(ZT_[:], XT_[:, 0:1024], MU[:], RSTD[:], op0=ALU.subtract, op1=ALU.mult))
            quant(ZT_[:], QB[:, 0:1024], 1024)
            for ci in range(8):
                transpose_to(QB[:, ci * 128:(ci + 1) * 128], ZTT[:, ci * 128:(ci + 1) * 128])
            for (ooff, ow) in OCW:
                for ci in range(8):
                    s.op("tensor", lambda e, ci=ci, ooff=ooff, ow=ow: e.matmul(
                        ps_g[:, 0:ow], ZTT[:, ci * 128:(ci + 1) * 128],
                        WGT[:, ci * IN2 + ooff: ci * IN2 + ooff + ow], start=(ci == 0), stop=(ci == 7), skip_group_check=True))
                for ci in range(8):
                    s.op("tensor", lambda e, ci=ci, ooff=ooff, ow=ow: e.matmul(
                        ps_u[:, 0:ow], ZTT[:, ci * 128:(ci + 1) * 128],
                        WUT[:, ci * IN2 + ooff: ci * IN2 + ooff + ow], start=(ci == 0), stop=(ci == 7), skip_group_check=True))
                s.op("scalar", lambda e, ow=ow: e.activation(GA[:, 0:ow], ps_g[:, 0:ow], AF.Silu))
                s.op("scalar", lambda e, ow=ow: e.activation(UA[:, 0:ow], ps_u[:, 0:ow], AF.Copy))
                s.op("vector", lambda e, ow=ow: e.tensor_tensor(HH[:, 0:ow], GA[:, 0:ow], UA[:, 0:ow], op=ALU.mult))
                quant(HH[:, 0:ow], QB[:, 0:ow], ow)
                for k in range(ow // 128):
                    transpose_to(QB[:, k * 128:(k + 1) * 128], HT[:, ooff + k * 128: ooff + (k + 1) * 128])
            for oc2 in range(2):
                for ri in range(22):
                    s.op("tensor", lambda e, ri=ri, oc2=oc2: e.matmul(
                        ps_g[:], HT[:, ri * 128:(ri + 1) * 128],
                        WDT[:, ri * D + oc2 * 512: ri * D + oc2 * 512 + 512], start=(ri == 0), stop=(ri == 21), skip_group_check=True))
                s.op("vector", lambda e, oc2=oc2: e.tensor_tensor(
                    OUTT[:, oc2 * 512:(oc2 + 1) * 512], ps_g[:], XT_[:, oc2 * 512: oc2 * 512 + 512], op=ALU.add))
            s.dma("sync", x2o_d[t * 128:(t + 1) * 128, :], OUTT[:])

        s.build(blk)
    return nc


def run_l0_ffn(d, x1):
    nc = build_l0_ffn()
    idn = np.eye(128).astype(BF16NP)
    in_maps = []
    for c in range(8):
        b, half = c // 2, c % 2
        in_maps.append({
            "x1o": np.ascontiguousarray(x1[b, half * TOWN:(half + 1) * TOWN]),
            "wg": d["L0_ffn_w_gate"], "wu": d["L0_ffn_w_up"], "wd": d["L0_ffn_w_down"],
            "idn": idn,
        })
    res = run_bass_kernel_spmd(nc, in_maps, core_ids=list(range(8)))
    x2 = np.empty((B, S, D), np.float32)
    for c in range(8):
        b, half = c // 2, c % 2
        x2[b, half * TOWN:(half + 1) * TOWN] = res.results[c]["x2o"]
    return x2




def build_l1_attn():
    nc = bass.Bass("TRN2", target_bir_lowering=False)
    xT = nc.dram_tensor("xT", [D, S], F32, kind="ExternalInput").ap()
    xTo = nc.dram_tensor("xTo", [D, TOWN], F32, kind="ExternalInput").ap()
    wq_d = nc.dram_tensor("wq", [D, D], F32, kind="ExternalInput").ap()
    wk_d = nc.dram_tensor("wk", [D, D], F32, kind="ExternalInput").ap()
    wv_d = nc.dram_tensor("wv", [D, D], F32, kind="ExternalInput").ap()
    wo_d = nc.dram_tensor("wo", [D, D], F32, kind="ExternalInput").ap()
    bq_d = nc.dram_tensor("bq", [128, 8], F32, kind="ExternalInput").ap()
    bk_d = nc.dram_tensor("bk", [128, 8], F32, kind="ExternalInput").ap()
    bo_d = nc.dram_tensor("bo", [128, 8], F32, kind="ExternalInput").ap()
    bvr_d = nc.dram_tensor("bvr", [1, D], F32, kind="ExternalInput").ap()
    qidx_d = nc.dram_tensor("qidx", [1, TOWN], F32, kind="ExternalInput").ap()
    kidx_d = nc.dram_tensor("kidx", [128, 32], F32, kind="ExternalInput").ap()
    x3T_d = nc.dram_tensor("x3T", [D, TOWN], F32, kind="ExternalOutput").ap()
    z0T_d = nc.dram_tensor("z0T", [D, TOWN], F32, kind="ExternalOutput").ap()
    ctx_d = nc.dram_tensor("ctxh", [D, TOWN], BF16).ap()
    qT_d = nc.dram_tensor("qTh", [D, TOWN], BF16).ap()
    st_d = nc.dram_tensor("sth", [4, S], F32).ap()

    ex = contextlib.ExitStack()
    with ex:
        sem = ex.enter_context(nc.semaphore("ser"))
        A_K = ex.enter_context(nc.sbuf_tensor("A_K", [128, 8 * S], BF16))
        A_V = ex.enter_context(nc.sbuf_tensor("A_V", [128, 32 * 1040], BF16))
        A_W = ex.enter_context(nc.sbuf_tensor("A_W", [128, 8192], BF16))
        A_P = ex.enter_context(nc.sbuf_tensor("A_P", [128, 4096], BF16))
        WS = ex.enter_context(nc.sbuf_tensor("WS", [128, 16384], BF16))
        QS = ex.enter_context(nc.sbuf_tensor("QS", [128, 512], BF16))
        QIF = ex.enter_context(nc.sbuf_tensor("QIF", [128, 512], F32))
        KIX = ex.enter_context(nc.sbuf_tensor("KIX", [128, 32], F32))
        R0 = ex.enter_context(nc.sbuf_tensor("R0", [1, 512], F32))
        R1 = ex.enter_context(nc.sbuf_tensor("R1", [1, 512], F32))
        R2 = ex.enter_context(nc.sbuf_tensor("R2", [1, 512], F32))
        RL = ex.enter_context(nc.sbuf_tensor("RL", [1, 512], F32))
        RLB = ex.enter_context(nc.sbuf_tensor("RLB", [64, 512], F32))
        CTXS = ex.enter_context(nc.sbuf_tensor("CTXS", [64, 512], BF16))
        ONE1 = ex.enter_context(nc.sbuf_tensor("ONE1", [128, 1], F32))
        ONER = ex.enter_context(nc.sbuf_tensor("ONER", [1, 128], F32))
        BIA = ex.enter_context(nc.sbuf_tensor("BIA", [128, 24], F32))
        BVRB = ex.enter_context(nc.sbuf_tensor("BVRB", [128, D], F32))
        ps_s = ex.enter_context(nc.psum_tensor("ps_s", [128, 2048], F32))
        ps_o = ex.enter_context(nc.psum_tensor("ps_o", [65, 512], F32))
        ps_b = ex.enter_context(nc.psum_tensor("ps_b", [128, 512], F32))
        ps_c = ex.enter_context(nc.psum_tensor("ps_c", [128, 512], F32))
        blk = ex.enter_context(nc.Block())

        HBt = WS[:, 0:4096]
        WLv = WS[:, 4096:6144].bitcast(F32)
        XSv = WS[:, 6144:7168].bitcast(F32)
        SQv = WS[:, 7168:8192].bitcast(F32)
        HFv = WS[:, 8192:9216].bitcast(F32)
        VSv = WS[:, 9216:10240].bitcast(F32)
        OA = A_V[:, 0:4096]
        YA = A_V[:, 8192:16384].bitcast(F32)
        XA = A_V[:, 16384:24576].bitcast(F32)

        s = Ser(nc, sem)
        s.op("vector", lambda e: e.memset(ONE1[:], 1.0))
        s.op("vector", lambda e: e.memset(ONER[:], 1.0))
        s.dma("sync", BIA[:, 0:8], bq_d)
        s.dma("sync", BIA[:, 8:16], bk_d)
        s.dma("sync", BIA[:, 16:24], bo_d)
        s.dma("sync", BVRB[:], bvr_d.partition_broadcast(128))
        s.dma("sync", KIX[:], kidx_d)
        for tt in range(32):
            s.op("gpsimd", lambda e, tt=tt: e.memset(
                A_V[:, tt * 1040:(tt + 1) * 1040].rearrange("p (h d) -> p h d", d=65)[:, :, 64:65], 1.0))

        def stats_sweep(src, ntg, mu_row, rstd_row):
            for tg in range(ntg):
                for ci in range(8):
                    s.dma("sync", XSv, src[ci * 128:(ci + 1) * 128, tg * 512:(tg + 1) * 512])
                    s.op("scalar", lambda e: e.activation(SQv, XSv, AF.Square))
                    s.op("tensor", lambda e, ci=ci: e.matmul(
                        ps_s[0:1, 0:512], ONE1[:], XSv, start=(ci == 0), stop=(ci == 7), skip_group_check=True))
                    s.op("tensor", lambda e, ci=ci: e.matmul(
                        ps_s[0:1, 512:1024], ONE1[:], SQv, start=(ci == 0), stop=(ci == 7), skip_group_check=True))
                s.op("vector", lambda e: e.tensor_scalar(R0[:], ps_s[0:1, 0:512], 1.0 / D, None, op0=ALU.mult))
                s.op("vector", lambda e: e.tensor_scalar(R1[:], ps_s[0:1, 512:1024], 1.0 / D, None, op0=ALU.mult))
                s.op("vector", lambda e: e.tensor_tensor(R2[:], R0[:], R0[:], op=ALU.mult))
                s.op("vector", lambda e: e.tensor_tensor(R2[:], R1[:], R2[:], op=ALU.subtract))
                s.op("vector", lambda e: e.tensor_scalar(R2[:], R2[:], 1e-5, None, op0=ALU.add))
                s.op("scalar", lambda e: e.activation(R2[:], R2[:], AF.Sqrt))
                s.op("vector", lambda e: e.reciprocal(R2[:], R2[:]))
                s.dma("sync", st_d[mu_row:mu_row + 1, tg * 512:(tg + 1) * 512], R0[:])
                s.dma("sync", st_d[rstd_row:rstd_row + 1, tg * 512:(tg + 1) * 512], R2[:])

        def load_w(w_d):
            for ci in range(8):
                s.dma("sync", WLv, w_d[ci * 128:(ci + 1) * 128, :])
                s.op("vector", lambda e, ci=ci: e.tensor_copy(A_W[:, ci * 1024:(ci + 1) * 1024], WLv))

        def make_h(src, tg, mu_row, rstd_row):
            s.dma("sync", R0[:], st_d[mu_row:mu_row + 1, tg * 512:(tg + 1) * 512])
            s.dma("sync", R2[:], st_d[rstd_row:rstd_row + 1, tg * 512:(tg + 1) * 512])
            s.op("tensor", lambda e: e.matmul(ps_b[:], ONER[:], R0[:], start=True, stop=True))
            s.op("tensor", lambda e: e.matmul(ps_c[:], ONER[:], R2[:], start=True, stop=True))
            for ci in range(8):
                s.dma("sync", XSv, src[ci * 128:(ci + 1) * 128, tg * 512:(tg + 1) * 512])
                s.op("vector", lambda e: e.tensor_tensor(HFv, XSv, ps_b[:], op=ALU.subtract))
                s.op("vector", lambda e, ci=ci: e.tensor_tensor(HBt[:, ci * 512:(ci + 1) * 512], HFv, ps_c[:], op=ALU.mult))

        stats_sweep(xT, 8, 0, 1)
        stats_sweep(xTo, 4, 2, 3)

        load_w(wk_d)
        for tg in range(8):
            make_h(xT, tg, 0, 1)
            for oc in range(8):
                for ci in range(8):
                    s.op("tensor", lambda e, oc=oc, ci=ci: e.matmul(
                        ps_s[:, 0:512], A_W[:, ci * 1024 + oc * 128: ci * 1024 + oc * 128 + 128],
                        HBt[:, ci * 512:(ci + 1) * 512], start=(ci == 0), stop=(ci == 7)))
                s.op("vector", lambda e, oc=oc, tg=tg: e.tensor_scalar(
                    A_K[:, oc * S + tg * 512: oc * S + tg * 512 + 512], ps_s[:, 0:512],
                    BIA[:, 8 + oc:9 + oc], None, op0=ALU.add))

        load_w(wv_d)
        for tg in range(8):
            make_h(xT, tg, 0, 1)
            for tt4 in range(4):
                tt = tg * 4 + tt4
                for nh in range(2):
                    for ci in range(8):
                        s.op("tensor", lambda e, tt4=tt4, nh=nh, ci=ci: e.matmul(
                            ps_b[:], HBt[:, ci * 512 + tt4 * 128: ci * 512 + tt4 * 128 + 128],
                            A_W[:, ci * 1024 + nh * 512: ci * 1024 + nh * 512 + 512],
                            start=(ci == 0), stop=(ci == 7)))
                    s.op("scalar", lambda e: e.activation(VSv, ps_b[:], AF.Copy))
                    s.op("vector", lambda e, tt=tt, nh=nh: e.tensor_tensor(
                        A_V[:, tt * 1040 + nh * 520: tt * 1040 + nh * 520 + 520].rearrange("p (h d) -> p h d", d=65)[:, :, 0:64],
                        VSv.rearrange("p (h d) -> p h d", d=64),
                        BVRB[:, nh * 512:(nh + 1) * 512].rearrange("p (h d) -> p h d", d=64), op=ALU.add))

        load_w(wq_d)
        for tg in range(4):
            make_h(xTo, tg, 2, 3)
            for oc in range(8):
                for ci in range(8):
                    s.op("tensor", lambda e, oc=oc, ci=ci: e.matmul(
                        ps_s[:, 0:512], A_W[:, ci * 1024 + oc * 128: ci * 1024 + oc * 128 + 128],
                        HBt[:, ci * 512:(ci + 1) * 512], start=(ci == 0), stop=(ci == 7)))
                s.op("vector", lambda e, oc=oc: e.tensor_scalar(QS[:], ps_s[:, 0:512], BIA[:, oc:oc + 1], None, op0=ALU.add))
                s.dma("sync", qT_d[oc * 128:(oc + 1) * 128, tg * 512:(tg + 1) * 512], QS[:])

        MSKA = WS[:]
        for qg in range(4):
            for ci in range(8):
                s.dma("sync", A_W[:, ci * 512:(ci + 1) * 512], qT_d[ci * 128:(ci + 1) * 128, qg * 512:(qg + 1) * 512])
            s.dma("sync", QIF[:], qidx_d[0:1, qg * 512:(qg + 1) * 512].partition_broadcast(128))
            for kc in range(32):
                s.op("vector", lambda e, kc=kc: e.tensor_scalar(
                    MSKA[:, kc * 512:(kc + 1) * 512], QIF[:], KIX[:, kc:kc + 1], None, op0=ALU.is_ge))
            for h in range(16):
                hr = (h % 2) * 64
                hc = (h // 2)
                for kb in range(8):
                    for j in range(4):
                        kc = kb * 4 + j
                        s.op("tensor", lambda e, hr=hr, hc=hc, kc=kc, j=j: e.matmul(
                            ps_s[:, j * 512:(j + 1) * 512],
                            A_K[hr:hr + 64, hc * S + kc * 128: hc * S + kc * 128 + 128],
                            A_W[hr:hr + 64, hc * 512: hc * 512 + 512],
                            start=True, stop=True, skip_group_check=True))
                    s.op("scalar", lambda e: e.activation(A_P[:, 0:2048], ps_s[:], AF.Exp, scale=0.03125))
                    eng = "vector" if kb % 2 == 0 else "gpsimd"
                    s.op(eng, lambda e, kb=kb: e.tensor_tensor(
                        A_P[:, 0:2048], A_P[:, 0:2048], MSKA[:, kb * 2048:(kb + 1) * 2048], op=ALU.mult))
                    for j in range(4):
                        kc = kb * 4 + j
                        s.op("tensor", lambda e, kc=kc, h=h, j=j, kb=kb: e.matmul(
                            ps_o[:], A_V[:, kc * 1040 + 65 * h: kc * 1040 + 65 * h + 65],
                            A_P[:, j * 512:(j + 1) * 512],
                            start=(kb == 0 and j == 0), stop=(kb == 7 and j == 3), skip_group_check=True))
                s.op("vector", lambda e: e.reciprocal(RL[:], ps_o[64:65, :]))
                s.op("tensor", lambda e: e.matmul(ps_b[0:64, :], ONER[:, 0:64], RL[:], start=True, stop=True))
                s.op("scalar", lambda e: e.activation(RLB[:], ps_b[0:64, :], AF.Copy))
                s.op("vector", lambda e: e.tensor_tensor(CTXS[:], ps_o[0:64, :], RLB[:], op=ALU.mult))
                s.dma("sync", ctx_d[h * 64:(h + 1) * 64, qg * 512:(qg + 1) * 512], CTXS[:])

        load_w(wo_d)
        for tg in range(4):
            for ci in range(8):
                s.dma("sync", A_P[:, ci * 512:(ci + 1) * 512], ctx_d[ci * 128:(ci + 1) * 128, tg * 512:(tg + 1) * 512])
            for oc in range(8):
                for ci in range(8):
                    s.op("tensor", lambda e, oc=oc, ci=ci: e.matmul(
                        ps_s[:, 0:512], A_W[:, ci * 1024 + oc * 128: ci * 1024 + oc * 128 + 128],
                        A_P[:, ci * 512:(ci + 1) * 512], start=(ci == 0), stop=(ci == 7)))
                s.op("vector", lambda e, oc=oc: e.tensor_scalar(OA[:, oc * 512:(oc + 1) * 512], ps_s[:, 0:512],
                                                                BIA[:, 16 + oc:17 + oc], None, op0=ALU.add))
            for ci in range(8):
                s.dma("sync", XA[:, ci * 512:(ci + 1) * 512], xTo[ci * 128:(ci + 1) * 128, tg * 512:(tg + 1) * 512])
                s.op("vector", lambda e, ci=ci: e.tensor_tensor(
                    YA[:, ci * 512:(ci + 1) * 512], XA[:, ci * 512:(ci + 1) * 512], OA[:, ci * 512:(ci + 1) * 512], op=ALU.add))
                s.dma("sync", x3T_d[ci * 128:(ci + 1) * 128, tg * 512:(tg + 1) * 512], YA[:, ci * 512:(ci + 1) * 512])
                s.op("scalar", lambda e, ci=ci: e.activation(SQv, YA[:, ci * 512:(ci + 1) * 512], AF.Square))
                s.op("tensor", lambda e, ci=ci: e.matmul(
                    ps_s[0:1, 0:512], ONE1[:], YA[:, ci * 512:(ci + 1) * 512], start=(ci == 0), stop=(ci == 7), skip_group_check=True))
                s.op("tensor", lambda e, ci=ci: e.matmul(
                    ps_s[0:1, 512:1024], ONE1[:], SQv, start=(ci == 0), stop=(ci == 7), skip_group_check=True))
            s.op("vector", lambda e: e.tensor_scalar(R0[:], ps_s[0:1, 0:512], 1.0 / D, None, op0=ALU.mult))
            s.op("vector", lambda e: e.tensor_scalar(R1[:], ps_s[0:1, 512:1024], 1.0 / D, None, op0=ALU.mult))
            s.op("vector", lambda e: e.tensor_tensor(R2[:], R0[:], R0[:], op=ALU.mult))
            s.op("vector", lambda e: e.tensor_tensor(R2[:], R1[:], R2[:], op=ALU.subtract))
            s.op("vector", lambda e: e.tensor_scalar(R2[:], R2[:], 1e-5, None, op0=ALU.add))
            s.op("scalar", lambda e: e.activation(R2[:], R2[:], AF.Sqrt))
            s.op("vector", lambda e: e.reciprocal(R2[:], R2[:]))
            s.op("tensor", lambda e: e.matmul(ps_b[:], ONER[:], R0[:], start=True, stop=True))
            s.op("tensor", lambda e: e.matmul(ps_c[:], ONER[:], R2[:], start=True, stop=True))
            for ci in range(8):
                s.op("vector", lambda e, ci=ci: e.tensor_tensor(HFv, YA[:, ci * 512:(ci + 1) * 512], ps_b[:], op=ALU.subtract))
                s.op("vector", lambda e: e.tensor_tensor(HFv, HFv, ps_c[:], op=ALU.mult))
                s.dma("sync", z0T_d[ci * 128:(ci + 1) * 128, tg * 512:(tg + 1) * 512], HFv)

        s.build(blk)
    return nc


def run_l1_attn(d, x2):
    nc = build_l1_attn()
    x2T_all = t2d(x2)
    kidx = (np.arange(32)[None, :] * 128 + np.arange(128)[:, None]).astype(np.float32)
    in_maps = []
    for c in range(8):
        b, half = c // 2, c % 2
        own = slice(half * TOWN, (half + 1) * TOWN)
        in_maps.append({
            "xT": x2T_all[b],
            "xTo": np.ascontiguousarray(x2T_all[b][:, own]),
            "wq": t2d(d["L1_attn_Wq_w"]), "wk": t2d(d["L1_attn_Wk_w"]),
            "wv": t2d(d["L1_attn_Wv_w"]), "wo": t2d(d["L1_attn_Wo_w"]),
            "bq": bcol(d["L1_attn_Wq_b"]), "bk": bcol(d["L1_attn_Wk_b"]), "bo": bcol(d["L1_attn_Wo_b"]),
            "bvr": d["L1_attn_Wv_b"].reshape(1, D),
            "qidx": np.arange(half * TOWN, (half + 1) * TOWN, dtype=np.float32).reshape(1, TOWN),
            "kidx": kidx,
        })
    res = run_bass_kernel_spmd(nc, in_maps, core_ids=list(range(8)))
    x3 = np.empty((B, S, D), np.float32)
    z0 = np.empty((B, S, D), np.float32)
    for c in range(8):
        b, half = c // 2, c % 2
        x3[b, half * TOWN:(half + 1) * TOWN] = res.results[c]["x3T"].T
        z0[b, half * TOWN:(half + 1) * TOWN] = res.results[c]["z0T"].T
    return x3, z0



HALO = 384
NBUF = HALO + TOWN
NQ = TOWN + 128
DI = 2730
ACH = [128] * 21 + [42]
GW = [(0, 512), (512, 512), (1024, 512), (1536, 512), (2048, NBUF - 2048)]
GQ = [(0, 512), (512, 512), (1024, 512), (1536, 512), (2048, NQ - 2048)]


def build_l1f(last):
    nc = bass.Bass("TRN2", target_bir_lowering=False)
    zH_d = nc.dram_tensor("zH", [D, 1 + NBUF], F32, kind="ExternalInput").ap()
    wqkv_d = nc.dram_tensor("wqkv", [D, 3 * D], F32, kind="ExternalInput").ap()
    wo2_d = nc.dram_tensor("wo2", [D, D], F32, kind="ExternalInput").ap()
    w1_d = nc.dram_tensor("w1", [D, 2 * 2816], F32, kind="ExternalInput").ap()
    w2_d = nc.dram_tensor("w2", [2816, D], F32, kind="ExternalInput").ap()
    b1a_d = nc.dram_tensor("b1a", [128, 22], F32, kind="ExternalInput").ap()
    b1g_d = nc.dram_tensor("b1g", [128, 22], F32, kind="ExternalInput").ap()
    b2_d = nc.dram_tensor("b2c", [128, 8], F32, kind="ExternalInput").ap()
    lng_d = nc.dram_tensor("lng", [128, 8], F32, kind="ExternalInput").ap()
    lnb_d = nc.dram_tensor("lnb", [128, 8], F32, kind="ExternalInput").ap()
    qsc_d = nc.dram_tensor("qsc", [128, 1], F32, kind="ExternalInput").ap()
    ksc_d = nc.dram_tensor("ksc", [128, 1], F32, kind="ExternalInput").ap()
    csq_d = nc.dram_tensor("csq", [128, NQ], F32, kind="ExternalInput").ap()
    snq_d = nc.dram_tensor("snq", [128, NQ], F32, kind="ExternalInput").ap()
    csk_d = nc.dram_tensor("csk", [128, NBUF], F32, kind="ExternalInput").ap()
    snk_d = nc.dram_tensor("snk", [128, NBUF], F32, kind="ExternalInput").ap()
    rotm_d = nc.dram_tensor("rotm", [128, 128], BF16, kind="ExternalInput").ap()
    sel2_d = nc.dram_tensor("sel2", [128, 2], F32, kind="ExternalInput").ap()
    bb2_d = nc.dram_tensor("bb2", [2, 128], F32, kind="ExternalInput").ap()
    qidx_d = nc.dram_tensor("qidx", [1, NQ], F32, kind="ExternalInput").ap()
    kix_d = nc.dram_tensor("kix", [128, 19], F32, kind="ExternalInput").ap()
    kix2_d = nc.dram_tensor("kix2", [128, 19], F32, kind="ExternalInput").ap()
    if last:
        xres_d = nc.dram_tensor("xres", [D, TOWN], F32, kind="ExternalInput").ap()
    zo_d = nc.dram_tensor("zo", [D, TOWN], F32, kind="ExternalOutput").ap()
    ctx_d = nc.dram_tensor("ctxh", [D, NQ], BF16).ap()
    z2_d = nc.dram_tensor("z2h", [D, NQ], F32).ap()
    w1b_d = nc.dram_tensor("w1bh", [128, 8 * 5632], BF16).ap()
    w2b_d = nc.dram_tensor("w2bh", [128, 22 * D], BF16).ap()

    ex = contextlib.ExitStack()
    with ex:
        sem = ex.enter_context(nc.semaphore("ser"))
        SH = ex.enter_context(nc.sbuf_tensor("SH", [128, 36864], BF16))
        ZB = ex.enter_context(nc.sbuf_tensor("ZB", [128, 8 * NBUF], BF16))
        TB1 = ex.enter_context(nc.sbuf_tensor("TB1", [128, NBUF], F32))
        TB2 = ex.enter_context(nc.sbuf_tensor("TB2", [128, NBUF], F32))
        TB3 = ex.enter_context(nc.sbuf_tensor("TB3", [128, NQ], F32))
        TB4 = ex.enter_context(nc.sbuf_tensor("TB4", [128, NQ], F32))
        W2A = ex.enter_context(nc.sbuf_tensor("W2A", [128, 2816], BF16))
        CTXB = ex.enter_context(nc.sbuf_tensor("CTXB", [128, NQ], BF16))
        WOB = ex.enter_context(nc.sbuf_tensor("WOB", [128, 8192], BF16))
        MSKA = ex.enter_context(nc.sbuf_tensor("MSKA", [128, 17 * 384], BF16))
        QIF = ex.enter_context(nc.sbuf_tensor("QIF", [128, 128], F32))
        KIX = ex.enter_context(nc.sbuf_tensor("KIX", [128, 19], F32))
        KIX2 = ex.enter_context(nc.sbuf_tensor("KIX2", [128, 19], F32))
        PP = ex.enter_context(nc.sbuf_tensor("PP", [128, 768], BF16))
        N2 = ex.enter_context(nc.sbuf_tensor("N2", [2, NBUF], F32))
        RL = ex.enter_context(nc.sbuf_tensor("RL", [1, 256], F32))
        RLB = ex.enter_context(nc.sbuf_tensor("RLB", [64, 256], F32))
        ROTM = ex.enter_context(nc.sbuf_tensor("ROTM", [128, 128], BF16))
        SEL2 = ex.enter_context(nc.sbuf_tensor("SEL2", [128, 2], F32))
        BB2 = ex.enter_context(nc.sbuf_tensor("BB2", [2, 128], F32))
        QSC = ex.enter_context(nc.sbuf_tensor("QSC", [128, 2], F32))
        ONER = ex.enter_context(nc.sbuf_tensor("ONER", [1, 128], F32))
        ONE1 = ex.enter_context(nc.sbuf_tensor("ONE1", [128, 1], F32))
        BIA = ex.enter_context(nc.sbuf_tensor("BIA", [128, 68], F32))
        WSL = ex.enter_context(nc.sbuf_tensor("WSL", [128, 512], F32))
        R0 = ex.enter_context(nc.sbuf_tensor("R0", [1, 512], F32))
        R1 = ex.enter_context(nc.sbuf_tensor("R1", [1, 512], F32))
        R2 = ex.enter_context(nc.sbuf_tensor("R2", [1, 512], F32))
        psA = ex.enter_context(nc.psum_tensor("psA", [128, 512], F32))
        psB = ex.enter_context(nc.psum_tensor("psB", [128, 512], F32))
        psC = ex.enter_context(nc.psum_tensor("psC", [128, 512], F32))
        psD = ex.enter_context(nc.psum_tensor("psD", [65, 256], F32))
        psS = ex.enter_context(nc.psum_tensor("psS", [128, 1024], F32))
        psN = ex.enter_context(nc.psum_tensor("psN", [2, 512], F32))
        blk = ex.enter_context(nc.Block())

        QR = SH[:, 0:4864].bitcast(F32)
        QH = SH[:, 4864:9728].bitcast(F32)
        SQ2 = SH[:, 9728:14592].bitcast(F32)
        QHB = SH[:, 14592:17024]
        Q2 = SH[:, 17024:19200]
        K2 = SH[:, 19200:21632]
        VT = SH[:, 21632:24102]
        WQS = SH[:, 24102:27174]
        XF = SH[:, 0:8192].bitcast(F32)
        XN = SH[:, 8192:12288]
        H2T = SH[:, 12288:23552]
        WSLB = SH[:, 23552:27648]
        GEV = SH[:, 27648:28672].bitcast(F32)
        AEV = SH[:, 28672:29696].bitcast(F32)
        OUV = SH[:, 29696:30720].bitcast(F32)
        CXS = SH[:, 30720:34816]

        s = Ser(nc, sem)
        s.op("vector", lambda e: e.memset(ONER[:], 1.0))
        s.op("vector", lambda e: e.memset(ONE1[:], 1.0))
        s.dma("sync", ROTM[:], rotm_d)
        s.dma("sync", SEL2[:], sel2_d)
        s.dma("sync", BB2[:], bb2_d)
        s.dma("sync", QSC[:, 0:1], qsc_d)
        s.dma("sync", QSC[:, 1:2], ksc_d)
        s.dma("sync", KIX[:], kix_d)
        s.dma("sync", KIX2[:], kix2_d)
        s.dma("sync", BIA[:, 0:22], b1a_d)
        s.dma("sync", BIA[:, 22:44], b1g_d)
        s.dma("sync", BIA[:, 44:52], b2_d)
        s.dma("sync", BIA[:, 52:60], lng_d)
        s.dma("sync", BIA[:, 60:68], lnb_d)

        for ti in range(17):
            s.dma("sync", QIF[:], qidx_d[0:1, ti * 128:(ti + 1) * 128].partition_broadcast(128))
            for c3 in range(3):
                ck = ti + c3
                ms = MSKA[:, (ti * 3 + c3) * 128: (ti * 3 + c3) * 128 + 128]
                s.op("vector", lambda e, ms=ms, ck=ck: e.tensor_scalar(
                    ms, QIF[:], KIX[:, ck:ck + 1], None, op0=ALU.is_ge))
                s.op("vector", lambda e, ms=ms, ck=ck: e.scalar_tensor_tensor(
                    ms, QIF[:], KIX2[:, ck:ck + 1], ms, op0=ALU.is_le, op1=ALU.mult))

        W1G = [(g * 512, 512) for g in range(5)] + [(2560, 256)]
        for ci in range(8):
            for part in range(2):
                for (go, gw) in W1G:
                    s.dma("sync", WSL[:, 0:gw], w1_d[ci * 128:(ci + 1) * 128, part * 2816 + go: part * 2816 + go + gw])
                    s.op("vector", lambda e, gw=gw: e.tensor_copy(QHB[:, 0:gw], WSL[:, 0:gw]))
                    s.dma("sync", w1b_d[:, ci * 5632 + part * 2816 + go: ci * 5632 + part * 2816 + go + gw], QHB[:, 0:gw])
        for ri in range(22):
            for g2 in range(2):
                s.dma("sync", WSL[:, 0:512], w2_d[ri * 128:(ri + 1) * 128, g2 * 512:(g2 + 1) * 512])
                s.op("vector", lambda e: e.tensor_copy(QHB[:, 0:512], WSL[:, 0:512]))
                s.dma("sync", w2b_d[:, ri * 1024 + g2 * 512: ri * 1024 + g2 * 512 + 512], QHB[:, 0:512])

        for ci in range(8):
            off = 1 if ci < 4 else 0
            for g in range(5):
                w = 512 if g < 4 else NBUF - 2048
                s.dma("sync", WSL[:, 0:w], zH_d[ci * 128:(ci + 1) * 128, off + g * 512: off + g * 512 + w])
                s.op("vector", lambda e, ci=ci, g=g, w=w: e.tensor_copy(
                    ZB[:, ci * NBUF + g * 512: ci * NBUF + g * 512 + w], WSL[:, 0:w]))

        def rotary(kind):
            boff = 0
            groups = GQ if kind == "q" else GW
            scol = 0 if kind == "q" else 1
            out = Q2 if kind == "q" else K2
            n = NQ if kind == "q" else NBUF
            Ta = TB3 if kind == "q" else TB1
            Tb = TB4 if kind == "q" else TB2
            s.op("scalar", lambda e, n=n: e.activation(SQ2[:, 0:n], QR[:, 0:n], AF.Square))
            for (go, gw) in groups:
                s.op("tensor", lambda e, go=go, gw=gw: e.matmul(
                    psN[:, 0:gw], SEL2[:], SQ2[:, go:go + gw], start=True, stop=True))
                s.op("vector", lambda e, go=go, gw=gw: e.tensor_copy(N2[:, go:go + gw], psN[:, 0:gw]))
            s.op("scalar", lambda e, n=n: e.activation(N2[:, 0:n], N2[:, 0:n], AF.Sqrt))
            s.op("vector", lambda e, n=n: e.tensor_scalar(N2[:, 0:n], N2[:, 0:n], 1e-12, None, op0=ALU.max))
            s.op("vector", lambda e, n=n: e.reciprocal(N2[:, 0:n], N2[:, 0:n]))
            for (go, gw) in groups:
                s.op("tensor", lambda e, go=go, gw=gw: e.matmul(
                    psB[:, 0:gw], BB2[:], N2[:, go:go + gw], start=True, stop=True))
                s.op("vector", lambda e, go=go, gw=gw, scol=scol: e.scalar_tensor_tensor(
                    QH[:, go:go + gw], QR[:, go:go + gw], QSC[:, scol:scol + 1], psB[:, 0:gw],
                    op0=ALU.mult, op1=ALU.mult))
                s.op("vector", lambda e, go=go, gw=gw: e.tensor_copy(QHB[:, go:go + gw], QH[:, go:go + gw]))
                s.op("tensor", lambda e, go=go, gw=gw: e.matmul(
                    psC[:, 0:gw], ROTM[:], QHB[:, go:go + gw], start=True, stop=True))
                s.op("vector", lambda e, go=go, gw=gw, boff=boff, Ta=Ta: e.tensor_tensor(
                    QH[:, go:go + gw], QH[:, go:go + gw], Ta[:, boff + go: boff + go + gw], op=ALU.mult))
                s.op("vector", lambda e, go=go, gw=gw, boff=boff, Tb=Tb: e.tensor_tensor(
                    SQ2[:, go:go + gw], psC[:, 0:gw], Tb[:, boff + go: boff + go + gw], op=ALU.mult))
                s.op("vector", lambda e, go=go, gw=gw, out=out: e.tensor_tensor(
                    out[:, go:go + gw], QH[:, go:go + gw], SQ2[:, go:go + gw], op=ALU.add))

        s.dma("sync", TB1[:, 0:NBUF], csk_d)
        s.dma("sync", TB2[:, 0:NBUF], snk_d)
        s.dma("sync", TB3[:, 0:NQ], csq_d)
        s.dma("sync", TB4[:, 0:NQ], snq_d)
        for st in range(8):
            for part in range(3):
                for ci in range(8):
                    s.dma("sync", WSL[:, 0:128], wqkv_d[ci * 128:(ci + 1) * 128, part * D + st * 128: part * D + st * 128 + 128])
                    s.op("vector", lambda e, part=part, ci=ci: e.tensor_copy(
                        WQS[:, (part * 8 + ci) * 128:(part * 8 + ci) * 128 + 128], WSL[:, 0:128]))
            for tt in range(19):
                s.op("gpsimd", lambda e, tt=tt: e.memset(
                    VT[:, tt * 130:(tt + 1) * 130].rearrange("p (h d) -> p h d", d=65)[:, :, 64:65], 1.0))
                for ci in range(8):
                    s.op("tensor", lambda e, tt=tt, ci=ci: e.matmul(
                        psA[:, 0:128], ZB[:, ci * NBUF + tt * 128: ci * NBUF + tt * 128 + 128],
                        WQS[:, (2 * 8 + ci) * 128:(2 * 8 + ci) * 128 + 128],
                        start=(ci == 0), stop=(ci == 7)))
                s.op("vector", lambda e, tt=tt: e.tensor_copy(
                    VT[:, tt * 130:(tt + 1) * 130].rearrange("p (h d) -> p h d", d=65)[:, :, 0:64],
                    psA[:, 0:128].rearrange("p (h d) -> p h d", d=64)))
            for (go, gw) in GW:
                for ci in range(8):
                    s.op("tensor", lambda e, go=go, gw=gw, ci=ci: e.matmul(
                        psA[:, 0:gw], WQS[:, (1 * 8 + ci) * 128:(1 * 8 + ci) * 128 + 128],
                        ZB[:, ci * NBUF + go: ci * NBUF + go + gw],
                        start=(ci == 0), stop=(ci == 7)))
                s.op("scalar", lambda e, go=go, gw=gw: e.activation(QR[:, go:go + gw], psA[:, 0:gw], AF.Copy))
            rotary("k")
            for (go, gw) in GQ:
                for ci in range(8):
                    s.op("tensor", lambda e, go=go, gw=gw, ci=ci: e.matmul(
                        psA[:, 0:gw], WQS[:, (0 * 8 + ci) * 128:(0 * 8 + ci) * 128 + 128],
                        ZB[:, ci * NBUF + 256 + go: ci * NBUF + 256 + go + gw],
                        start=(ci == 0), stop=(ci == 7)))
                s.op("scalar", lambda e, go=go, gw=gw: e.activation(QR[:, go:go + gw], psA[:, 0:gw], AF.Copy))
            rotary("q")
            for ti in range(17):
                for h2 in range(2):
                    hr = h2 * 64
                    for c3 in range(3):
                        ck = ti + c3
                        s.op("tensor", lambda e, hr=hr, ck=ck, c3=c3, ti=ti, h2=h2: e.matmul(
                            psS[:, (h2 * 3 + c3) * 128:(h2 * 3 + c3 + 1) * 128],
                            K2[hr:hr + 64, ck * 128: ck * 128 + 128],
                            Q2[hr:hr + 64, ti * 128: ti * 128 + 128],
                            start=True, stop=True, skip_group_check=True))
                s.op("scalar", lambda e: e.activation(PP[:, 0:768], psS[:, 0:768], AF.Exp, scale=8.0))
                s.op("vector", lambda e, ti=ti: e.tensor_tensor(
                    PP[:, 0:768].rearrange("p (h w) -> p h w", w=384),
                    PP[:, 0:768].rearrange("p (h w) -> p h w", w=384),
                    MSKA[:, ti * 384:(ti + 1) * 384].unsqueeze(1).broadcast_to([128, 2, 384]), op=ALU.mult))
                for h2 in range(2):
                    for c3 in range(3):
                        ck = ti + c3
                        s.op("tensor", lambda e, ck=ck, h2=h2, c3=c3: e.matmul(
                            psD[:, h2 * 128:(h2 + 1) * 128],
                            VT[:, ck * 130 + h2 * 65: ck * 130 + h2 * 65 + 65],
                            PP[:, (h2 * 3 + c3) * 128:(h2 * 3 + c3 + 1) * 128],
                            start=(c3 == 0), stop=(c3 == 2), skip_group_check=True))
                s.op("vector", lambda e: e.tensor_scalar(RL[:, 0:256], psD[64:65, 0:256], 1e-20, None, op0=ALU.add))
                s.op("vector", lambda e: e.reciprocal(RL[:, 0:256], RL[:, 0:256]))
                s.op("tensor", lambda e: e.matmul(psB[0:64, 0:256], ONER[:, 0:64], RL[:, 0:256], start=True, stop=True))
                s.op("scalar", lambda e: e.activation(RLB[:, 0:256], psB[0:64, 0:256], AF.Copy))
                for h2 in range(2):
                    s.op("vector", lambda e, h2=h2, ti=ti: e.tensor_tensor(
                        CTXB[h2 * 64:(h2 + 1) * 64, ti * 128:(ti + 1) * 128],
                        psD[0:64, h2 * 128:(h2 + 1) * 128], RLB[:, h2 * 128:(h2 + 1) * 128], op=ALU.mult))
            s.dma("sync", ctx_d[st * 128:(st + 1) * 128, :], CTXB[:])

        for ci in range(8):
            s.dma("sync", WSL[:, 0:512], wo2_d[ci * 128:(ci + 1) * 128, 0:512])
            s.op("vector", lambda e, ci=ci: e.tensor_copy(WOB[:, ci * 1024: ci * 1024 + 512], WSL[:, 0:512]))
            s.dma("sync", WSL[:, 0:512], wo2_d[ci * 128:(ci + 1) * 128, 512:1024])
            s.op("vector", lambda e, ci=ci: e.tensor_copy(WOB[:, ci * 1024 + 512: ci * 1024 + 1024], WSL[:, 0:512]))
        for (go, gw) in GQ:
            for ci in range(8):
                s.dma("sync", CXS[:, ci * 512: ci * 512 + gw], ctx_d[ci * 128:(ci + 1) * 128, go:go + gw])
            for oc in range(8):
                for ci in range(8):
                    s.op("tensor", lambda e, oc=oc, ci=ci, gw=gw: e.matmul(
                        psA[:, 0:gw], WOB[:, ci * 1024 + oc * 128: ci * 1024 + oc * 128 + 128],
                        CXS[:, ci * 512: ci * 512 + gw], start=(ci == 0), stop=(ci == 7)))
                s.dma("sync", WSL[:, 0:gw], zH_d[oc * 128:(oc + 1) * 128, 1 + 256 + go: 1 + 256 + go + gw])
                s.op("vector", lambda e, gw=gw: e.tensor_tensor(OUV[:, 0:gw], psA[:, 0:gw], WSL[:, 0:gw], op=ALU.add))
                s.dma("sync", z2_d[oc * 128:(oc + 1) * 128, go:go + gw], OUV[:, 0:gw])

        for tg in range(4):
            for ci in range(8):
                off = 128 if ci < 4 else 127
                s.dma("sync", XF[:, ci * 512:(ci + 1) * 512], z2_d[ci * 128:(ci + 1) * 128, off + tg * 512: off + tg * 512 + 512])
            for ci in range(8):
                s.op("scalar", lambda e, ci=ci: e.activation(GEV[:], XF[:, ci * 512:(ci + 1) * 512], AF.Square))
                s.op("tensor", lambda e, ci=ci: e.matmul(
                    psB[0:1, 0:512], ONE1[:], XF[:, ci * 512:(ci + 1) * 512], start=(ci == 0), stop=(ci == 7), skip_group_check=True))
                s.op("tensor", lambda e, ci=ci: e.matmul(
                    psC[0:1, 0:512], ONE1[:], GEV[:], start=(ci == 0), stop=(ci == 7), skip_group_check=True))
            s.op("vector", lambda e: e.tensor_scalar(R0[:], psB[0:1, 0:512], 1.0 / D, None, op0=ALU.mult))
            s.op("vector", lambda e: e.tensor_scalar(R1[:], psC[0:1, 0:512], 1.0 / D, None, op0=ALU.mult))
            s.op("vector", lambda e: e.tensor_tensor(R2[:], R0[:], R0[:], op=ALU.mult))
            s.op("vector", lambda e: e.tensor_tensor(R2[:], R1[:], R2[:], op=ALU.subtract))
            s.op("vector", lambda e: e.tensor_scalar(R2[:], R2[:], 1e-5, None, op0=ALU.add))
            s.op("scalar", lambda e: e.activation(R2[:], R2[:], AF.Sqrt))
            s.op("vector", lambda e: e.reciprocal(R2[:], R2[:]))
            s.op("tensor", lambda e: e.matmul(psB[:], ONER[:], R0[:], start=True, stop=True))
            s.op("tensor", lambda e: e.matmul(psC[:], ONER[:], R2[:], start=True, stop=True))
            for ci in range(8):
                s.op("vector", lambda e, ci=ci: e.tensor_tensor(GEV[:], XF[:, ci * 512:(ci + 1) * 512], psB[:], op=ALU.subtract))
                s.op("vector", lambda e: e.tensor_tensor(GEV[:], GEV[:], psC[:], op=ALU.mult))
                s.op("vector", lambda e, ci=ci: e.tensor_scalar(
                    XN[:, ci * 512:(ci + 1) * 512], GEV[:], BIA[:, 52 + ci:53 + ci], BIA[:, 60 + ci:61 + ci], op0=ALU.mult, op1=ALU.add))
            w1r = w1b_d.rearrange("p (c w) -> p c w", w=5632)
            for ac in range(22):
                s.dma("sync", WSLB[:, 0:1024].rearrange("p (c w) -> p c w", w=128),
                      w1r[:, :, ac * 128: ac * 128 + 128])
                s.dma("sync", WSLB[:, 1024:2048].rearrange("p (c w) -> p c w", w=128),
                      w1r[:, :, 2816 + ac * 128: 2816 + ac * 128 + 128])
                for ci in range(8):
                    s.op("tensor", lambda e, ci=ci: e.matmul(
                        psA[:], WSLB[:, ci * 128: ci * 128 + 128],
                        XN[:, ci * 512:(ci + 1) * 512], start=(ci == 0), stop=(ci == 7), skip_group_check=True))
                for ci in range(8):
                    s.op("tensor", lambda e, ci=ci: e.matmul(
                        psB[:], WSLB[:, 1024 + ci * 128: 1024 + ci * 128 + 128],
                        XN[:, ci * 512:(ci + 1) * 512], start=(ci == 0), stop=(ci == 7), skip_group_check=True))
                s.op("vector", lambda e, ac=ac: e.tensor_scalar(
                    AEV[:], psA[:], BIA[:, ac:ac + 1], None, op0=ALU.add))
                s.op("scalar", lambda e, ac=ac: e.activation(
                    GEV[:], psB[:], AF.Gelu, bias=BIA[:, 22 + ac:23 + ac]))
                s.op("vector", lambda e, ac=ac: e.tensor_tensor(
                    H2T[:, ac * 512:(ac + 1) * 512], AEV[:], GEV[:], op=ALU.mult))
            w2r = w2b_d.rearrange("p (r w) -> p r w", w=1024)
            for oc in range(8):
                s.dma("sync", W2A[:].rearrange("p (r w) -> p r w", w=128),
                      w2r[:, :, oc * 128:(oc + 1) * 128])
                for ri in range(22):
                    s.op("tensor", lambda e, ri=ri: e.matmul(
                        psA[:], W2A[:, ri * 128:(ri + 1) * 128], H2T[:, ri * 512:(ri + 1) * 512],
                        start=(ri == 0), stop=(ri == 21), skip_group_check=True))
                s.op("vector", lambda e, oc=oc: e.tensor_scalar(OUV[:], psA[:], BIA[:, 44 + oc:45 + oc], None, op0=ALU.add))
                s.dma("sync", WSL[:], z2_d[oc * 128:(oc + 1) * 128, 128 + tg * 512: 128 + tg * 512 + 512])
                s.op("vector", lambda e: e.tensor_tensor(OUV[:], OUV[:], WSL[:], op=ALU.add))
                if last:
                    s.dma("sync", WSL[:], xres_d[oc * 128:(oc + 1) * 128, tg * 512:(tg + 1) * 512])
                    s.op("vector", lambda e: e.tensor_tensor(OUV[:], OUV[:], WSL[:], op=ALU.add))
                s.dma("sync", zo_d[oc * 128:(oc + 1) * 128, tg * 512:(tg + 1) * 512], OUV[:])

        s.build(blk)
    return nc


def rot_tables_host(own0):
    half = np.arange(0, DH, 2, dtype=np.float32)
    inv_freq = (1.0 / 10000.0 ** (half / DH)).astype(np.float32)
    base_scale = ((half + 0.4 * DH) / (1.4 * DH)).astype(np.float32)

    def tabs(pos):
        fr = pos[:, None] * inv_freq[None, :]
        fr64 = np.concatenate([fr, fr], 1)
        sc = base_scale[None, :] ** ((pos - S // 2) / 128.0)[:, None]
        sc64 = np.concatenate([sc, sc], 1).astype(np.float32)
        cos, sin = np.cos(fr64), np.sin(fr64)
        csq = (cos * sc64).T
        snq = (sin * sc64).T
        csk = (cos / sc64).T
        snk = (sin / sc64).T
        stack = lambda a: np.ascontiguousarray(np.concatenate([a, a], 0), dtype=np.float32)
        return stack(csq), stack(snq), stack(csk), stack(snk)
    pk = (own0 - HALO + np.arange(NBUF)).astype(np.float32)
    pq = (own0 - 128 + np.arange(NQ)).astype(np.float32)
    csq, snq, _, _ = tabs(pq)
    _, _, csk, snk = tabs(pk)
    return csq, snq, csk, snk


def bchunk(v, widths):
    out = np.zeros((128, len(widths)), np.float32)
    o = 0
    for i, w in enumerate(widths):
        out[0:w, i] = v[o:o + w]
        o += w
    return out


def l1f_inmaps(d, i, z, last, x3=None):
    rotm = np.zeros((128, 128), np.float32)
    for dd in range(128):
        m = dd % 64
        if m < 32:
            rotm[dd, (dd // 64) * 64 + m + 32] = -1.0
        else:
            rotm[dd, (dd // 64) * 64 + m - 32] = 1.0
    rotmT = np.ascontiguousarray(rotm.T).astype(BF16NP)
    sel2 = np.zeros((128, 2), np.float32); sel2[0:64, 0] = 1; sel2[64:128, 1] = 1
    bb2 = np.zeros((2, 128), np.float32); bb2[0, 0:64] = 1; bb2[1, 64:128] = 1
    zT = t2d(z)
    maps = []
    for c in range(8):
        b, halfc = c // 2, c % 2
        own0 = halfc * TOWN
        zH = np.zeros((D, 1 + NBUF), np.float32)
        lo = own0 - HALO - 1
        src_lo = max(0, lo)
        zH[:, 1 + NBUF - (own0 + TOWN - src_lo) - 0 - 0:] = 0
        seg = zT[b][:, src_lo: own0 + TOWN]
        zH[:, 1 + NBUF - seg.shape[1]:] = seg
        csq, snq, csk, snk = rot_tables_host(own0)
        qidx = (own0 - 128 + np.arange(NQ)).astype(np.float32)
        qidx[qidx < 0] = -1e6
        kbuf = (own0 - HALO + np.arange(NBUF)).astype(np.float32)
        kbuf[kbuf < 0] = -1e6
        kix = np.zeros((128, 19), np.float32)
        for ck in range(19):
            kix[:, ck] = kbuf[ck * 128:(ck + 1) * 128]
        kix2 = kix + 256.0
        mp = {
            "zH": zH, "wqkv": d["L1_ffn_Wqkv"][i], "wo2": d["L1_ffn_Wo"][i],
            "w1": np.ascontiguousarray(np.concatenate(
                [np.pad(d["L1_ffn_W1"][i][:, :DI], ((0, 0), (0, 86))),
                 np.pad(d["L1_ffn_W1"][i][:, DI:], ((0, 0), (0, 86)))], axis=1)),
            "w2": np.pad(d["L1_ffn_W2"][i], ((0, 86), (0, 0))),
            "b1a": bchunk(np.pad(d["L1_ffn_b1"][i][:DI], (0, 86)), [128] * 22),
            "b1g": bchunk(np.pad(d["L1_ffn_b1"][i][DI:], (0, 86)), [128] * 22),
            "b2c": bcol(d["L1_ffn_b2"][i]), "lng": bcol(d["L1_ffn_ln_g"][i]), "lnb": bcol(d["L1_ffn_ln_b"][i]),
            "qsc": np.tile(d["L1_ffn_q_scale"][i], 2).reshape(128, 1).astype(np.float32),
            "ksc": np.tile(d["L1_ffn_k_scale"][i], 2).reshape(128, 1).astype(np.float32),
            "csq": csq[:, 0:NQ], "snq": snq[:, 0:NQ], "csk": csk[:, 0:NBUF], "snk": snk[:, 0:NBUF],
            "rotm": rotmT, "sel2": sel2, "bb2": bb2,
            "qidx": qidx.reshape(1, NQ), "kix": kix, "kix2": kix2,
        }
        if last:
            mp["xres"] = np.ascontiguousarray(t2d(x3[b])[:, own0:own0 + TOWN])
        maps.append(mp)
    return maps


def run_l1f(nc, maps):
    res = run_bass_kernel_spmd(nc, maps, core_ids=list(range(8)))
    z = np.empty((B, S, D), np.float32)
    for c in range(8):
        b, halfc = c // 2, c % 2
        z[b, halfc * TOWN:(halfc + 1) * TOWN] = res.results[c]["zo"].T
    return z




def kernel(**inputs):
    d = {k: np.asarray(v, dtype=np.float32) for k, v in inputs.items()}
    x1 = run_l0_attn(d)
    x2 = run_l0_ffn(d, x1)
    x3, z = run_l1_attn(d, x2)
    nc_mid = build_l1f(False)
    for i in range(5):
        z = run_l1f(nc_mid, l1f_inmaps(d, i, z, False))
    nc_last = build_l1f(True)
    x4 = run_l1f(nc_last, l1f_inmaps(d, 5, z, True, x3))
    return x4


